# version 21
# baseline (speedup 1.0000x reference)
import numpy as np
import ml_dtypes

import concourse.bass as bass
import concourse.mybir as mybir
from concourse.bass_utils import run_bass_kernel_spmd

F32 = mybir.dt.float32
BF16 = mybir.dt.bfloat16
AF = mybir.ActivationFunctionType
AX = mybir.AxisListType
ALU = mybir.AluOpType

NCORES = 8
P = 128

FULL_CFG = dict(H=16, KC=16, NSLOT=8, HC=64, HGRP=8)


class Ctr:
    def __init__(self, sem, step):
        self.sem = sem
        self.step = step
        self.val = 0

    def plan(self):
        self.val += self.step
        return self.val


class Plan:

    def __init__(self):
        self.q = {k: [] for k in ("t", "v", "s", "y", "g")}

    def emit(self, eng, fn, ctr=None):
        val = ctr.plan() if ctr is not None else None

        def run(e, fn=fn, ctr=ctr):
            inst = fn(e)
            if ctr is not None:
                inst.then_inc(ctr.sem, ctr.step)

        self.q[eng].append(run)
        return val

    def wait(self, eng, ctr, val):
        if val is None or val <= 0:
            return
        self.q[eng].append(lambda e, s=ctr.sem, v=val: e.wait_ge(s, v))


def build_nc(cfg):
    H, KC, NSLOT, HC, HGRP = (cfg["H"], cfg["KC"], cfg["NSLOT"], cfg["HC"],
                              cfg["HGRP"])
    TOK = NSLOT * P
    DM = KC * P
    TH = TOK // 2
    NW = NSLOT // 2

    nc = bass.Bass("TRN2", target_bir_lowering=False)

    x_d = nc.dram_tensor("x", [DM, TOK], F32, kind="ExternalInput").ap()
    wq_d = nc.dram_tensor("wq", [H, P, KC, P], F32, kind="ExternalInput").ap()
    wk_d = nc.dram_tensor("wk", [H, P, KC, P], F32, kind="ExternalInput").ap()
    wv_d = nc.dram_tensor("wv", [H, P, KC, P], F32, kind="ExternalInput").ap()
    wo_d = nc.dram_tensor("wo", [KC, P, H, P], F32, kind="ExternalInput").ap()
    wg_d = nc.dram_tensor("wg", [HC, P, KC, P], F32, kind="ExternalInput").ap()
    wu_d = nc.dram_tensor("wu", [HC, P, KC, P], F32, kind="ExternalInput").ap()
    wd_d = nc.dram_tensor("wd", [KC, P, HC, P], F32, kind="ExternalInput").ap()
    cos_d = nc.dram_tensor("cos2", [P, TOK], BF16, kind="ExternalInput").ap()
    sin_d = nc.dram_tensor("sin2", [P, TOK], BF16, kind="ExternalInput").ap()
    msk_d = nc.dram_tensor("mask01", [P, NCORES * P], BF16, kind="ExternalInput").ap()
    idn_d = nc.dram_tensor("ident", [P, P], BF16, kind="ExternalInput").ap()
    prm_d = nc.dram_tensor("perm", [P, P], BF16, kind="ExternalInput").ap()
    out_d = nc.dram_tensor("out", [DM, TOK], F32, kind="ExternalOutput").ap()

    kloc = nc.dram_tensor("kloc", [H * NSLOT * P, P], BF16)
    vloc = nc.dram_tensor("vloc", [H * NSLOT * P, P], BF16)
    qloc = nc.dram_tensor("qloc", [H * P, TOK], BF16)
    HH = H // 2
    kgathA = nc.dram_tensor("kgathA", [NCORES * HH * NSLOT * P, P], BF16)
    kgathB = nc.dram_tensor("kgathB", [NCORES * HH * NSLOT * P, P], BF16)
    vgathA = nc.dram_tensor("vgathA", [NCORES * HH * NSLOT * P, P], BF16)
    vgathB = nc.dram_tensor("vgathB", [NCORES * HH * NSLOT * P, P], BF16)

    kloc4 = kloc.ap().rearrange("(h s d) t -> h s d t", h=H, s=NSLOT)
    vloc4 = vloc.ap().rearrange("(h s t) d -> h s t d", h=H, s=NSLOT)
    qloc3 = qloc.ap().rearrange("(h d) t -> h d t", h=H)
    kg5A = kgathA.ap().rearrange("(r h s d) t -> r h s d t", r=NCORES, h=HH, s=NSLOT)
    kg5B = kgathB.ap().rearrange("(r h s d) t -> r h s d t", r=NCORES, h=HH, s=NSLOT)
    vg5A = vgathA.ap().rearrange("(r h s t) d -> r h s t d", r=NCORES, h=HH, s=NSLOT)
    vg5B = vgathB.ap().rearrange("(r h s t) d -> r h s t d", r=NCORES, h=HH, s=NSLOT)

    NWB = 3
    OVLW = max(
        3 * TOK + HGRP * TOK // 2 + 512 + HGRP * P,
        7 * TOK + 512,
        4 * TOK + TOK + 1024 + 780 + 1024,
        max(3 * TOK + HGRP * TOK // 2 + 512,
            4 * TOK + TOK + 1024 + 780 + 1024)
        + (NSLOT // 2) * P // 2 + 4 + NCORES * P // 2,
    )

    from contextlib import ExitStack
    with ExitStack() as _ctx:
        ec = _ctx.enter_context
        XS = ec(nc.sbuf_tensor("XS", [P, KC * TOK], F32))
        XH = ec(nc.sbuf_tensor("XH", [P, KC * TOK], BF16))
        RS = ec(nc.sbuf_tensor("RS", [P, KC * TOK], F32))
        WRING = ec(nc.sbuf_tensor("WRING", [P, NWB * KC * P], BF16))
        OVL = ec(nc.sbuf_tensor("OVL", [P, OVLW], F32))
        IDB = ec(nc.sbuf_tensor("IDB", [P, P], BF16))
        PRM = ec(nc.sbuf_tensor("PRM", [P, P], BF16))
        ONESC = ec(nc.sbuf_tensor("ONESC", [P, 1], BF16))
        ONESR = ec(nc.sbuf_tensor("ONESR", [1, P], BF16))
        pS0 = ec(nc.psum_tensor("pS0", [P, 1024], F32))
        pS1 = ec(nc.psum_tensor("pS1", [P, 1024], F32))
        pD0 = ec(nc.psum_tensor("pD0", [P, 512], F32))
        pD1 = ec(nc.psum_tensor("pD1", [P, 512], F32))
        pD2 = ec(nc.psum_tensor("pD2", [P, 512], F32))
        pD3 = ec(nc.psum_tensor("pD3", [P, 512], F32))
        ld0_s = ec(nc.semaphore("ld0"))
        qstg_s = ec(nc.semaphore("qstg"))
        kstg_s = ec(nc.semaphore("kstg"))
        vstg_s = ec(nc.semaphore("vstg"))
        kstg2_s = ec(nc.semaphore("kstg2"))
        vstg2_s = ec(nc.semaphore("vstg2"))
        qb_ss = [ec(nc.semaphore(f"qb{i}")) for i in range(2)]
        xc_ss = [ec(nc.semaphore(f"xc{i}")) for i in range(KC)]
        kb_ss = [ec(nc.semaphore(f"kb{i}")) for i in range(4)]
        vb_ss = [ec(nc.semaphore(f"vb{i}")) for i in range(3)]
        ring_ss = [ec(nc.semaphore(f"ring{i}")) for i in range(4)]
        wd_ss = [ec(nc.semaphore(f"wdb{i}")) for i in range(2)]
        tv_s = ec(nc.semaphore("tv"))
        vv_s = ec(nc.semaphore("vv"))
        sv_s = ec(nc.semaphore("sv"))
        cc_s = ec(nc.semaphore("cc"))
        block = ec(nc.Block())
        ld0 = Ctr(ld0_s, 16)
        qstg = Ctr(qstg_s, 16)
        kstg = Ctr(kstg_s, 16)
        vstg = Ctr(vstg_s, 16)
        kstg2 = Ctr(kstg2_s, 16)
        vstg2 = Ctr(vstg2_s, 16)
        qb_c = [Ctr(s, 16) for s in qb_ss]
        xc_c = [Ctr(s, 16) for s in xc_ss]
        kb_c = [Ctr(s, 16) for s in kb_ss]
        vb_c = [Ctr(s, 16) for s in vb_ss]
        ring_c = [Ctr(s, 16) for s in ring_ss]
        wd_c = [Ctr(s, 16) for s in wd_ss]
        tv = Ctr(tv_s, 1)
        vv = Ctr(vv_s, 1)
        sv = Ctr(sv_s, 1)
        cc = Ctr(cc_s, 1)
        pl = Plan()

        ovl = OVL.ap()

        def carve_bf(c0, n):
            assert n % 2 == 0
            return ovl[:, c0:c0 + n // 2].bitcast(BF16)

        def carve_f(c0, n):
            return ovl[:, c0:c0 + n]

        COS2 = carve_bf(0, TOK)
        SIN2 = carve_bf(TOK // 2, TOK)
        QKB = carve_bf(TOK, TOK)
        RTMP = carve_bf(TOK + TOK // 2, TOK)
        RU = carve_bf(2 * TOK, TOK)
        QSTG = carve_bf(2 * TOK + TOK // 2, TOK)
        KSTG = carve_bf(3 * TOK, TOK)
        VTSTG = carve_bf(3 * TOK + TOK // 2, TOK)
        SQA_A = carve_bf(4 * TOK, TOK)
        SQB_A = VTSTG
        SMALLS_A = carve_f(4 * TOK + TOK // 2, 2 * TOK + TOK // 2)
        TMPA_A = carve_f(7 * TOK, 512)
        TMPA2_A = carve_f(4 * TOK, min(512, TOK // 2))

        _b = 4 * TOK
        QTB = carve_bf(_b, 2 * TOK)
        _b += TOK
        KB = carve_bf(_b, 2048)
        _b += 1024
        VB = carve_bf(_b, 1560)
        _b += 780
        PB = carve_bf(_b, 2048)
        _b += 1024
        CFGB_END = _b
        _c = max(3 * TOK + HGRP * TOK // 2 + 512, CFGB_END)
        OBS = carve_bf(_c, NW * P)
        _c += NW * P // 2
        LINV = carve_f(_c, max(NW, 4))
        _c += max(NW, 4)
        MASK = carve_bf(_c, NCORES * P)
        _c += NCORES * P // 2
        CFGB2_END = _c
        assert CFGB_END <= OVLW and CFGB2_END <= OVLW

        SIL = carve_bf(0, TOK)
        HREG = carve_bf(TOK // 2, HGRP * TOK)
        SMALLS_C = carve_f(TOK // 2 + HGRP * TOK // 2, 2 * TOK + TOK // 2)
        TMPA_C = carve_f(3 * TOK + HGRP * TOK // 2, 512)
        TMPA2_C = carve_f(4 * TOK, min(512, TOK // 2))
        WDBUF = carve_bf(3 * TOK + HGRP * TOK // 2 + 512, 2 * HGRP * P)
        assert 3 * TOK + HGRP * TOK // 2 + 512 + HGRP * P <= OVLW, OVLW

        banks = {"S0": pS0, "S1": pS1,
                 "D0": pD0, "D1": pD1, "D2": pD2, "D3": pD3}
        tpsum = pD2.ap().bitcast(BF16)
        tpsum_w = pD0.ap().bitcast(BF16)
        bank_free = {k: None for k in banks}

        def t_mm(out, lhsT, rhs, start, stop, inc=False):
            return pl.emit(
                "t", lambda t: t.matmul(out, lhsT, rhs, start=start, stop=stop),
                tv if inc else None)

        def wait_bank(eng, name):
            bf = bank_free[name]
            if bf is not None:
                pl.wait(eng, bf[0], bf[1])

        ring_srcs = (
            [(wq_d, wk_d, wv_d)[pn][h] for h in range(H) for pn in range(3)]
            + [wo_d[mc] for mc in range(KC)]
            + [(wg_d, wu_d)[pn][hc] for hc in range(HC) for pn in range(2)]
        )
        NR = len(ring_srcs)
        ring_consumed = {}
        ring_loaded = {}
        ring_next = [0]

        def ring_buf3(i):
            return WRING.ap()[:, (i % NWB) * KC * P:(i % NWB + 1) * KC * P].rearrange(
                "p (kc d) -> p kc d", d=P)

        def ring_ensure(upto):
            while ring_next[0] <= min(upto, NR - 1):
                j = ring_next[0]
                if j >= NWB:
                    c, vval = ring_consumed[j - NWB]
                    pl.wait("g", c, vval)
                rc = ring_c[j % NWB]
                ring_loaded[j] = (rc, pl.emit(
                    "g", lambda g, s3=ring_srcs[j], d3=ring_buf3(j):
                    g.dma_start(out=d3, in_=s3), rc))
                ring_next[0] += 1

        ring_ensure(NWB - 1)

        x3s = x_d.rearrange("(kc p) t -> p kc t", p=P)
        XS3 = XS.ap().rearrange("p (kc t) -> p kc t", t=TOK)
        L_cos = pl.emit("s", lambda s: s.dma_start(out=COS2, in_=cos_d), ld0)
        L_sin = pl.emit("s", lambda s: s.dma_start(out=SIN2, in_=sin_d), ld0)
        L_idn = pl.emit("s", lambda s: s.dma_start(out=IDB[:], in_=idn_d), ld0)
        L_prm = pl.emit("s", lambda s: s.dma_start(out=PRM[:], in_=prm_d), ld0)
        L_msk = pl.emit("s", lambda s: s.dma_start(out=MASK, in_=msk_d), ld0)
        L_ph0 = ld0.val
        L_xc = {}
        for kc in range(KC):
            xq = "y" if (kc < 4 or kc % 2 == 1) else "s"
            L_xc[kc] = pl.emit(xq, lambda s, k=kc: s.dma_start(
                out=XS3[:, k, :], in_=x3s[:, k, :]), xc_c[kc])

        pl.emit("v", lambda v: v.memset(ONESC[:], 1.0), vv)
        V_ones = pl.emit("v", lambda v: v.memset(ONESR[:], 1.0), vv)

        def ln(src_chunk, cast_slot, sq_slot, apply_out, smalls, tmpa,
               pre_wait_v=None, pre_wait_t=None, cast_dbuf=False,
               kc_done_cb=None, pre_kc=None, pre_sq=None):
            V_cast = {}
            S_sq = {}
            T_statk = {}
            for kc in range(KC):
                if kc == 0:
                    if pre_wait_v is not None:
                        pl.wait("v", *pre_wait_v)
                    pl.wait("v", ld0, ld0.val)
                if pre_kc is not None:
                    pl.wait("v", *pre_kc(kc))
                if cast_dbuf and kc >= 2:
                    pl.wait("v", tv, T_statk[kc - 2])
                V_cast[kc] = pl.emit(
                    "v", lambda v, s=src_chunk(kc), d=cast_slot(kc):
                    v.tensor_copy(d, s), vv)
                if kc == 0:
                    pl.wait("s", ld0, ld0.val)
                    if pre_wait_v is not None:
                        pl.wait("s", *pre_wait_v)
                    if pre_sq is not None:
                        pl.wait("s", *pre_sq)
                if pre_kc is not None:
                    pl.wait("s", *pre_kc(kc))
                if kc >= 2:
                    pl.wait("s", tv, T_statk[kc - 2])
                S_sq[kc] = pl.emit(
                    "s", lambda s, a=src_chunk(kc), d=sq_slot(kc):
                    s.activation(d, a, AF.Square), sv)
                if kc == 0:
                    if pre_wait_t is not None:
                        pl.wait("t", *pre_wait_t)
                    for bname in ("D0", "D1", "D2", "D3"):
                        wait_bank("t", bname)
                    pl.wait("t", vv, V_ones)
                pl.wait("t", vv, V_cast[kc])
                pl.wait("t", sv, S_sq[kc])
                st, sp = kc == 0, kc == KC - 1
                t_mm(pD0[0:1, 0:TH], ONESC[:], cast_slot(kc)[:, 0:TH], st, sp)
                t_mm(pD1[0:1, 0:TH], ONESC[:], cast_slot(kc)[:, TH:TOK], st, sp)
                t_mm(pD2[0:1, 0:TH], ONESC[:], sq_slot(kc)[:, 0:TH], st, sp)
                T_statk[kc] = t_mm(pD3[0:1, 0:TH], ONESC[:],
                                   sq_slot(kc)[:, TH:TOK], st, sp, inc=True)
            T_stats = T_statk[KC - 1]

            STTA = smalls[0:1, 0:TOK]
            STTB = smalls[0:1, TOK:2 * TOK]
            meanb = smalls[0:1, 2 * TOK:2 * TOK + TOK // 2].bitcast(BF16)
            rstdb = meanb

            def chain_v(fn):
                pl.wait("v", vv, vv.val)
                return pl.emit("v", fn, vv)

            pl.wait("v", tv, T_stats)
            pl.emit("v", lambda v: v.tensor_copy(STTA[:, 0:TH], pD0[0:1, 0:TH]), vv)
            chain_v(lambda v: v.tensor_copy(STTA[:, TH:TOK], pD1[0:1, 0:TH]))
            chain_v(lambda v: v.tensor_copy(STTB[:, 0:TH], pD2[0:1, 0:TH]))
            V_evac = chain_v(lambda v: v.tensor_copy(STTB[:, TH:TOK],
                                                     pD3[0:1, 0:TH]))
            inv = 1.0 / DM
            chain_v(lambda v: v.tensor_scalar_mul(STTA, STTA, inv))
            V_mb = chain_v(lambda v: v.tensor_copy(meanb, STTA))
            chain_v(lambda v: v.tensor_mul(STTA, STTA, STTA))
            chain_v(lambda v: v.tensor_scalar_mul(STTB, STTB, inv))
            chain_v(lambda v: v.tensor_sub(STTB, STTB, STTA))
            V_var = chain_v(lambda v: v.tensor_scalar_add(STTB, STTB, 1e-5))
            pl.wait("s", vv, V_var)
            c = pl.emit("s", lambda s: s.activation(STTA, STTB, AF.Ln), sv)
            pl.wait("s", sv, c)
            S_rstd = pl.emit("s", lambda s: s.activation(STTB, STTA, AF.Exp,
                                                         scale=-0.5), sv)

            pl.wait("t", vv, V_evac)
            pl.wait("t", vv, V_mb)
            t_mm(pD0[:, 0:TH], ONESR[:], meanb[0:1, 0:TH], True, True)
            T_bcm = t_mm(pD1[:, 0:TH], ONESR[:], meanb[0:1, TH:TOK],
                         True, True, inc=True)
            pl.wait("v", tv, T_bcm)
            pl.wait("v", sv, S_rstd)
            V_rb = chain_v(lambda v: v.tensor_copy(rstdb, STTB))
            pl.wait("t", vv, V_rb)
            t_mm(pD2[:, 0:TH], ONESR[:], rstdb[0:1, 0:TH], True, True)
            T_bc = t_mm(pD3[:, 0:TH], ONESR[:], rstdb[0:1, TH:TOK],
                        True, True, inc=True)

            last_apply = None
            kc_ms = {}
            tmpa_last = {}
            pair_i = 0
            for kc in range(KC):
                for half in range(2):
                    mb = (pD0 if half == 0 else pD1)[:, 0:TH]
                    rb = (pD2 if half == 0 else pD3)[:, 0:TH]
                    if kc == 0 and half == 0:
                        pl.wait("v", tv, T_bc)
                    sl = slice(half * TH, (half + 1) * TH)
                    ta = tmpa[pair_i % 2]
                    if pair_i >= 2:
                        pl.wait("v", vv, tmpa_last[pair_i % 2])
                    pl.emit("v", lambda v, s=src_chunk(kc), m=mb, d=ta, q=sl:
                            v.tensor_sub(d[:, 0:TH], s[:, q], m), vv)
                    last_apply = apply_out(kc, half, ta, rb)
                    tmpa_last[pair_i % 2] = last_apply
                    pair_i += 1
                    if half == 1:
                        kc_ms[kc] = last_apply
                        if kc_done_cb is not None:
                            kc_done_cb(kc, last_apply)
            for bname in ("D0", "D1", "D2", "D3"):
                bank_free[bname] = (vv, last_apply)
            return last_apply, kc_ms

        def xs_chunk(kc):
            return XS.ap()[:, kc * TOK:(kc + 1) * TOK]

        def xh_chunk(kc):
            return XH.ap()[:, kc * TOK:(kc + 1) * TOK]

        def rs_chunk(kc):
            return RS.ap()[:, kc * TOK:(kc + 1) * TOK]

        def ln12_apply(kc, half, tmpa, rb):
            sl = slice(half * TH, (half + 1) * TH)
            pl.wait("v", vv, vv.val)
            return pl.emit("v", lambda v, d=xh_chunk(kc), a=tmpa, r=rb, q=sl:
                           v.tensor_mul(d[:, q], a[:, 0:TH], r), vv)

        def ln34_apply(kc, half, tmpa, rb):
            sl = slice(half * TH, (half + 1) * TH)
            pl.wait("v", vv, vv.val)
            pl.emit("v", lambda v, a=tmpa, r=rb:
                    v.tensor_mul(a[:, 0:TH], a[:, 0:TH], r), vv)
            pl.wait("v", vv, vv.val)
            return pl.emit("v", lambda v, d=xs_chunk(kc), a=tmpa, q=sl:
                           v.tensor_add(d[:, q], d[:, q], a[:, 0:TH]), vv)

        V_xhat1, M_xhat1 = ln(
            xs_chunk, cast_slot=xh_chunk,
            sq_slot=lambda kc: SQA_A if kc % 2 == 0 else SQB_A,
            apply_out=ln12_apply, smalls=SMALLS_A, tmpa=(TMPA_A, TMPA2_A),
            pre_kc=lambda kc: (xc_c[kc], L_xc[kc]))

        pair = {0: "S0", 1: "S1"}
        L_kst = {}
        L_vst = {}
        L_qst = {}
        _qkv_state = {"qkb_free": None, "tpsum_free": None, "V_rope_last": 0,
                      "C_k1": None, "C_k2": None, "C_v1": None, "C_v2": None}

        qkv_tails = []

        def qkv_flush_tail():
            if qkv_tails:
                qkv_tails.pop(0)()

        def make_tail(h, pnum, i, bn, b, T_proj):
            def tail():
                self_ = _qkv_state
                pl.wait("s", tv, T_proj)
                for w_ in (self_["qkb_free"] or []):
                    pl.wait("s", *w_)
                c = pl.emit("s", lambda s, b=b:
                            s.activation(QKB[:, 0:TH], b[:, 0:TH], AF.Copy), sv)
                pl.wait("s", sv, c)
                S_evac = pl.emit("s", lambda s, b=b:
                                 s.activation(QKB[:, TH:TOK], b[:, 512:512 + TH],
                                              AF.Copy), sv)
                bank_free[bn] = (sv, S_evac)

                if pnum < 2:
                    stg = QSTG if pnum == 0 else KSTG
                    stgc = qstg if pnum == 0 else (kstg if h < HH else kstg2)
                    wait_bank("t", "D0")
                    wait_bank("t", "D1")
                    pl.wait("t", sv, S_evac)
                    t_mm(pD0[:, 0:TH], PRM[:], QKB[:, 0:TH], True, True)
                    T_swp = t_mm(pD1[:, 0:TH], PRM[:], QKB[:, TH:TOK], True,
                                 True, inc=True)
                    if pnum == 1:
                        pl.wait("v", kstg, kstg.val)
                        pl.wait("v", kstg2, kstg2.val)
                    else:
                        pl.wait("v", stgc, stgc.val)
                    pl.wait("v", sv, S_evac)
                    if h == 0 and pnum == 0:
                        pl.wait("v", ld0, L_ph0)
                    pl.wait("v", vv, vv.val)
                    c1 = pl.emit("v", lambda v: v.tensor_mul(RTMP, QKB, COS2), vv)
                    self_["qkb_free"] = [(vv, c1), (tv, T_swp)]
                    pl.wait("v", tv, T_swp)
                    pl.emit("v", lambda v: v.tensor_mul(
                        RU[:, 0:TH], pD0[:, 0:TH], SIN2[:, 0:TH]), vv)
                    c2 = pl.emit("v", lambda v: v.tensor_mul(
                        RU[:, TH:TOK], pD1[:, 0:TH], SIN2[:, TH:TOK]), vv)
                    bank_free["D0"] = (vv, c2)
                    bank_free["D1"] = (vv, c2)
                    pl.wait("v", vv, vv.val)
                    V_rope = pl.emit("v", lambda v, d=stg: v.tensor_add(
                        d, RTMP, RU), vv)
                    self_["V_rope_last"] = V_rope
                    pl.wait("y", vv, V_rope)
                    if pnum == 0:
                        L_qst[h] = pl.emit(
                            "y", lambda s, hh=h: s.dma_start(out=qloc3[hh],
                                                             in_=QSTG), qstg)
                    else:
                        ks = KSTG.rearrange("d (s t) -> d s t", s=NSLOT)
                        kd = kloc4[h].rearrange("s d t -> d s t")
                        L_kst[h] = pl.emit(
                            "y", lambda s, a=kd, b2=ks: s.dma_start(out=a, in_=b2),
                            stgc)
                        if h == HH and pnum == 1:
                            pl.wait("g", kstg, kstg.val)
                            self_["C_k1"] = pl.emit(
                                "g", lambda g: g.collective_compute(
                                    "AllGather", ALU.bypass,
                                    replica_groups=[list(range(NCORES))],
                                    ins=[kloc.ap()[0:HH * NSLOT * P, :].opt()],
                                    outs=[kgathA.ap().opt()]), cc)

                else:
                    vstgc = vstg if h < HH else vstg2
                    if self_["tpsum_free"] is not None:
                        pl.wait("t", *self_["tpsum_free"])
                    wait_bank("t", "D2")
                    pl.wait("t", sv, S_evac)
                    for sl_ in range(NSLOT):
                        T_vtr = pl.emit(
                            "t", lambda t, ss=sl_: t.transpose(
                                tpsum[:, ss * P:(ss + 1) * P],
                                QKB[:, ss * P:(ss + 1) * P], IDB[:]),
                            tv if sl_ == NSLOT - 1 else None)
                    self_["qkb_free"] = [(tv, T_vtr)]
                    pl.wait("s", vstg, vstg.val)
                    pl.wait("s", vstg2, vstg2.val)
                    pl.wait("s", tv, T_vtr)
                    S_vt = pl.emit("s", lambda s: s.activation(
                        VTSTG, tpsum[:, 0:NSLOT * P], AF.Copy), sv)
                    self_["tpsum_free"] = (sv, S_vt)
                    bank_free["D2"] = (sv, S_vt)
                    pl.wait("y", sv, S_vt)
                    vs = VTSTG.rearrange("t (s d) -> t s d", s=NSLOT)
                    vd = vloc4[h].rearrange("s t d -> t s d")
                    L_vst[h] = pl.emit(
                        "y", lambda s, a=vd, b2=vs: s.dma_start(out=a, in_=b2),
                        vstgc)
                    if h == HH:
                        pl.wait("g", vstg, vstg.val)
                        self_["C_v1"] = pl.emit(
                            "g", lambda g: g.collective_compute(
                                "AllGather", ALU.bypass,
                                replica_groups=[list(range(NCORES))],
                                ins=[vloc.ap()[0:HH * NSLOT * P, :].opt()],
                                outs=[vgathA.ap().opt()]), cc)

            return tail

        for h in range(H):
            for pnum in range(3):
                i = 3 * h + pnum
                ring_ensure(i + NWB - 1)
                bn = pair[i % 2]
                b = banks[bn]
                pl.wait("t", *ring_loaded[i])
                if i == 0:
                    pl.wait("t", ld0, L_ph0)
                wait_bank("t", bn)
                wb3 = ring_buf3(i)
                for kc in range(KC):
                    if i == 0:
                        pl.wait("t", vv, M_xhat1[kc])
                    st, sp = kc == 0, kc == KC - 1
                    t_mm(b[:, 0:TH], wb3[:, kc, :], xh_chunk(kc)[:, 0:TH], st, sp)
                    T_proj = t_mm(b[:, 512:512 + TH], wb3[:, kc, :],
                                  xh_chunk(kc)[:, TH:TOK], st, sp,
                                  inc=(kc == KC - 1))
                ring_consumed[i] = (tv, T_proj)
                qkv_flush_tail()
                qkv_tails.append(make_tail(h, pnum, i, bn, b, T_proj))
        qkv_flush_tail()
        C_k1 = _qkv_state["C_k1"]
        C_v1 = _qkv_state["C_v1"]
        C_k2 = None
        C_v2 = None
        T_sc_last = None
        V_rope_last = _qkv_state["V_rope_last"]


        pl.wait("v", vv, vv.val)
        V_vbones = None
        for par in range(3):
            for i4 in range(4):
                col = par * 520 + i4 * 130 + 128
                V_vbones = pl.emit(
                    "v", lambda v, cl=col: v.memset(VB[:, cl:cl + 1], 1.0), vv)

        sbank_names = ["S0", "S1"]
        obank_names = ["D0", "D1", "D2", "D3"]
        kb_free = {}
        vb_free = {}
        pb_free = {}
        L_q = {}
        ii = 0
        jj = 0
        S_ot_last = 0
        S_ot_head = {}
        T_otr_last = [0]
        waves = [(0, NW), (NW, NSLOT)]

        for h in range(2):
            pl.wait("s", qstg, qstg.val)
            pl.wait("s", vv, V_rope_last)
            L_q[h] = pl.emit("s", lambda s, hh=h: s.dma_start(
                out=QTB[:, (hh % 2) * TOK:(hh % 2 + 1) * TOK], in_=qloc3[hh]),
                qb_c[h % 2])

        for h in range(H):
            if h == min(2, HH) and C_k2 is None:
                if T_sc_last is not None:
                    pl.wait("g", tv, T_sc_last)
                pl.wait("g", kstg2, kstg2.val)
                C_k2 = pl.emit("g", lambda g: g.collective_compute(
                    "AllGather", ALU.bypass,
                    replica_groups=[list(range(NCORES))],
                    ins=[kloc.ap()[HH * NSLOT * P:, :].opt()],
                    outs=[kgathB.ap().opt()]), cc)
                pl.wait("g", vstg2, vstg2.val)
                C_v2 = pl.emit("g", lambda g: g.collective_compute(
                    "AllGather", ALU.bypass,
                    replica_groups=[list(range(NCORES))],
                    ins=[vloc.ap()[HH * NSLOT * P:, :].opt()],
                    outs=[vgathB.ap().opt()]), cc)
            qv = QTB[:, (h % 2) * TOK:(h % 2 + 1) * TOK]
            T_sc_last = None
            for iw, (s_lo, s_hi) in enumerate(waves):
                jmax = 2 * (s_hi - 1) + 2
                T_stop = {}
                pending = []
                pairbuf = []
                np_ = [0]
                ns_ = [0]

                def flush_pair():
                    if not pairbuf:
                        return
                    pp = np_[0] % 2
                    sbank = ns_[0] % 2
                    np_[0] += 1
                    ns_[0] += 1
                    sbn = sbank_names[sbank]
                    sb_v = banks[sbn]
                    n_it = len(pairbuf)
                    pbv_pair = PB[:, pp * 1024:pp * 1024 + 1024]
                    while pending and any(e.get("ppar") == pp for e in pending):
                        emit_pv(pending.pop(0))
                    if pp in pb_free:
                        pl.wait("s", *pb_free[pp])
                    pl.wait("s", tv, pairbuf[-1]["T_sc"])
                    S_exp = pl.emit(
                        "s", lambda sc, d=pbv_pair[:, 0:n_it * 512],
                        b=sb_v[:, 0:n_it * 512]:
                        sc.activation(d, b, AF.Exp, scale=1.0 / 128.0), sv)
                    bank_free[sbn] = (sv, S_exp)
                    for it in pairbuf:
                        pv_ctr, pv_val = sv, S_exp
                        if it["diag"] is not None:
                            half = it["diag"]
                            pc = it["pcols"]
                            stride = pc[1] - pc[0]
                            if stride == P:
                                p_ap = pbv_pair.rearrange(
                                    "p (kt q) -> p kt q", q=P)[:, pc[0] // P:
                                                               pc[0] // P + 4, :]
                            else:
                                off = pc[0]
                                p_ap = pbv_pair.rearrange(
                                    "p (kt q) -> p kt q", q=stride)[:, :, off:off + P]
                            m_ap = MASK[:, half * 512:(half + 1) * 512].rearrange(
                                "p (kt q) -> p kt q", q=P)
                            pl.wait("v", ld0, L_msk)
                            pl.wait("v", sv, S_exp)
                            V_msk = pl.emit(
                                "v", lambda v, d=p_ap, m=m_ap:
                                v.tensor_mul(d, d, m), vv)
                            pv_ctr, pv_val = vv, V_msk
                        it["pv_ctr"], it["pv_val"] = pv_ctr, pv_val
                        it["ppar"] = pp
                        it["pbv"] = pbv_pair
                        pending.append(it)
                    pairbuf.clear()

                def emit_pv(p):
                    pl.wait("t", p["pv_ctr"], p["pv_val"])
                    pl.wait("t", vb_c[p["jpar"]], p["L_v"])
                    if p["first"]:
                        wait_bank("t", p["ob"])
                    for kt in range(4):
                        pc = p["pcols"][kt]
                        T_pv = t_mm(
                            p["ob_t"][:, 0:129],
                            p["pbv"][:, pc:pc + P],
                            VB[:, p["jpar"] * 520 + kt * 130:
                               p["jpar"] * 520 + kt * 130 + 129],
                            start=(p["first"] and kt == 0),
                            stop=(p["lastj"] and kt == 3),
                            inc=(kt == 3))
                    pbf = pb_free.get(p["ppar"])
                    pb_free[p["ppar"]] = (tv, T_pv)
                    vb_free[p["jpar"]] = (tv, T_pv)
                    if p["lastj"]:
                        T_stop[p["s_"]] = T_pv

                jj0 = jj

                kloads = {}
                vloads = {}

                def issue_k(jn):
                    if jn in kloads or jn >= jmax:
                        return
                    jsq = jj0 + jn
                    par = jsq % 4
                    r0n = (4 * jn) % NCORES
                    spn = jn // 2
                    kg5 = kg5A if h < HH else kg5B
                    hl = h if h < HH else h - HH
                    if iw == 0 and jn == 0:
                        pl.wait("y", cc, C_k1 if h < HH else C_k2)
                    if par in kb_free:
                        pl.wait("y", *kb_free[par])
                    ksrc = kg5[r0n:r0n + 4, hl, spn, :, :].rearrange("r d t -> d r t")
                    kdst = KB[:, par * 512:(par + 1) * 512].rearrange(
                        "d (r t) -> d r t", r=4)
                    kloads[jn] = (pl.emit("y", lambda s, a=kdst, b=ksrc:
                                          s.dma_start(out=a, in_=b), kb_c[par]), par)

                def issue_v(jn):
                    if jn in vloads or jn >= jmax:
                        return
                    jsq = jj0 + jn
                    par = jsq % 3
                    r0n = (4 * jn) % NCORES
                    spn = jn // 2
                    vg5 = vg5A if h < HH else vg5B
                    hl = h if h < HH else h - HH
                    if ii == 0 and jn == 0:
                        pl.wait("s", vv, V_vbones)
                    if iw == 0 and jn == 0:
                        pl.wait("s", cc, C_v1 if h < HH else C_v2)
                    if par in vb_free:
                        pl.wait("s", *vb_free[par])
                    vsrc = vg5[r0n:r0n + 4, hl, spn, :, :].rearrange("r t d -> t r d")
                    vdst = VB[:, par * 520:par * 520 + 520].rearrange(
                        "t (r e) -> t r e", e=130)[:, :, 0:128]
                    vloads[jn] = (pl.emit("s", lambda s, a=vdst, b=vsrc:
                                          s.dma_start(out=a, in_=b), vb_c[par]), par)

                for j in range(jmax):
                    jseq = jj
                    jj += 1
                    while pairbuf and pairbuf[0]["jseq"] <= jseq - 2:
                        flush_pair()
                    while pending and pending[0]["jseq"] <= jseq - 2:
                        emit_pv(pending.pop(0))
                    issue_k(j)
                    issue_v(j)
                    issue_v(j + 1)
                    issue_k(j + 1)
                    issue_k(j + 2)
                    L_k, kpar = kloads[j]
                    L_v, jpar = vloads[j]
                    slots = list(range(max(s_lo, j // 2), s_hi))
                    si_ = 0
                    while si_ < len(slots):
                        sbn = sbank_names[ns_[0] % 2]
                        sb_v = banks[sbn]
                        q_ = len(pairbuf)
                        two = (q_ == 0 and si_ + 1 < len(slots))
                        if q_ == 0:
                            wait_bank("t", sbn)
                        pl.wait("t", kb_c[kpar], L_k)
                        pl.wait("t", qb_c[h % 2], L_q[h])
                        if two:
                            s_a = slots[si_]
                            for kt in range(4):
                                T_sc = t_mm(
                                    sb_v[:, kt * 256:(kt + 1) * 256],
                                    KB[:, kpar * 512 + kt * P:
                                       kpar * 512 + (kt + 1) * P],
                                    qv[:, s_a * P:(s_a + 2) * P],
                                    True, True, inc=(kt == 3))
                            for idx in range(2):
                                s_ = slots[si_ + idx]
                                diag = None
                                if j == 2 * s_ or j == 2 * s_ + 1:
                                    diag = j - 2 * s_
                                pairbuf.append(dict(
                                    T_sc=T_sc, diag=diag, jpar=jpar, jseq=jseq,
                                    pcols=[kt * 256 + idx * P for kt in range(4)],
                                    ob=obank_names[s_ - s_lo],
                                    ob_t=banks[obank_names[s_ - s_lo]],
                                    first=(j == 0), lastj=(j == 2 * s_ + 1),
                                    L_v=L_v, s_=s_))
                            si_ += 2
                            ii += 2
                        else:
                            s_ = slots[si_]
                            for kt in range(4):
                                T_sc = t_mm(
                                    sb_v[:, q_ * 512 + kt * P:
                                         q_ * 512 + (kt + 1) * P],
                                    KB[:, kpar * 512 + kt * P:
                                       kpar * 512 + (kt + 1) * P],
                                    qv[:, s_ * P:(s_ + 1) * P],
                                    True, True, inc=(kt == 3))
                            diag = None
                            if j == 2 * s_ or j == 2 * s_ + 1:
                                diag = j - 2 * s_
                            pairbuf.append(dict(
                                T_sc=T_sc, diag=diag, jpar=jpar, jseq=jseq,
                                pcols=[q_ * 512 + kt * P for kt in range(4)],
                                ob=obank_names[s_ - s_lo],
                                ob_t=banks[obank_names[s_ - s_lo]],
                                first=(j == 0), lastj=(j == 2 * s_ + 1),
                                L_v=L_v, s_=s_))
                            si_ += 1
                            ii += 1
                        T_sc_last = T_sc
                        kb_free[kpar] = (tv, T_sc)
                        if len(pairbuf) == 2:
                            flush_pair()
                        while len(pending) > 3:
                            emit_pv(pending.pop(0))

                flush_pair()
                while pending:
                    emit_pv(pending.pop(0))

                V_on = {}
                for s_ in range(s_lo, s_hi):
                    si = s_ - s_lo
                    ob = obank_names[si]
                    ob_t = banks[ob]
                    li = LINV[:, si:si + 1]
                    pl.wait("v", tv, T_stop[s_])
                    c3 = pl.emit("v", lambda v, o=ob_t, l=li:
                                 v.reciprocal(l, o[:, 128:129]), vv)
                    pl.wait("v", vv, c3)
                    V_on[si] = pl.emit(
                        "v", lambda v, o=ob_t, l=li,
                        d=OBS[:, si * P:(si + 1) * P]:
                        v.tensor_scalar_mul(d, o[:, 0:128], l), vv)
                    bank_free[ob] = (vv, V_on[si])
                T_otr = None
                for si in range(s_hi - s_lo):
                    pl.wait("t", vv, V_on[0])
                    pl.wait("t", vv, V_on[si])
                    T_otr = pl.emit("t", lambda t, ss=si: t.transpose(
                        tpsum_w[:, ss * P:(ss + 1) * P],
                        OBS[:, ss * P:(ss + 1) * P], IDB[:]), tv)
                T_otr_last[0] = T_otr
                pl.wait("s", tv, T_otr)
                S_ot = pl.emit("s", lambda sc, hh=h, w=iw: sc.activation(
                    XH.ap()[:, hh * TOK + w * NW * P:hh * TOK + (w + 1) * NW * P],
                    tpsum_w[:, 0:NW * P], AF.Copy), sv)
                S_ot_last = S_ot
                S_ot_head[h] = S_ot
                bank_free["D0"] = (sv, S_ot)

            hh2 = h + 2
            if hh2 < H:
                pl.wait("s", tv, T_sc_last)
                L_q[hh2] = pl.emit("s", lambda s, q2=hh2: s.dma_start(
                    out=QTB[:, (q2 % 2) * TOK:(q2 % 2 + 1) * TOK],
                    in_=qloc3[q2]), qb_c[hh2 % 2])

        V_ro = {}
        T_oproj = {}
        for mc in range(KC):
            i = 3 * H + mc
            ring_ensure(i + NWB - 1)
            bn = pair[mc % 2]
            b = banks[bn]
            pl.wait("t", *ring_loaded[i])
            wait_bank("t", bn)
            wb3 = ring_buf3(i)
            for hh in range(H):
                if mc == 0:
                    pl.wait("t", sv, S_ot_head[hh])
                st, sp = hh == 0, hh == H - 1
                ot = XH.ap()[:, hh * TOK:(hh + 1) * TOK]
                t_mm(b[:, 0:TH], wb3[:, hh, :], ot[:, 0:TH], st, sp)
                T_o = t_mm(b[:, 512:512 + TH], wb3[:, hh, :], ot[:, TH:TOK], st, sp,
                           inc=(hh == H - 1))
            T_oproj[mc] = T_o
            ring_consumed[i] = (tv, T_o)
            pl.wait("v", tv, T_o)
            pl.emit("v", lambda v, b=b, d=rs_chunk(mc):
                    v.tensor_copy(d[:, 0:TH], b[:, 0:TH]), vv)
            V_ro[mc] = pl.emit("v", lambda v, b=b, d=rs_chunk(mc):
                               v.tensor_copy(d[:, TH:TOK], b[:, 512:512 + TH]), vv)
            bank_free[bn] = (vv, V_ro[mc])

        V_res1, M_res1 = ln(
            rs_chunk,
            cast_slot=lambda kc: xh_chunk(kc % 2),
            sq_slot=lambda kc: HREG[:, (kc % 2) * TOK:(kc % 2 + 1) * TOK],
            apply_out=ln34_apply, smalls=SMALLS_C, tmpa=(TMPA_C, TMPA2_C),
            pre_kc=lambda kc: (vv, V_ro[kc]), pre_wait_t=(tv, T_oproj[KC - 1]),
            cast_dbuf=True)

        V_xhat2, M_xhat2 = ln(
            xs_chunk, cast_slot=xh_chunk,
            sq_slot=lambda kc: HREG[:, (kc % 2) * TOK:(kc % 2 + 1) * TOK],
            apply_out=ln12_apply, smalls=SMALLS_C, tmpa=(TMPA_C, TMPA2_C),
            pre_kc=lambda kc: (vv, M_res1[kc]))

        L_wd = {}
        wd_free = {}
        V_d = {}
        h_free = {}
        prev_vh = [None]
        wd2 = wd_d
        ngrp = HC // HGRP

        def wd_load(g_, mc):
            key = (g_, mc)
            if key in L_wd:
                return
            if not L_wd:
                pl.wait("g", tv, T_otr_last[0])
            par = mc % 2
            if par in wd_free:
                pl.wait("g", *wd_free[par])
                del wd_free[par]
            src = wd2[mc][:, g_ * HGRP:(g_ + 1) * HGRP, :]
            dst = WDBUF[:, par * HGRP * P:(par + 1) * HGRP * P].rearrange(
                "p (i m) -> p i m", m=P)
            L_wd[key] = (wd_c[par], pl.emit(
                "g", lambda g, a=dst, b=src: g.dma_start(out=a, in_=b), wd_c[par]))

        for g_ in range(ngrp):
            V_h = {}
            for i_ in range(HGRP):
                hc = g_ * HGRP + i_
                sg = 3 * H + KC + 2 * hc
                su = sg + 1
                ring_ensure(sg + NWB - 1)
                pl.wait("t", *ring_loaded[sg])
                wb_g = ring_buf3(sg)
                wb_u = ring_buf3(su)
                wait_bank("t", "S0")
                for kc in range(KC):
                    if hc == 0:
                        pl.wait("t", vv, M_xhat2[kc])
                    st, sp = kc == 0, kc == KC - 1
                    t_mm(pS0[:, 0:TH], wb_g[:, kc, :], xh_chunk(kc)[:, 0:TH], st, sp)
                    T_g = t_mm(pS0[:, 512:512 + TH], wb_g[:, kc, :],
                               xh_chunk(kc)[:, TH:TOK], st, sp, inc=(kc == KC - 1))
                ring_consumed[sg] = (tv, T_g)
                ring_ensure(su + NWB - 1)
                pl.wait("t", *ring_loaded[su])
                wait_bank("t", "S1")
                for kc in range(KC):
                    st, sp = kc == 0, kc == KC - 1
                    t_mm(pS1[:, 0:TH], wb_u[:, kc, :], xh_chunk(kc)[:, 0:TH], st, sp)
                    T_u = t_mm(pS1[:, 512:512 + TH], wb_u[:, kc, :],
                               xh_chunk(kc)[:, TH:TOK], st, sp, inc=(kc == KC - 1))
                ring_consumed[su] = (tv, T_u)

                pl.wait("s", tv, T_g)
                if prev_vh[0] is not None:
                    pl.wait("s", vv, prev_vh[0])
                c4 = pl.emit("s", lambda s: s.activation(SIL[:, 0:TH], pS0[:, 0:TH],
                                                         AF.Silu), sv)
                pl.wait("s", sv, c4)
                S_sil = pl.emit("s", lambda s: s.activation(
                    SIL[:, TH:TOK], pS0[:, 512:512 + TH], AF.Silu), sv)
                bank_free["S0"] = (sv, S_sil)

                hreg = HREG[:, i_ * TOK:(i_ + 1) * TOK]
                if i_ in h_free:
                    pl.wait("v", *h_free[i_])
                pl.wait("v", sv, S_sil)
                pl.wait("v", tv, T_u)
                pl.emit("v", lambda v, d=hreg: v.tensor_mul(
                    d[:, 0:TH], SIL[:, 0:TH], pS1[:, 0:TH]), vv)
                V_h[i_] = pl.emit("v", lambda v, d=hreg: v.tensor_mul(
                    d[:, TH:TOK], SIL[:, TH:TOK], pS1[:, 512:512 + TH]), vv)
                bank_free["S1"] = (vv, V_h[i_])
                prev_vh[0] = V_h[i_]

            dpair = {0: ("D0", "D1"), 1: ("D2", "D3")}
            wd_load(g_, 0)
            wd_load(g_, 1)
            for mc in range(KC):
                d0n, d1n = dpair[mc % 2]
                d0, d1 = banks[d0n], banks[d1n]
                wait_bank("t", d0n)
                wait_bank("t", d1n)
                pl.wait("t", *L_wd[(g_, mc)])
                wb_d = WDBUF[:, (mc % 2) * HGRP * P:(mc % 2 + 1) * HGRP * P].rearrange(
                    "p (i m) -> p i m", m=P)
                for i_ in range(HGRP):
                    if mc == 0:
                        pl.wait("t", vv, V_h[i_])
                    hreg = HREG[:, i_ * TOK:(i_ + 1) * TOK]
                    st, sp = i_ == 0, i_ == HGRP - 1
                    t_mm(d0[:, 0:TH], wb_d[:, i_, :], hreg[:, 0:TH], st, sp)
                    T_d = t_mm(d1[:, 0:TH], wb_d[:, i_, :], hreg[:, TH:TOK], st, sp,
                               inc=(i_ == HGRP - 1))
                wd_free[mc % 2] = (tv, T_d)
                if mc + 2 < KC:
                    wd_load(g_, mc + 2)
                elif g_ + 1 < ngrp:
                    wd_load(g_ + 1, mc + 2 - KC)
                pl.wait("v", tv, T_d)
                if g_ == 0:
                    pl.emit("v", lambda v, b=d0, d=rs_chunk(mc):
                            v.tensor_copy(d[:, 0:TH], b[:, 0:TH]), vv)
                    V_dd = pl.emit("v", lambda v, b=d1, d=rs_chunk(mc):
                                   v.tensor_copy(d[:, TH:TOK], b[:, 0:TH]), vv)
                else:
                    pl.wait("v", vv, V_d[mc])
                    pl.emit("v", lambda v, b=d0, d=rs_chunk(mc):
                            v.tensor_add(d[:, 0:TH], d[:, 0:TH], b[:, 0:TH]), vv)
                    V_dd = pl.emit("v", lambda v, b=d1, d=rs_chunk(mc):
                                   v.tensor_add(d[:, TH:TOK], d[:, TH:TOK], b[:, 0:TH]),
                                   vv)
                V_d[mc] = V_dd
                bank_free[d0n] = (vv, V_dd)
                bank_free[d1n] = (vv, V_dd)
                if mc == KC - 1:
                    for i_ in range(HGRP):
                        h_free[i_] = (tv, T_d)

        out3 = out_d.rearrange("(kc p) t -> p kc t", p=P)

        def out_chunk_cb(kc, milestone):
            pl.wait("y", vv, milestone)
            pl.emit("y", lambda s, k=kc: s.dma_start(
                out=out3[:, k, :], in_=XS3[:, k, :]), ld0)

        V_res2, _ = ln(
            rs_chunk,
            cast_slot=lambda kc: xh_chunk(kc % 2),
            sq_slot=lambda kc: HREG[:, (kc % 2) * TOK:(kc % 2 + 1) * TOK],
            apply_out=ln34_apply, smalls=SMALLS_C, tmpa=(TMPA_C, TMPA2_C),
            pre_kc=lambda kc: (vv, V_d[kc]), pre_wait_t=(tv, tv.val),
            pre_sq=(vv, V_d[KC - 1]),
            cast_dbuf=True, kc_done_cb=out_chunk_cb)

        pl.wait("y", ld0, ld0.val)

        @block.tensor
        def _(t):
            for fn in pl.q["t"]:
                fn(t)

        @block.vector
        def _(v):
            for fn in pl.q["v"]:
                fn(v)

        @block.scalar
        def _(s):
            for fn in pl.q["s"]:
                fn(s)

        @block.sync
        def _(s):
            for fn in pl.q["y"]:
                fn(s)

        @block.gpsimd
        def _(g):
            for fn in pl.q["g"]:
                fn(g)

    return nc



def owned_tokens(c, nslot):
    return np.concatenate(
        [np.arange(P) + P * (NCORES * s + c) for s in range(nslot)])


def prep_inputs(x, w_aq, w_ak, w_av, w_ao, w_gate, w_up, w_down, cfg):
    H, KC, NSLOT, HC = cfg["H"], cfg["KC"], cfg["NSLOT"], cfg["HC"]
    DM = KC * P

    def pre_qkv(w):
        return np.ascontiguousarray(
            np.asarray(w, dtype=np.float32).reshape(H, KC, P, P).transpose(0, 2, 1, 3))

    wq = pre_qkv(w_aq)
    wk = pre_qkv(w_ak)
    wv = pre_qkv(w_av)
    wo = np.ascontiguousarray(
        np.asarray(w_ao, dtype=np.float32).reshape(H, P, KC, P).transpose(2, 1, 0, 3))
    wg = np.ascontiguousarray(
        np.asarray(w_gate, dtype=np.float32).reshape(HC, P, KC, P).transpose(0, 3, 2, 1))
    wu = np.ascontiguousarray(
        np.asarray(w_up, dtype=np.float32).reshape(HC, P, KC, P).transpose(0, 3, 2, 1))
    wd = np.ascontiguousarray(
        np.asarray(w_down, dtype=np.float32).reshape(KC, P, HC, P).transpose(0, 3, 2, 1))

    ident = np.eye(P, dtype=ml_dtypes.bfloat16)
    perm = np.zeros((P, P), dtype=ml_dtypes.bfloat16)
    perm[(np.arange(P) + P // 2) % P, np.arange(P)] = 1.0
    xf = np.asarray(x, dtype=np.float32).reshape(-1, DM)
    in_maps = []
    for c in range(NCORES):
        idx = owned_tokens(c, NSLOT)
        xc = np.ascontiguousarray(xf[idx, :].T)
        pos = idx.astype(np.float64)
        dims = np.arange(64, dtype=np.float64)
        freqs = 10000.0 ** (-dims / 64.0)
        rad = freqs[:, None] * pos[None, :]
        cos2 = np.concatenate([np.cos(rad), np.cos(rad)], axis=0)
        sin2 = np.concatenate([-np.sin(rad), np.sin(rad)], axis=0)
        kk = np.arange(P)[:, None, None]
        kt = np.arange(NCORES)[None, :, None]
        qi = np.arange(P)[None, None, :]
        m01 = (128 * kt + kk <= 128 * c + qi).astype(np.float32).reshape(
            P, NCORES * P)
        in_maps.append({
            "x": xc,
            "wq": wq, "wk": wk, "wv": wv, "wo": wo,
            "wg": wg, "wu": wu, "wd": wd,
            "cos2": cos2.astype(ml_dtypes.bfloat16),
            "sin2": sin2.astype(ml_dtypes.bfloat16),
            "mask01": m01.astype(ml_dtypes.bfloat16),
            "ident": ident,
            "perm": perm,
        })
    return in_maps


_NC_CACHE = {}


def kernel(L0_attn_x, L0_attn_w_aq, L0_attn_w_ak, L0_attn_w_av, L0_attn_w_ao,
           L0_ffn_w_gate, L0_ffn_w_up, L0_ffn_w_down):
    cfg = FULL_CFG
    in_maps = prep_inputs(L0_attn_x, L0_attn_w_aq, L0_attn_w_ak, L0_attn_w_av,
                          L0_attn_w_ao, L0_ffn_w_gate, L0_ffn_w_up,
                          L0_ffn_w_down, cfg)
    if "full" not in _NC_CACHE:
        _NC_CACHE["full"] = build_nc(cfg)
    nc = _NC_CACHE["full"]
    res = run_bass_kernel_spmd(nc, in_maps, core_ids=list(range(NCORES)))
    T = cfg["NSLOT"] * P * NCORES
    DM = cfg["KC"] * P
    out = np.empty((1, T, DM), dtype=np.float32)
    for c in range(NCORES):
        idx = owned_tokens(c, cfg["NSLOT"])
        out[0, idx, :] = res.results[c]["out"].T
    return out


# version 23
# speedup vs baseline: 1.0093x; 1.0093x over previous
import numpy as np
import ml_dtypes

import concourse.bass as bass
import concourse.mybir as mybir
from concourse.bass_utils import run_bass_kernel_spmd

F32 = mybir.dt.float32
BF16 = mybir.dt.bfloat16
AF = mybir.ActivationFunctionType
AX = mybir.AxisListType
ALU = mybir.AluOpType

NCORES = 8
P = 128

FULL_CFG = dict(H=16, KC=16, NSLOT=8, HC=64, HGRP=8)


class Ctr:
    def __init__(self, sem, step):
        self.sem = sem
        self.step = step
        self.val = 0

    def plan(self):
        self.val += self.step
        return self.val


class Plan:

    def __init__(self):
        self.q = {k: [] for k in ("t", "v", "s", "y", "g")}

    def emit(self, eng, fn, ctr=None):
        val = ctr.plan() if ctr is not None else None

        def run(e, fn=fn, ctr=ctr):
            inst = fn(e)
            if ctr is not None:
                inst.then_inc(ctr.sem, ctr.step)

        self.q[eng].append(run)
        return val

    def wait(self, eng, ctr, val):
        if val is None or val <= 0:
            return
        self.q[eng].append(lambda e, s=ctr.sem, v=val: e.wait_ge(s, v))


def build_nc(cfg):
    H, KC, NSLOT, HC, HGRP = (cfg["H"], cfg["KC"], cfg["NSLOT"], cfg["HC"],
                              cfg["HGRP"])
    TOK = NSLOT * P
    DM = KC * P
    TH = TOK // 2
    NW = NSLOT // 2

    nc = bass.Bass("TRN2", target_bir_lowering=False)

    x_d = nc.dram_tensor("x", [DM, TOK], F32, kind="ExternalInput").ap()
    wq_d = nc.dram_tensor("wq", [H, P, KC, P], F32, kind="ExternalInput").ap()
    wk_d = nc.dram_tensor("wk", [H, P, KC, P], F32, kind="ExternalInput").ap()
    wv_d = nc.dram_tensor("wv", [H, P, KC, P], F32, kind="ExternalInput").ap()
    wo_d = nc.dram_tensor("wo", [KC, P, H, P], F32, kind="ExternalInput").ap()
    wg_d = nc.dram_tensor("wg", [HC, P, KC, P], F32, kind="ExternalInput").ap()
    wu_d = nc.dram_tensor("wu", [HC, P, KC, P], F32, kind="ExternalInput").ap()
    wd_d = nc.dram_tensor("wd", [KC, P, HC, P], F32, kind="ExternalInput").ap()
    cos_d = nc.dram_tensor("cos2", [P, TOK], BF16, kind="ExternalInput").ap()
    sin_d = nc.dram_tensor("sin2", [P, TOK], BF16, kind="ExternalInput").ap()
    msk_d = nc.dram_tensor("mask01", [P, NCORES * P], BF16, kind="ExternalInput").ap()
    idn_d = nc.dram_tensor("ident", [P, P], BF16, kind="ExternalInput").ap()
    prm_d = nc.dram_tensor("perm", [P, P], BF16, kind="ExternalInput").ap()
    out_d = nc.dram_tensor("out", [DM, TOK], F32, kind="ExternalOutput").ap()

    kloc = nc.dram_tensor("kloc", [H * NSLOT * P, P], BF16)
    vloc = nc.dram_tensor("vloc", [H * NSLOT * P, P], BF16)
    qloc = nc.dram_tensor("qloc", [H * P, TOK], BF16)
    HH = H // 2
    kgathA = nc.dram_tensor("kgathA", [NCORES * HH * NSLOT * P, P], BF16)
    kgathB = nc.dram_tensor("kgathB", [NCORES * HH * NSLOT * P, P], BF16)
    vgathA = nc.dram_tensor("vgathA", [NCORES * HH * NSLOT * P, P], BF16)
    vgathB = nc.dram_tensor("vgathB", [NCORES * HH * NSLOT * P, P], BF16)

    kloc4 = kloc.ap().rearrange("(h s d) t -> h s d t", h=H, s=NSLOT)
    vloc4 = vloc.ap().rearrange("(h s t) d -> h s t d", h=H, s=NSLOT)
    qloc3 = qloc.ap().rearrange("(h d) t -> h d t", h=H)
    kg5A = kgathA.ap().rearrange("(r h s d) t -> r h s d t", r=NCORES, h=HH, s=NSLOT)
    kg5B = kgathB.ap().rearrange("(r h s d) t -> r h s d t", r=NCORES, h=HH, s=NSLOT)
    vg5A = vgathA.ap().rearrange("(r h s t) d -> r h s t d", r=NCORES, h=HH, s=NSLOT)
    vg5B = vgathB.ap().rearrange("(r h s t) d -> r h s t d", r=NCORES, h=HH, s=NSLOT)

    NWB = 3
    OVLW = max(
        3 * TOK + HGRP * TOK // 2 + 512 + HGRP * P,
        7 * TOK + 512,
        4 * TOK + TOK + 1024 + 780 + 1536,
    )

    from contextlib import ExitStack
    with ExitStack() as _ctx:
        ec = _ctx.enter_context
        XS = ec(nc.sbuf_tensor("XS", [P, KC * TOK], F32))
        XH = ec(nc.sbuf_tensor("XH", [P, KC * TOK], BF16))
        RS = ec(nc.sbuf_tensor("RS", [P, KC * TOK], F32))
        WRING = ec(nc.sbuf_tensor("WRING", [P, NWB * KC * P], BF16))
        OVL = ec(nc.sbuf_tensor("OVL", [P, OVLW], F32))
        IDB = ec(nc.sbuf_tensor("IDB", [P, P], BF16))
        PRM = ec(nc.sbuf_tensor("PRM", [P, P], BF16))
        ONESC = ec(nc.sbuf_tensor("ONESC", [P, 1], BF16))
        ONESR = ec(nc.sbuf_tensor("ONESR", [1, P], BF16))
        pS0 = ec(nc.psum_tensor("pS0", [P, 1024], F32))
        pS1 = ec(nc.psum_tensor("pS1", [P, 1024], F32))
        pD0 = ec(nc.psum_tensor("pD0", [P, 512], F32))
        pD1 = ec(nc.psum_tensor("pD1", [P, 512], F32))
        pD2 = ec(nc.psum_tensor("pD2", [P, 512], F32))
        pD3 = ec(nc.psum_tensor("pD3", [P, 512], F32))
        ld0_s = ec(nc.semaphore("ld0"))
        qstg_s = ec(nc.semaphore("qstg"))
        kstg_s = ec(nc.semaphore("kstg"))
        vstg_s = ec(nc.semaphore("vstg"))
        kstg2_s = ec(nc.semaphore("kstg2"))
        vstg2_s = ec(nc.semaphore("vstg2"))
        qb_ss = [ec(nc.semaphore(f"qb{i}")) for i in range(2)]
        xc_ss = [ec(nc.semaphore(f"xc{i}")) for i in range(KC)]
        kb_ss = [ec(nc.semaphore(f"kb{i}")) for i in range(4)]
        vb_ss = [ec(nc.semaphore(f"vb{i}")) for i in range(3)]
        ring_ss = [ec(nc.semaphore(f"ring{i}")) for i in range(4)]
        wd_ss = [ec(nc.semaphore(f"wdb{i}")) for i in range(2)]
        tv_s = ec(nc.semaphore("tv"))
        vv_s = ec(nc.semaphore("vv"))
        sv_s = ec(nc.semaphore("sv"))
        cc_s = ec(nc.semaphore("cc"))
        block = ec(nc.Block())
        ld0 = Ctr(ld0_s, 16)
        qstg = Ctr(qstg_s, 16)
        kstg = Ctr(kstg_s, 16)
        vstg = Ctr(vstg_s, 16)
        kstg2 = Ctr(kstg2_s, 16)
        vstg2 = Ctr(vstg2_s, 16)
        qb_c = [Ctr(s, 16) for s in qb_ss]
        xc_c = [Ctr(s, 16) for s in xc_ss]
        kb_c = [Ctr(s, 16) for s in kb_ss]
        vb_c = [Ctr(s, 16) for s in vb_ss]
        ring_c = [Ctr(s, 16) for s in ring_ss]
        wd_c = [Ctr(s, 16) for s in wd_ss]
        tv = Ctr(tv_s, 1)
        vv = Ctr(vv_s, 1)
        sv = Ctr(sv_s, 1)
        cc = Ctr(cc_s, 1)
        pl = Plan()

        ovl = OVL.ap()

        def carve_bf(c0, n):
            assert n % 2 == 0
            return ovl[:, c0:c0 + n // 2].bitcast(BF16)

        def carve_f(c0, n):
            return ovl[:, c0:c0 + n]

        COS2 = carve_bf(0, TOK)
        SIN2 = carve_bf(TOK // 2, TOK)
        QKB = carve_bf(TOK, TOK)
        RTMP = carve_bf(TOK + TOK // 2, TOK)
        RU = carve_bf(2 * TOK, TOK)
        QSTG = carve_bf(2 * TOK + TOK // 2, TOK)
        KSTG = carve_bf(3 * TOK, TOK)
        VTSTG = carve_bf(3 * TOK + TOK // 2, TOK)
        SQA_A = carve_bf(4 * TOK, TOK)
        SQB_A = VTSTG
        SMALLS_A = carve_f(4 * TOK + TOK // 2, 2 * TOK + TOK // 2)
        TMPA_A = carve_f(7 * TOK, 512)
        TMPA2_A = carve_f(4 * TOK, min(512, TOK // 2))

        _b = 4 * TOK
        QTB = carve_bf(_b, 2 * TOK)
        _b += TOK
        KB = carve_bf(_b, 2048)
        _b += 1024
        VB = carve_bf(_b, 1560)
        _b += 780
        PB = carve_bf(_b, 3072)
        _b += 1536
        CFGB_END = _b
        MASK = carve_bf(0, NCORES * P)
        OBS = carve_bf(NCORES * P // 2, NW * P)
        LINV = carve_f(NCORES * P // 2 + NW * P // 2, max(NW, 4))
        assert NCORES * P // 2 + NW * P // 2 + max(NW, 4) <= 4 * TOK
        assert CFGB_END <= OVLW

        SIL = carve_bf(0, TOK)
        HREG = carve_bf(TOK // 2, HGRP * TOK)
        SMALLS_C = carve_f(TOK // 2 + HGRP * TOK // 2, 2 * TOK + TOK // 2)
        TMPA_C = carve_f(3 * TOK + HGRP * TOK // 2, 512)
        TMPA2_C = carve_f(4 * TOK, min(512, TOK // 2))
        WDBUF = carve_bf(3 * TOK + HGRP * TOK // 2 + 512, 2 * HGRP * P)
        assert 3 * TOK + HGRP * TOK // 2 + 512 + HGRP * P <= OVLW, OVLW

        banks = {"S0": pS0, "S1": pS1,
                 "D0": pD0, "D1": pD1, "D2": pD2, "D3": pD3}
        tpsum = pD2.ap().bitcast(BF16)
        tpsum_w = pD0.ap().bitcast(BF16)
        bank_free = {k: None for k in banks}

        def t_mm(out, lhsT, rhs, start, stop, inc=False):
            return pl.emit(
                "t", lambda t: t.matmul(out, lhsT, rhs, start=start, stop=stop),
                tv if inc else None)

        def wait_bank(eng, name):
            bf = bank_free[name]
            if bf is not None:
                pl.wait(eng, bf[0], bf[1])

        ring_srcs = (
            [(wq_d, wk_d, wv_d)[pn][h] for h in range(H) for pn in range(3)]
            + [wo_d[mc] for mc in range(KC)]
            + [(wg_d, wu_d)[pn][hc] for hc in range(HC) for pn in range(2)]
        )
        NR = len(ring_srcs)
        ring_consumed = {}
        ring_loaded = {}
        ring_next = [0]

        def ring_buf3(i):
            return WRING.ap()[:, (i % NWB) * KC * P:(i % NWB + 1) * KC * P].rearrange(
                "p (kc d) -> p kc d", d=P)

        def ring_ensure(upto):
            while ring_next[0] <= min(upto, NR - 1):
                j = ring_next[0]
                if j >= NWB:
                    c, vval = ring_consumed[j - NWB]
                    pl.wait("g", c, vval)
                rc = ring_c[j % NWB]
                ring_loaded[j] = (rc, pl.emit(
                    "g", lambda g, s3=ring_srcs[j], d3=ring_buf3(j):
                    g.dma_start(out=d3, in_=s3), rc))
                ring_next[0] += 1

        ring_ensure(NWB - 1)

        x3s = x_d.rearrange("(kc p) t -> p kc t", p=P)
        XS3 = XS.ap().rearrange("p (kc t) -> p kc t", t=TOK)
        L_cos = pl.emit("s", lambda s: s.dma_start(out=COS2, in_=cos_d), ld0)
        L_sin = pl.emit("s", lambda s: s.dma_start(out=SIN2, in_=sin_d), ld0)
        L_idn = pl.emit("s", lambda s: s.dma_start(out=IDB[:], in_=idn_d), ld0)
        L_prm = pl.emit("s", lambda s: s.dma_start(out=PRM[:], in_=prm_d), ld0)
        L_ph0 = ld0.val
        L_xc = {}
        for kc in range(KC):
            xq = "y" if (kc < 4 or kc % 2 == 1) else "s"
            L_xc[kc] = pl.emit(xq, lambda s, k=kc: s.dma_start(
                out=XS3[:, k, :], in_=x3s[:, k, :]), xc_c[kc])

        pl.emit("v", lambda v: v.memset(ONESC[:], 1.0), vv)
        V_ones = pl.emit("v", lambda v: v.memset(ONESR[:], 1.0), vv)

        def ln(src_chunk, cast_slot, sq_slot, apply_out, smalls, tmpa,
               pre_wait_v=None, pre_wait_t=None, cast_dbuf=False,
               kc_done_cb=None, pre_kc=None, pre_sq=None):
            V_cast = {}
            S_sq = {}
            T_statk = {}
            for kc in range(KC):
                if kc == 0:
                    if pre_wait_v is not None:
                        pl.wait("v", *pre_wait_v)
                    pl.wait("v", ld0, ld0.val)
                if pre_kc is not None:
                    pl.wait("v", *pre_kc(kc))
                if cast_dbuf and kc >= 2:
                    pl.wait("v", tv, T_statk[kc - 2])
                V_cast[kc] = pl.emit(
                    "v", lambda v, s=src_chunk(kc), d=cast_slot(kc):
                    v.tensor_copy(d, s), vv)
                if kc == 0:
                    pl.wait("s", ld0, ld0.val)
                    if pre_wait_v is not None:
                        pl.wait("s", *pre_wait_v)
                    if pre_sq is not None:
                        pl.wait("s", *pre_sq)
                if pre_kc is not None:
                    pl.wait("s", *pre_kc(kc))
                if kc >= 2:
                    pl.wait("s", tv, T_statk[kc - 2])
                S_sq[kc] = pl.emit(
                    "s", lambda s, a=src_chunk(kc), d=sq_slot(kc):
                    s.activation(d, a, AF.Square), sv)
                if kc == 0:
                    if pre_wait_t is not None:
                        pl.wait("t", *pre_wait_t)
                    for bname in ("D0", "D1", "D2", "D3"):
                        wait_bank("t", bname)
                    pl.wait("t", vv, V_ones)
                pl.wait("t", vv, V_cast[kc])
                pl.wait("t", sv, S_sq[kc])
                st, sp = kc == 0, kc == KC - 1
                t_mm(pD0[0:1, 0:TH], ONESC[:], cast_slot(kc)[:, 0:TH], st, sp)
                t_mm(pD1[0:1, 0:TH], ONESC[:], cast_slot(kc)[:, TH:TOK], st, sp)
                t_mm(pD2[0:1, 0:TH], ONESC[:], sq_slot(kc)[:, 0:TH], st, sp)
                T_statk[kc] = t_mm(pD3[0:1, 0:TH], ONESC[:],
                                   sq_slot(kc)[:, TH:TOK], st, sp, inc=True)
            T_stats = T_statk[KC - 1]

            STTA = smalls[0:1, 0:TOK]
            STTB = smalls[0:1, TOK:2 * TOK]
            meanb = smalls[0:1, 2 * TOK:2 * TOK + TOK // 2].bitcast(BF16)
            rstdb = meanb

            def chain_v(fn):
                pl.wait("v", vv, vv.val)
                return pl.emit("v", fn, vv)

            pl.wait("v", tv, T_stats)
            pl.emit("v", lambda v: v.tensor_copy(STTA[:, 0:TH], pD0[0:1, 0:TH]), vv)
            chain_v(lambda v: v.tensor_copy(STTA[:, TH:TOK], pD1[0:1, 0:TH]))
            chain_v(lambda v: v.tensor_copy(STTB[:, 0:TH], pD2[0:1, 0:TH]))
            V_evac = chain_v(lambda v: v.tensor_copy(STTB[:, TH:TOK],
                                                     pD3[0:1, 0:TH]))
            inv = 1.0 / DM
            chain_v(lambda v: v.tensor_scalar_mul(STTA, STTA, inv))
            V_mb = chain_v(lambda v: v.tensor_copy(meanb, STTA))
            chain_v(lambda v: v.tensor_mul(STTA, STTA, STTA))
            chain_v(lambda v: v.tensor_scalar_mul(STTB, STTB, inv))
            chain_v(lambda v: v.tensor_sub(STTB, STTB, STTA))
            V_var = chain_v(lambda v: v.tensor_scalar_add(STTB, STTB, 1e-5))
            pl.wait("s", vv, V_var)
            c = pl.emit("s", lambda s: s.activation(STTA, STTB, AF.Ln), sv)
            pl.wait("s", sv, c)
            S_rstd = pl.emit("s", lambda s: s.activation(STTB, STTA, AF.Exp,
                                                         scale=-0.5), sv)

            pl.wait("t", vv, V_evac)
            pl.wait("t", vv, V_mb)
            t_mm(pD0[:, 0:TH], ONESR[:], meanb[0:1, 0:TH], True, True)
            T_bcm = t_mm(pD1[:, 0:TH], ONESR[:], meanb[0:1, TH:TOK],
                         True, True, inc=True)
            pl.wait("v", tv, T_bcm)
            pl.wait("v", sv, S_rstd)
            V_rb = chain_v(lambda v: v.tensor_copy(rstdb, STTB))
            pl.wait("t", vv, V_rb)
            t_mm(pD2[:, 0:TH], ONESR[:], rstdb[0:1, 0:TH], True, True)
            T_bc = t_mm(pD3[:, 0:TH], ONESR[:], rstdb[0:1, TH:TOK],
                        True, True, inc=True)

            last_apply = None
            kc_ms = {}
            tmpa_last = {}
            pair_i = 0
            for kc in range(KC):
                for half in range(2):
                    mb = (pD0 if half == 0 else pD1)[:, 0:TH]
                    rb = (pD2 if half == 0 else pD3)[:, 0:TH]
                    if kc == 0 and half == 0:
                        pl.wait("v", tv, T_bc)
                    sl = slice(half * TH, (half + 1) * TH)
                    ta = tmpa[pair_i % 2]
                    if pair_i >= 2:
                        pl.wait("v", vv, tmpa_last[pair_i % 2])
                    pl.emit("v", lambda v, s=src_chunk(kc), m=mb, d=ta, q=sl:
                            v.tensor_sub(d[:, 0:TH], s[:, q], m), vv)
                    last_apply = apply_out(kc, half, ta, rb)
                    tmpa_last[pair_i % 2] = last_apply
                    pair_i += 1
                    if half == 1:
                        kc_ms[kc] = last_apply
                        if kc_done_cb is not None:
                            kc_done_cb(kc, last_apply)
            for bname in ("D0", "D1", "D2", "D3"):
                bank_free[bname] = (vv, last_apply)
            return last_apply, kc_ms

        def xs_chunk(kc):
            return XS.ap()[:, kc * TOK:(kc + 1) * TOK]

        def xh_chunk(kc):
            return XH.ap()[:, kc * TOK:(kc + 1) * TOK]

        def rs_chunk(kc):
            return RS.ap()[:, kc * TOK:(kc + 1) * TOK]

        def ln12_apply(kc, half, tmpa, rb):
            sl = slice(half * TH, (half + 1) * TH)
            pl.wait("v", vv, vv.val)
            return pl.emit("v", lambda v, d=xh_chunk(kc), a=tmpa, r=rb, q=sl:
                           v.tensor_mul(d[:, q], a[:, 0:TH], r), vv)

        def ln34_apply(kc, half, tmpa, rb):
            sl = slice(half * TH, (half + 1) * TH)
            pl.wait("v", vv, vv.val)
            pl.emit("v", lambda v, a=tmpa, r=rb:
                    v.tensor_mul(a[:, 0:TH], a[:, 0:TH], r), vv)
            pl.wait("v", vv, vv.val)
            return pl.emit("v", lambda v, d=xs_chunk(kc), a=tmpa, q=sl:
                           v.tensor_add(d[:, q], d[:, q], a[:, 0:TH]), vv)

        V_xhat1, M_xhat1 = ln(
            xs_chunk, cast_slot=xh_chunk,
            sq_slot=lambda kc: SQA_A if kc % 2 == 0 else SQB_A,
            apply_out=ln12_apply, smalls=SMALLS_A, tmpa=(TMPA_A, TMPA2_A),
            pre_kc=lambda kc: (xc_c[kc], L_xc[kc]))

        pair = {0: "S0", 1: "S1"}
        L_kst = {}
        L_vst = {}
        L_qst = {}
        _qkv_state = {"qkb_free": None, "tpsum_free": None, "V_rope_last": 0,
                      "C_k1": None, "C_k2": None, "C_v1": None, "C_v2": None}

        qkv_tails = []

        def qkv_flush_tail():
            if qkv_tails:
                qkv_tails.pop(0)()

        def make_tail(h, pnum, i, bn, b, T_proj):
            def tail():
                self_ = _qkv_state
                pl.wait("s", tv, T_proj)
                for w_ in (self_["qkb_free"] or []):
                    pl.wait("s", *w_)
                c = pl.emit("s", lambda s, b=b:
                            s.activation(QKB[:, 0:TH], b[:, 0:TH], AF.Copy), sv)
                pl.wait("s", sv, c)
                S_evac = pl.emit("s", lambda s, b=b:
                                 s.activation(QKB[:, TH:TOK], b[:, 512:512 + TH],
                                              AF.Copy), sv)
                bank_free[bn] = (sv, S_evac)

                if pnum < 2:
                    stg = QSTG if pnum == 0 else KSTG
                    stgc = qstg if pnum == 0 else (kstg if h < HH else kstg2)
                    wait_bank("t", "D0")
                    wait_bank("t", "D1")
                    pl.wait("t", sv, S_evac)
                    t_mm(pD0[:, 0:TH], PRM[:], QKB[:, 0:TH], True, True)
                    T_swp = t_mm(pD1[:, 0:TH], PRM[:], QKB[:, TH:TOK], True,
                                 True, inc=True)
                    if pnum == 1:
                        pl.wait("v", kstg, kstg.val)
                        pl.wait("v", kstg2, kstg2.val)
                    else:
                        pl.wait("v", stgc, stgc.val)
                    pl.wait("v", sv, S_evac)
                    if h == 0 and pnum == 0:
                        pl.wait("v", ld0, L_ph0)
                    pl.wait("v", vv, vv.val)
                    c1 = pl.emit("v", lambda v: v.tensor_mul(RTMP, QKB, COS2), vv)
                    self_["qkb_free"] = [(vv, c1), (tv, T_swp)]
                    pl.wait("v", tv, T_swp)
                    pl.emit("v", lambda v: v.tensor_mul(
                        RU[:, 0:TH], pD0[:, 0:TH], SIN2[:, 0:TH]), vv)
                    c2 = pl.emit("v", lambda v: v.tensor_mul(
                        RU[:, TH:TOK], pD1[:, 0:TH], SIN2[:, TH:TOK]), vv)
                    bank_free["D0"] = (vv, c2)
                    bank_free["D1"] = (vv, c2)
                    pl.wait("v", vv, vv.val)
                    V_rope = pl.emit("v", lambda v, d=stg: v.tensor_add(
                        d, RTMP, RU), vv)
                    self_["V_rope_last"] = V_rope
                    pl.wait("y", vv, V_rope)
                    if pnum == 0:
                        L_qst[h] = pl.emit(
                            "y", lambda s, hh=h: s.dma_start(out=qloc3[hh],
                                                             in_=QSTG), qstg)
                    else:
                        ks = KSTG.rearrange("d (s t) -> d s t", s=NSLOT)
                        kd = kloc4[h].rearrange("s d t -> d s t")
                        L_kst[h] = pl.emit(
                            "y", lambda s, a=kd, b2=ks: s.dma_start(out=a, in_=b2),
                            stgc)
                        if h == HH and pnum == 1:
                            pl.wait("g", kstg, kstg.val)
                            self_["C_k1"] = pl.emit(
                                "g", lambda g: g.collective_compute(
                                    "AllGather", ALU.bypass,
                                    replica_groups=[list(range(NCORES))],
                                    ins=[kloc.ap()[0:HH * NSLOT * P, :].opt()],
                                    outs=[kgathA.ap().opt()]), cc)

                else:
                    vstgc = vstg if h < HH else vstg2
                    if self_["tpsum_free"] is not None:
                        pl.wait("t", *self_["tpsum_free"])
                    wait_bank("t", "D2")
                    pl.wait("t", sv, S_evac)
                    for sl_ in range(NSLOT):
                        T_vtr = pl.emit(
                            "t", lambda t, ss=sl_: t.transpose(
                                tpsum[:, ss * P:(ss + 1) * P],
                                QKB[:, ss * P:(ss + 1) * P], IDB[:]),
                            tv if sl_ == NSLOT - 1 else None)
                    self_["qkb_free"] = [(tv, T_vtr)]
                    pl.wait("s", vstg, vstg.val)
                    pl.wait("s", vstg2, vstg2.val)
                    pl.wait("s", tv, T_vtr)
                    S_vt = pl.emit("s", lambda s: s.activation(
                        VTSTG, tpsum[:, 0:NSLOT * P], AF.Copy), sv)
                    self_["tpsum_free"] = (sv, S_vt)
                    bank_free["D2"] = (sv, S_vt)
                    pl.wait("y", sv, S_vt)
                    vs = VTSTG.rearrange("t (s d) -> t s d", s=NSLOT)
                    vd = vloc4[h].rearrange("s t d -> t s d")
                    L_vst[h] = pl.emit(
                        "y", lambda s, a=vd, b2=vs: s.dma_start(out=a, in_=b2),
                        vstgc)
                    if h == HH:
                        pl.wait("g", vstg, vstg.val)
                        self_["C_v1"] = pl.emit(
                            "g", lambda g: g.collective_compute(
                                "AllGather", ALU.bypass,
                                replica_groups=[list(range(NCORES))],
                                ins=[vloc.ap()[0:HH * NSLOT * P, :].opt()],
                                outs=[vgathA.ap().opt()]), cc)

            return tail

        for h in range(H):
            for pnum in range(3):
                i = 3 * h + pnum
                ring_ensure(i + NWB - 1)
                bn = pair[i % 2]
                b = banks[bn]
                pl.wait("t", *ring_loaded[i])
                if i == 0:
                    pl.wait("t", ld0, L_ph0)
                wait_bank("t", bn)
                wb3 = ring_buf3(i)
                for kc in range(KC):
                    if i == 0:
                        pl.wait("t", vv, M_xhat1[kc])
                    st, sp = kc == 0, kc == KC - 1
                    t_mm(b[:, 0:TH], wb3[:, kc, :], xh_chunk(kc)[:, 0:TH], st, sp)
                    T_proj = t_mm(b[:, 512:512 + TH], wb3[:, kc, :],
                                  xh_chunk(kc)[:, TH:TOK], st, sp,
                                  inc=(kc == KC - 1))
                ring_consumed[i] = (tv, T_proj)
                qkv_flush_tail()
                qkv_tails.append(make_tail(h, pnum, i, bn, b, T_proj))
        qkv_flush_tail()
        C_k1 = _qkv_state["C_k1"]
        C_v1 = _qkv_state["C_v1"]
        C_k2 = None
        C_v2 = None
        T_sc_last = None
        V_rope_last = _qkv_state["V_rope_last"]


        pl.wait("s", vv, _qkv_state["V_rope_last"])
        L_msk = pl.emit("s", lambda s: s.dma_start(out=MASK, in_=msk_d), ld0)

        pl.wait("v", vv, vv.val)
        V_vbones = None
        for par in range(3):
            for i4 in range(4):
                col = par * 520 + i4 * 130 + 128
                V_vbones = pl.emit(
                    "v", lambda v, cl=col: v.memset(VB[:, cl:cl + 1], 1.0), vv)

        sbank_names = ["S0", "S1"]
        obank_names = ["D0", "D1", "D2", "D3"]
        kb_free = {}
        vb_free = {}
        pb_free = {}
        L_q = {}
        ii = 0
        jj = 0
        S_ot_last = 0
        S_ot_head = {}
        T_otr_last = [0]
        waves = [(0, NW), (NW, NSLOT)]

        for h in range(2):
            pl.wait("s", qstg, qstg.val)
            pl.wait("s", vv, V_rope_last)
            L_q[h] = pl.emit("s", lambda s, hh=h: s.dma_start(
                out=QTB[:, (hh % 2) * TOK:(hh % 2 + 1) * TOK], in_=qloc3[hh]),
                qb_c[h % 2])

        for h in range(H):
            if h == min(2, HH) and C_k2 is None:
                if T_sc_last is not None:
                    pl.wait("g", tv, T_sc_last)
                pl.wait("g", kstg2, kstg2.val)
                C_k2 = pl.emit("g", lambda g: g.collective_compute(
                    "AllGather", ALU.bypass,
                    replica_groups=[list(range(NCORES))],
                    ins=[kloc.ap()[HH * NSLOT * P:, :].opt()],
                    outs=[kgathB.ap().opt()]), cc)
                pl.wait("g", vstg2, vstg2.val)
                C_v2 = pl.emit("g", lambda g: g.collective_compute(
                    "AllGather", ALU.bypass,
                    replica_groups=[list(range(NCORES))],
                    ins=[vloc.ap()[HH * NSLOT * P:, :].opt()],
                    outs=[vgathB.ap().opt()]), cc)
            qv = QTB[:, (h % 2) * TOK:(h % 2 + 1) * TOK]
            T_sc_last = None
            for iw, (s_lo, s_hi) in enumerate(waves):
                jmax = 2 * (s_hi - 1) + 2
                T_stop = {}
                pending = []
                pairbuf = []
                np_ = [0]
                ns_ = [0]

                def flush_pair():
                    if not pairbuf:
                        return
                    pp = np_[0] % 3
                    sbank = ns_[0] % 2
                    np_[0] += 1
                    ns_[0] += 1
                    sbn = sbank_names[sbank]
                    sb_v = banks[sbn]
                    n_it = len(pairbuf)
                    pbv_pair = PB[:, pp * 1024:pp * 1024 + 1024]
                    while pending and any(e.get("ppar") == pp for e in pending):
                        emit_pv(pending.pop(0))
                    if pp in pb_free:
                        pl.wait("s", *pb_free[pp])
                    pl.wait("s", tv, pairbuf[-1]["T_sc"])
                    S_exp = pl.emit(
                        "s", lambda sc, d=pbv_pair[:, 0:n_it * 512],
                        b=sb_v[:, 0:n_it * 512]:
                        sc.activation(d, b, AF.Exp, scale=1.0 / 128.0), sv)
                    bank_free[sbn] = (sv, S_exp)
                    for it in pairbuf:
                        pv_ctr, pv_val = sv, S_exp
                        if it["diag"] is not None:
                            half = it["diag"]
                            pc = it["pcols"]
                            stride = pc[1] - pc[0]
                            if stride == P:
                                p_ap = pbv_pair.rearrange(
                                    "p (kt q) -> p kt q", q=P)[:, pc[0] // P:
                                                               pc[0] // P + 4, :]
                            else:
                                off = pc[0]
                                p_ap = pbv_pair.rearrange(
                                    "p (kt q) -> p kt q", q=stride)[:, :, off:off + P]
                            m_ap = MASK[:, half * 512:(half + 1) * 512].rearrange(
                                "p (kt q) -> p kt q", q=P)
                            pl.wait("v", ld0, L_msk)
                            pl.wait("v", sv, S_exp)
                            V_msk = pl.emit(
                                "v", lambda v, d=p_ap, m=m_ap:
                                v.tensor_mul(d, d, m), vv)
                            pv_ctr, pv_val = vv, V_msk
                        it["pv_ctr"], it["pv_val"] = pv_ctr, pv_val
                        it["ppar"] = pp
                        it["pbv"] = pbv_pair
                        pending.append(it)
                    pairbuf.clear()

                def emit_pv(p):
                    pl.wait("t", p["pv_ctr"], p["pv_val"])
                    pl.wait("t", vb_c[p["jpar"]], p["L_v"])
                    if p["first"]:
                        wait_bank("t", p["ob"])
                    for kt in range(4):
                        pc = p["pcols"][kt]
                        T_pv = t_mm(
                            p["ob_t"][:, 0:129],
                            p["pbv"][:, pc:pc + P],
                            VB[:, p["jpar"] * 520 + kt * 130:
                               p["jpar"] * 520 + kt * 130 + 129],
                            start=(p["first"] and kt == 0),
                            stop=(p["lastj"] and kt == 3),
                            inc=(kt == 3))
                    pbf = pb_free.get(p["ppar"])
                    pb_free[p["ppar"]] = (tv, T_pv)
                    vb_free[p["jpar"]] = (tv, T_pv)
                    if p["lastj"]:
                        T_stop[p["s_"]] = T_pv

                jj0 = jj

                kloads = {}
                vloads = {}

                def issue_k(jn):
                    if jn in kloads or jn >= jmax:
                        return
                    jsq = jj0 + jn
                    par = jsq % 4
                    r0n = (4 * jn) % NCORES
                    spn = jn // 2
                    kg5 = kg5A if h < HH else kg5B
                    hl = h if h < HH else h - HH
                    if iw == 0 and jn == 0:
                        pl.wait("y", cc, C_k1 if h < HH else C_k2)
                    if par in kb_free:
                        pl.wait("y", *kb_free[par])
                    ksrc = kg5[r0n:r0n + 4, hl, spn, :, :].rearrange("r d t -> d r t")
                    kdst = KB[:, par * 512:(par + 1) * 512].rearrange(
                        "d (r t) -> d r t", r=4)
                    kloads[jn] = (pl.emit("y", lambda s, a=kdst, b=ksrc:
                                          s.dma_start(out=a, in_=b), kb_c[par]), par)

                def issue_v(jn):
                    if jn in vloads or jn >= jmax:
                        return
                    jsq = jj0 + jn
                    par = jsq % 3
                    r0n = (4 * jn) % NCORES
                    spn = jn // 2
                    vg5 = vg5A if h < HH else vg5B
                    hl = h if h < HH else h - HH
                    if ii == 0 and jn == 0:
                        pl.wait("s", vv, V_vbones)
                    if iw == 0 and jn == 0:
                        pl.wait("s", cc, C_v1 if h < HH else C_v2)
                    if par in vb_free:
                        pl.wait("s", *vb_free[par])
                    vsrc = vg5[r0n:r0n + 4, hl, spn, :, :].rearrange("r t d -> t r d")
                    vdst = VB[:, par * 520:par * 520 + 520].rearrange(
                        "t (r e) -> t r e", e=130)[:, :, 0:128]
                    vloads[jn] = (pl.emit("s", lambda s, a=vdst, b=vsrc:
                                          s.dma_start(out=a, in_=b), vb_c[par]), par)

                for j in range(jmax):
                    jseq = jj
                    jj += 1
                    while pairbuf and pairbuf[0]["jseq"] <= jseq - 2:
                        flush_pair()
                    while pending and pending[0]["jseq"] <= jseq - 2:
                        emit_pv(pending.pop(0))
                    issue_k(j)
                    issue_v(j)
                    issue_v(j + 1)
                    issue_k(j + 1)
                    issue_k(j + 2)
                    L_k, kpar = kloads[j]
                    L_v, jpar = vloads[j]
                    slots = list(range(max(s_lo, j // 2), s_hi))
                    si_ = 0
                    while si_ < len(slots):
                        sbn = sbank_names[ns_[0] % 2]
                        sb_v = banks[sbn]
                        q_ = len(pairbuf)
                        two = (q_ == 0 and si_ + 1 < len(slots))
                        if q_ == 0:
                            wait_bank("t", sbn)
                        pl.wait("t", kb_c[kpar], L_k)
                        pl.wait("t", qb_c[h % 2], L_q[h])
                        if two:
                            s_a = slots[si_]
                            for kt in range(4):
                                T_sc = t_mm(
                                    sb_v[:, kt * 256:(kt + 1) * 256],
                                    KB[:, kpar * 512 + kt * P:
                                       kpar * 512 + (kt + 1) * P],
                                    qv[:, s_a * P:(s_a + 2) * P],
                                    True, True, inc=(kt == 3))
                            for idx in range(2):
                                s_ = slots[si_ + idx]
                                diag = None
                                if j == 2 * s_ or j == 2 * s_ + 1:
                                    diag = j - 2 * s_
                                pairbuf.append(dict(
                                    T_sc=T_sc, diag=diag, jpar=jpar, jseq=jseq,
                                    pcols=[kt * 256 + idx * P for kt in range(4)],
                                    ob=obank_names[s_ - s_lo],
                                    ob_t=banks[obank_names[s_ - s_lo]],
                                    first=(j == 0), lastj=(j == 2 * s_ + 1),
                                    L_v=L_v, s_=s_))
                            si_ += 2
                            ii += 2
                        else:
                            s_ = slots[si_]
                            for kt in range(4):
                                T_sc = t_mm(
                                    sb_v[:, q_ * 512 + kt * P:
                                         q_ * 512 + (kt + 1) * P],
                                    KB[:, kpar * 512 + kt * P:
                                       kpar * 512 + (kt + 1) * P],
                                    qv[:, s_ * P:(s_ + 1) * P],
                                    True, True, inc=(kt == 3))
                            diag = None
                            if j == 2 * s_ or j == 2 * s_ + 1:
                                diag = j - 2 * s_
                            pairbuf.append(dict(
                                T_sc=T_sc, diag=diag, jpar=jpar, jseq=jseq,
                                pcols=[q_ * 512 + kt * P for kt in range(4)],
                                ob=obank_names[s_ - s_lo],
                                ob_t=banks[obank_names[s_ - s_lo]],
                                first=(j == 0), lastj=(j == 2 * s_ + 1),
                                L_v=L_v, s_=s_))
                            si_ += 1
                            ii += 1
                        T_sc_last = T_sc
                        kb_free[kpar] = (tv, T_sc)
                        if len(pairbuf) == 2:
                            flush_pair()
                        while len(pending) > 3:
                            emit_pv(pending.pop(0))

                flush_pair()
                while pending:
                    emit_pv(pending.pop(0))

                V_on = {}
                for s_ in range(s_lo, s_hi):
                    si = s_ - s_lo
                    ob = obank_names[si]
                    ob_t = banks[ob]
                    li = LINV[:, si:si + 1]
                    pl.wait("v", tv, T_stop[s_])
                    c3 = pl.emit("v", lambda v, o=ob_t, l=li:
                                 v.reciprocal(l, o[:, 128:129]), vv)
                    pl.wait("v", vv, c3)
                    V_on[si] = pl.emit(
                        "v", lambda v, o=ob_t, l=li,
                        d=OBS[:, si * P:(si + 1) * P]:
                        v.tensor_scalar_mul(d, o[:, 0:128], l), vv)
                    bank_free[ob] = (vv, V_on[si])
                T_otr = None
                for si in range(s_hi - s_lo):
                    pl.wait("t", vv, V_on[0])
                    pl.wait("t", vv, V_on[si])
                    T_otr = pl.emit("t", lambda t, ss=si: t.transpose(
                        tpsum_w[:, ss * P:(ss + 1) * P],
                        OBS[:, ss * P:(ss + 1) * P], IDB[:]), tv)
                T_otr_last[0] = T_otr
                pl.wait("s", tv, T_otr)
                S_ot = pl.emit("s", lambda sc, hh=h, w=iw: sc.activation(
                    XH.ap()[:, hh * TOK + w * NW * P:hh * TOK + (w + 1) * NW * P],
                    tpsum_w[:, 0:NW * P], AF.Copy), sv)
                S_ot_last = S_ot
                S_ot_head[h] = S_ot
                bank_free["D0"] = (sv, S_ot)

            hh2 = h + 2
            if hh2 < H:
                pl.wait("s", tv, T_sc_last)
                L_q[hh2] = pl.emit("s", lambda s, q2=hh2: s.dma_start(
                    out=QTB[:, (q2 % 2) * TOK:(q2 % 2 + 1) * TOK],
                    in_=qloc3[q2]), qb_c[hh2 % 2])

        V_ro = {}
        T_oproj = {}
        for mc in range(KC):
            i = 3 * H + mc
            ring_ensure(i + NWB - 1)
            bn = pair[mc % 2]
            b = banks[bn]
            pl.wait("t", *ring_loaded[i])
            wait_bank("t", bn)
            wb3 = ring_buf3(i)
            for hh in range(H):
                if mc == 0:
                    pl.wait("t", sv, S_ot_head[hh])
                st, sp = hh == 0, hh == H - 1
                ot = XH.ap()[:, hh * TOK:(hh + 1) * TOK]
                t_mm(b[:, 0:TH], wb3[:, hh, :], ot[:, 0:TH], st, sp)
                T_o = t_mm(b[:, 512:512 + TH], wb3[:, hh, :], ot[:, TH:TOK], st, sp,
                           inc=(hh == H - 1))
            T_oproj[mc] = T_o
            ring_consumed[i] = (tv, T_o)
            pl.wait("v", tv, T_o)
            pl.emit("v", lambda v, b=b, d=rs_chunk(mc):
                    v.tensor_copy(d[:, 0:TH], b[:, 0:TH]), vv)
            V_ro[mc] = pl.emit("v", lambda v, b=b, d=rs_chunk(mc):
                               v.tensor_copy(d[:, TH:TOK], b[:, 512:512 + TH]), vv)
            bank_free[bn] = (vv, V_ro[mc])

        V_res1, M_res1 = ln(
            rs_chunk,
            cast_slot=lambda kc: xh_chunk(kc % 2),
            sq_slot=lambda kc: HREG[:, (kc % 2) * TOK:(kc % 2 + 1) * TOK],
            apply_out=ln34_apply, smalls=SMALLS_C, tmpa=(TMPA_C, TMPA2_C),
            pre_kc=lambda kc: (vv, V_ro[kc]), pre_wait_t=(tv, T_oproj[KC - 1]),
            cast_dbuf=True)

        V_xhat2, M_xhat2 = ln(
            xs_chunk, cast_slot=xh_chunk,
            sq_slot=lambda kc: HREG[:, (kc % 2) * TOK:(kc % 2 + 1) * TOK],
            apply_out=ln12_apply, smalls=SMALLS_C, tmpa=(TMPA_C, TMPA2_C),
            pre_kc=lambda kc: (vv, M_res1[kc]))

        L_wd = {}
        wd_free = {}
        V_d = {}
        h_free = {}
        prev_vh = [None]
        wd2 = wd_d
        ngrp = HC // HGRP

        def wd_load(g_, mc):
            key = (g_, mc)
            if key in L_wd:
                return
            if not L_wd:
                pl.wait("g", tv, T_otr_last[0])
            par = mc % 2
            if par in wd_free:
                pl.wait("g", *wd_free[par])
                del wd_free[par]
            src = wd2[mc][:, g_ * HGRP:(g_ + 1) * HGRP, :]
            dst = WDBUF[:, par * HGRP * P:(par + 1) * HGRP * P].rearrange(
                "p (i m) -> p i m", m=P)
            L_wd[key] = (wd_c[par], pl.emit(
                "g", lambda g, a=dst, b=src: g.dma_start(out=a, in_=b), wd_c[par]))

        for g_ in range(ngrp):
            V_h = {}
            for i_ in range(HGRP):
                hc = g_ * HGRP + i_
                sg = 3 * H + KC + 2 * hc
                su = sg + 1
                ring_ensure(sg + NWB - 1)
                pl.wait("t", *ring_loaded[sg])
                wb_g = ring_buf3(sg)
                wb_u = ring_buf3(su)
                wait_bank("t", "S0")
                for kc in range(KC):
                    if hc == 0:
                        pl.wait("t", vv, M_xhat2[kc])
                    st, sp = kc == 0, kc == KC - 1
                    t_mm(pS0[:, 0:TH], wb_g[:, kc, :], xh_chunk(kc)[:, 0:TH], st, sp)
                    T_g = t_mm(pS0[:, 512:512 + TH], wb_g[:, kc, :],
                               xh_chunk(kc)[:, TH:TOK], st, sp, inc=(kc == KC - 1))
                ring_consumed[sg] = (tv, T_g)
                ring_ensure(su + NWB - 1)
                pl.wait("t", *ring_loaded[su])
                wait_bank("t", "S1")
                for kc in range(KC):
                    st, sp = kc == 0, kc == KC - 1
                    t_mm(pS1[:, 0:TH], wb_u[:, kc, :], xh_chunk(kc)[:, 0:TH], st, sp)
                    T_u = t_mm(pS1[:, 512:512 + TH], wb_u[:, kc, :],
                               xh_chunk(kc)[:, TH:TOK], st, sp, inc=(kc == KC - 1))
                ring_consumed[su] = (tv, T_u)

                pl.wait("s", tv, T_g)
                if prev_vh[0] is not None:
                    pl.wait("s", vv, prev_vh[0])
                c4 = pl.emit("s", lambda s: s.activation(SIL[:, 0:TH], pS0[:, 0:TH],
                                                         AF.Silu), sv)
                pl.wait("s", sv, c4)
                S_sil = pl.emit("s", lambda s: s.activation(
                    SIL[:, TH:TOK], pS0[:, 512:512 + TH], AF.Silu), sv)
                bank_free["S0"] = (sv, S_sil)

                hreg = HREG[:, i_ * TOK:(i_ + 1) * TOK]
                if i_ in h_free:
                    pl.wait("v", *h_free[i_])
                pl.wait("v", sv, S_sil)
                pl.wait("v", tv, T_u)
                pl.emit("v", lambda v, d=hreg: v.tensor_mul(
                    d[:, 0:TH], SIL[:, 0:TH], pS1[:, 0:TH]), vv)
                V_h[i_] = pl.emit("v", lambda v, d=hreg: v.tensor_mul(
                    d[:, TH:TOK], SIL[:, TH:TOK], pS1[:, 512:512 + TH]), vv)
                bank_free["S1"] = (vv, V_h[i_])
                prev_vh[0] = V_h[i_]

            dpair = {0: ("D0", "D1"), 1: ("D2", "D3")}
            wd_load(g_, 0)
            wd_load(g_, 1)
            for mc in range(KC):
                d0n, d1n = dpair[mc % 2]
                d0, d1 = banks[d0n], banks[d1n]
                wait_bank("t", d0n)
                wait_bank("t", d1n)
                pl.wait("t", *L_wd[(g_, mc)])
                wb_d = WDBUF[:, (mc % 2) * HGRP * P:(mc % 2 + 1) * HGRP * P].rearrange(
                    "p (i m) -> p i m", m=P)
                for i_ in range(HGRP):
                    if mc == 0:
                        pl.wait("t", vv, V_h[i_])
                    hreg = HREG[:, i_ * TOK:(i_ + 1) * TOK]
                    st, sp = i_ == 0, i_ == HGRP - 1
                    t_mm(d0[:, 0:TH], wb_d[:, i_, :], hreg[:, 0:TH], st, sp)
                    T_d = t_mm(d1[:, 0:TH], wb_d[:, i_, :], hreg[:, TH:TOK], st, sp,
                               inc=(i_ == HGRP - 1))
                wd_free[mc % 2] = (tv, T_d)
                if mc + 2 < KC:
                    wd_load(g_, mc + 2)
                elif g_ + 1 < ngrp:
                    wd_load(g_ + 1, mc + 2 - KC)
                pl.wait("v", tv, T_d)
                if g_ == 0:
                    pl.emit("v", lambda v, b=d0, d=rs_chunk(mc):
                            v.tensor_copy(d[:, 0:TH], b[:, 0:TH]), vv)
                    V_dd = pl.emit("v", lambda v, b=d1, d=rs_chunk(mc):
                                   v.tensor_copy(d[:, TH:TOK], b[:, 0:TH]), vv)
                else:
                    pl.wait("v", vv, V_d[mc])
                    pl.emit("v", lambda v, b=d0, d=rs_chunk(mc):
                            v.tensor_add(d[:, 0:TH], d[:, 0:TH], b[:, 0:TH]), vv)
                    V_dd = pl.emit("v", lambda v, b=d1, d=rs_chunk(mc):
                                   v.tensor_add(d[:, TH:TOK], d[:, TH:TOK], b[:, 0:TH]),
                                   vv)
                V_d[mc] = V_dd
                bank_free[d0n] = (vv, V_dd)
                bank_free[d1n] = (vv, V_dd)
                if mc == KC - 1:
                    for i_ in range(HGRP):
                        h_free[i_] = (tv, T_d)

        out3 = out_d.rearrange("(kc p) t -> p kc t", p=P)

        def out_chunk_cb(kc, milestone):
            pl.wait("y", vv, milestone)
            pl.emit("y", lambda s, k=kc: s.dma_start(
                out=out3[:, k, :], in_=XS3[:, k, :]), ld0)

        V_res2, _ = ln(
            rs_chunk,
            cast_slot=lambda kc: xh_chunk(kc % 2),
            sq_slot=lambda kc: HREG[:, (kc % 2) * TOK:(kc % 2 + 1) * TOK],
            apply_out=ln34_apply, smalls=SMALLS_C, tmpa=(TMPA_C, TMPA2_C),
            pre_kc=lambda kc: (vv, V_d[kc]), pre_wait_t=(tv, tv.val),
            pre_sq=(vv, V_d[KC - 1]),
            cast_dbuf=True, kc_done_cb=out_chunk_cb)

        pl.wait("y", ld0, ld0.val)

        @block.tensor
        def _(t):
            for fn in pl.q["t"]:
                fn(t)

        @block.vector
        def _(v):
            for fn in pl.q["v"]:
                fn(v)

        @block.scalar
        def _(s):
            for fn in pl.q["s"]:
                fn(s)

        @block.sync
        def _(s):
            for fn in pl.q["y"]:
                fn(s)

        @block.gpsimd
        def _(g):
            for fn in pl.q["g"]:
                fn(g)

    return nc



def owned_tokens(c, nslot):
    return np.concatenate(
        [np.arange(P) + P * (NCORES * s + c) for s in range(nslot)])


def prep_inputs(x, w_aq, w_ak, w_av, w_ao, w_gate, w_up, w_down, cfg):
    H, KC, NSLOT, HC = cfg["H"], cfg["KC"], cfg["NSLOT"], cfg["HC"]
    DM = KC * P

    def pre_qkv(w):
        return np.ascontiguousarray(
            np.asarray(w, dtype=np.float32).reshape(H, KC, P, P).transpose(0, 2, 1, 3))

    wq = pre_qkv(w_aq)
    wk = pre_qkv(w_ak)
    wv = pre_qkv(w_av)
    wo = np.ascontiguousarray(
        np.asarray(w_ao, dtype=np.float32).reshape(H, P, KC, P).transpose(2, 1, 0, 3))
    wg = np.ascontiguousarray(
        np.asarray(w_gate, dtype=np.float32).reshape(HC, P, KC, P).transpose(0, 3, 2, 1))
    wu = np.ascontiguousarray(
        np.asarray(w_up, dtype=np.float32).reshape(HC, P, KC, P).transpose(0, 3, 2, 1))
    wd = np.ascontiguousarray(
        np.asarray(w_down, dtype=np.float32).reshape(KC, P, HC, P).transpose(0, 3, 2, 1))

    ident = np.eye(P, dtype=ml_dtypes.bfloat16)
    perm = np.zeros((P, P), dtype=ml_dtypes.bfloat16)
    perm[(np.arange(P) + P // 2) % P, np.arange(P)] = 1.0
    xf = np.asarray(x, dtype=np.float32).reshape(-1, DM)
    in_maps = []
    for c in range(NCORES):
        idx = owned_tokens(c, NSLOT)
        xc = np.ascontiguousarray(xf[idx, :].T)
        pos = idx.astype(np.float64)
        dims = np.arange(64, dtype=np.float64)
        freqs = 10000.0 ** (-dims / 64.0)
        rad = freqs[:, None] * pos[None, :]
        cos2 = np.concatenate([np.cos(rad), np.cos(rad)], axis=0)
        sin2 = np.concatenate([-np.sin(rad), np.sin(rad)], axis=0)
        kk = np.arange(P)[:, None, None]
        kt = np.arange(NCORES)[None, :, None]
        qi = np.arange(P)[None, None, :]
        m01 = (128 * kt + kk <= 128 * c + qi).astype(np.float32).reshape(
            P, NCORES * P)
        in_maps.append({
            "x": xc,
            "wq": wq, "wk": wk, "wv": wv, "wo": wo,
            "wg": wg, "wu": wu, "wd": wd,
            "cos2": cos2.astype(ml_dtypes.bfloat16),
            "sin2": sin2.astype(ml_dtypes.bfloat16),
            "mask01": m01.astype(ml_dtypes.bfloat16),
            "ident": ident,
            "perm": perm,
        })
    return in_maps


_NC_CACHE = {}


def kernel(L0_attn_x, L0_attn_w_aq, L0_attn_w_ak, L0_attn_w_av, L0_attn_w_ao,
           L0_ffn_w_gate, L0_ffn_w_up, L0_ffn_w_down):
    cfg = FULL_CFG
    in_maps = prep_inputs(L0_attn_x, L0_attn_w_aq, L0_attn_w_ak, L0_attn_w_av,
                          L0_attn_w_ao, L0_ffn_w_gate, L0_ffn_w_up,
                          L0_ffn_w_down, cfg)
    if "full" not in _NC_CACHE:
        _NC_CACHE["full"] = build_nc(cfg)
    nc = _NC_CACHE["full"]
    res = run_bass_kernel_spmd(nc, in_maps, core_ids=list(range(NCORES)))
    T = cfg["NSLOT"] * P * NCORES
    DM = cfg["KC"] * P
    out = np.empty((1, T, DM), dtype=np.float32)
    for c in range(NCORES):
        idx = owned_tokens(c, cfg["NSLOT"])
        out[0, idx, :] = res.results[c]["out"].T
    return out


# version 24
# speedup vs baseline: 1.0212x; 1.0117x over previous
import numpy as np
import ml_dtypes

import concourse.bass as bass
import concourse.mybir as mybir
from concourse.bass_utils import run_bass_kernel_spmd

F32 = mybir.dt.float32
BF16 = mybir.dt.bfloat16
AF = mybir.ActivationFunctionType
AX = mybir.AxisListType
ALU = mybir.AluOpType

NCORES = 8
P = 128

FULL_CFG = dict(H=16, KC=16, NSLOT=8, HC=64, HGRP=8)


class Ctr:
    def __init__(self, sem, step):
        self.sem = sem
        self.step = step
        self.val = 0

    def plan(self):
        self.val += self.step
        return self.val


class Plan:

    def __init__(self):
        self.q = {k: [] for k in ("t", "v", "s", "y", "g")}

    def emit(self, eng, fn, ctr=None):
        val = ctr.plan() if ctr is not None else None

        def run(e, fn=fn, ctr=ctr):
            inst = fn(e)
            if ctr is not None:
                inst.then_inc(ctr.sem, ctr.step)

        self.q[eng].append(run)
        return val

    def wait(self, eng, ctr, val):
        if val is None or val <= 0:
            return
        self.q[eng].append(lambda e, s=ctr.sem, v=val: e.wait_ge(s, v))


def build_nc(cfg):
    H, KC, NSLOT, HC, HGRP = (cfg["H"], cfg["KC"], cfg["NSLOT"], cfg["HC"],
                              cfg["HGRP"])
    TOK = NSLOT * P
    DM = KC * P
    TH = TOK // 2
    NW = NSLOT // 2

    nc = bass.Bass("TRN2", target_bir_lowering=False)

    x_d = nc.dram_tensor("x", [DM, TOK], F32, kind="ExternalInput").ap()
    wq_d = nc.dram_tensor("wq", [H, P, KC, P], F32, kind="ExternalInput").ap()
    wk_d = nc.dram_tensor("wk", [H, P, KC, P], F32, kind="ExternalInput").ap()
    wv_d = nc.dram_tensor("wv", [H, P, KC, P], F32, kind="ExternalInput").ap()
    wo_d = nc.dram_tensor("wo", [KC, P, H, P], F32, kind="ExternalInput").ap()
    wg_d = nc.dram_tensor("wg", [HC, P, KC, P], F32, kind="ExternalInput").ap()
    wu_d = nc.dram_tensor("wu", [HC, P, KC, P], F32, kind="ExternalInput").ap()
    wd_d = nc.dram_tensor("wd", [KC, P, HC, P], F32, kind="ExternalInput").ap()
    cos_d = nc.dram_tensor("cos2", [P, TOK], BF16, kind="ExternalInput").ap()
    sin_d = nc.dram_tensor("sin2", [P, TOK], BF16, kind="ExternalInput").ap()
    msk_d = nc.dram_tensor("mask01", [P, NCORES * P], BF16, kind="ExternalInput").ap()
    idn_d = nc.dram_tensor("ident", [P, P], BF16, kind="ExternalInput").ap()
    prm_d = nc.dram_tensor("perm", [P, P], BF16, kind="ExternalInput").ap()
    out_d = nc.dram_tensor("out", [DM, TOK], F32, kind="ExternalOutput").ap()

    kloc = nc.dram_tensor("kloc", [H * NSLOT * P, P], BF16)
    vloc = nc.dram_tensor("vloc", [H * NSLOT * P, P], BF16)
    qloc = nc.dram_tensor("qloc", [H * P, TOK], BF16)
    HH = H // 2
    kgathA = nc.dram_tensor("kgathA", [NCORES * HH * NSLOT * P, P], BF16)
    kgathB = nc.dram_tensor("kgathB", [NCORES * HH * NSLOT * P, P], BF16)
    vgathA = nc.dram_tensor("vgathA", [NCORES * HH * NSLOT * P, P], BF16)
    vgathB = nc.dram_tensor("vgathB", [NCORES * HH * NSLOT * P, P], BF16)

    kloc4 = kloc.ap().rearrange("(h s d) t -> h s d t", h=H, s=NSLOT)
    vloc4 = vloc.ap().rearrange("(h s t) d -> h s t d", h=H, s=NSLOT)
    qloc3 = qloc.ap().rearrange("(h d) t -> h d t", h=H)
    kg5A = kgathA.ap().rearrange("(r h s d) t -> r h s d t", r=NCORES, h=HH, s=NSLOT)
    kg5B = kgathB.ap().rearrange("(r h s d) t -> r h s d t", r=NCORES, h=HH, s=NSLOT)
    vg5A = vgathA.ap().rearrange("(r h s t) d -> r h s t d", r=NCORES, h=HH, s=NSLOT)
    vg5B = vgathB.ap().rearrange("(r h s t) d -> r h s t d", r=NCORES, h=HH, s=NSLOT)

    NWB = 3
    OVLW = max(
        3 * TOK + HGRP * TOK // 2 + 512 + HGRP * P,
        7 * TOK + 512,
        4 * TOK + TOK + 1024 + 780 + 1536,
    )

    from contextlib import ExitStack
    with ExitStack() as _ctx:
        ec = _ctx.enter_context
        XS = ec(nc.sbuf_tensor("XS", [P, KC * TOK], F32))
        XH = ec(nc.sbuf_tensor("XH", [P, KC * TOK], BF16))
        RS = ec(nc.sbuf_tensor("RS", [P, KC * TOK], F32))
        WRING = ec(nc.sbuf_tensor("WRING", [P, NWB * KC * P], BF16))
        OVL = ec(nc.sbuf_tensor("OVL", [P, OVLW], F32))
        IDB = ec(nc.sbuf_tensor("IDB", [P, P], BF16))
        PRM = ec(nc.sbuf_tensor("PRM", [P, P], BF16))
        ONESC = ec(nc.sbuf_tensor("ONESC", [P, 1], BF16))
        ONESR = ec(nc.sbuf_tensor("ONESR", [1, P], BF16))
        pS0 = ec(nc.psum_tensor("pS0", [P, 1024], F32))
        pS1 = ec(nc.psum_tensor("pS1", [P, 1024], F32))
        pD0 = ec(nc.psum_tensor("pD0", [P, 512], F32))
        pD1 = ec(nc.psum_tensor("pD1", [P, 512], F32))
        pD2 = ec(nc.psum_tensor("pD2", [P, 512], F32))
        pD3 = ec(nc.psum_tensor("pD3", [P, 512], F32))
        ld0_s = ec(nc.semaphore("ld0"))
        qstg_s = ec(nc.semaphore("qstg"))
        kstg_s = ec(nc.semaphore("kstg"))
        vstg_s = ec(nc.semaphore("vstg"))
        kstg2_s = ec(nc.semaphore("kstg2"))
        vstg2_s = ec(nc.semaphore("vstg2"))
        qb_ss = [ec(nc.semaphore(f"qb{i}")) for i in range(2)]
        xc_ss = [ec(nc.semaphore(f"xc{i}")) for i in range(KC)]
        kb_ss = [ec(nc.semaphore(f"kb{i}")) for i in range(4)]
        vb_ss = [ec(nc.semaphore(f"vb{i}")) for i in range(3)]
        ring_ss = [ec(nc.semaphore(f"ring{i}")) for i in range(4)]
        wd_ss = [ec(nc.semaphore(f"wdb{i}")) for i in range(2)]
        tv_s = ec(nc.semaphore("tv"))
        vv_s = ec(nc.semaphore("vv"))
        sv_s = ec(nc.semaphore("sv"))
        cc_s = ec(nc.semaphore("cc"))
        block = ec(nc.Block())
        ld0 = Ctr(ld0_s, 16)
        qstg = Ctr(qstg_s, 16)
        kstg = Ctr(kstg_s, 16)
        vstg = Ctr(vstg_s, 16)
        kstg2 = Ctr(kstg2_s, 16)
        vstg2 = Ctr(vstg2_s, 16)
        qb_c = [Ctr(s, 16) for s in qb_ss]
        xc_c = [Ctr(s, 16) for s in xc_ss]
        kb_c = [Ctr(s, 16) for s in kb_ss]
        vb_c = [Ctr(s, 16) for s in vb_ss]
        ring_c = [Ctr(s, 16) for s in ring_ss]
        wd_c = [Ctr(s, 16) for s in wd_ss]
        tv = Ctr(tv_s, 1)
        vv = Ctr(vv_s, 1)
        sv = Ctr(sv_s, 1)
        cc = Ctr(cc_s, 1)
        pl = Plan()

        ovl = OVL.ap()

        def carve_bf(c0, n):
            assert n % 2 == 0
            return ovl[:, c0:c0 + n // 2].bitcast(BF16)

        def carve_f(c0, n):
            return ovl[:, c0:c0 + n]

        COS2 = carve_bf(0, TOK)
        SIN2 = carve_bf(TOK // 2, TOK)
        QKB = carve_bf(TOK, TOK)
        RTMP = carve_bf(TOK + TOK // 2, TOK)
        RU = carve_bf(2 * TOK, TOK)
        QSTG = carve_bf(2 * TOK + TOK // 2, TOK)
        KSTG = carve_bf(3 * TOK, TOK)
        VTSTG = carve_bf(3 * TOK + TOK // 2, TOK)
        SQA_A = carve_bf(4 * TOK, TOK)
        SQB_A = VTSTG
        SMALLS_A = carve_f(4 * TOK + TOK // 2, 2 * TOK + TOK // 2)
        TMPA_A = carve_f(7 * TOK, 512)
        TMPA2_A = carve_f(4 * TOK, min(512, TOK // 2))

        _b = 4 * TOK
        QTB = carve_bf(_b, 2 * TOK)
        _b += TOK
        KB = carve_bf(_b, 2048)
        _b += 1024
        VB = carve_bf(_b, 1560)
        _b += 780
        PB = carve_bf(_b, 3072)
        _b += 1536
        CFGB_END = _b
        MASK = carve_bf(0, NCORES * P)
        OBS = carve_bf(NCORES * P // 2, NW * P)
        LINV = carve_f(NCORES * P // 2 + NW * P // 2, max(NW, 4))
        assert NCORES * P // 2 + NW * P // 2 + max(NW, 4) <= 4 * TOK
        assert CFGB_END <= OVLW

        SIL = carve_bf(0, TOK)
        HREG = carve_bf(TOK // 2, HGRP * TOK)
        SMALLS_C = carve_f(TOK // 2 + HGRP * TOK // 2, 2 * TOK + TOK // 2)
        TMPA_C = carve_f(3 * TOK + HGRP * TOK // 2, 512)
        TMPA2_C = carve_f(4 * TOK, min(512, TOK // 2))
        WDBUF = carve_bf(3 * TOK + HGRP * TOK // 2 + 512, 2 * HGRP * P)
        assert 3 * TOK + HGRP * TOK // 2 + 512 + HGRP * P <= OVLW, OVLW

        banks = {"S0": pS0, "S1": pS1,
                 "D0": pD0, "D1": pD1, "D2": pD2, "D3": pD3}
        tpsum = pD2.ap().bitcast(BF16)
        tpsum_w = pD0.ap().bitcast(BF16)
        bank_free = {k: None for k in banks}

        def t_mm(out, lhsT, rhs, start, stop, inc=False):
            return pl.emit(
                "t", lambda t: t.matmul(out, lhsT, rhs, start=start, stop=stop),
                tv if inc else None)

        def wait_bank(eng, name):
            bf = bank_free[name]
            if bf is not None:
                pl.wait(eng, bf[0], bf[1])

        ring_srcs = (
            [(wq_d, wk_d, wv_d)[pn][h] for h in range(H) for pn in range(3)]
            + [wo_d[mc] for mc in range(KC)]
            + [(wg_d, wu_d)[pn][hc] for hc in range(HC) for pn in range(2)]
        )
        NR = len(ring_srcs)
        ring_consumed = {}
        ring_loaded = {}
        ring_next = [0]

        def ring_buf3(i):
            return WRING.ap()[:, (i % NWB) * KC * P:(i % NWB + 1) * KC * P].rearrange(
                "p (kc d) -> p kc d", d=P)

        def ring_ensure(upto):
            while ring_next[0] <= min(upto, NR - 1):
                j = ring_next[0]
                if j >= NWB:
                    c, vval = ring_consumed[j - NWB]
                    pl.wait("g", c, vval)
                rc = ring_c[j % NWB]
                ring_loaded[j] = (rc, pl.emit(
                    "g", lambda g, s3=ring_srcs[j], d3=ring_buf3(j):
                    g.dma_start(out=d3, in_=s3), rc))
                ring_next[0] += 1

        ring_ensure(NWB - 1)

        x3s = x_d.rearrange("(kc p) t -> p kc t", p=P)
        XS3 = XS.ap().rearrange("p (kc t) -> p kc t", t=TOK)
        L_cos = pl.emit("s", lambda s: s.dma_start(out=COS2, in_=cos_d), ld0)
        L_sin = pl.emit("s", lambda s: s.dma_start(out=SIN2, in_=sin_d), ld0)
        L_idn = pl.emit("s", lambda s: s.dma_start(out=IDB[:], in_=idn_d), ld0)
        L_prm = pl.emit("s", lambda s: s.dma_start(out=PRM[:], in_=prm_d), ld0)
        L_ph0 = ld0.val
        L_xc = {}
        for kc in range(KC):
            xq = "y" if (kc < 4 or kc % 2 == 1) else "s"
            L_xc[kc] = pl.emit(xq, lambda s, k=kc: s.dma_start(
                out=XS3[:, k, :], in_=x3s[:, k, :]), xc_c[kc])

        pl.emit("v", lambda v: v.memset(ONESC[:], 1.0), vv)
        V_ones = pl.emit("v", lambda v: v.memset(ONESR[:], 1.0), vv)

        def ln(src_chunk, cast_slot, sq_slot, apply_out, smalls, tmpa,
               pre_wait_v=None, pre_wait_t=None, cast_dbuf=False,
               kc_done_cb=None, pre_kc=None, pre_sq=None):
            V_cast = {}
            S_sq = {}
            T_statk = {}
            for kc in range(KC):
                if kc == 0:
                    if pre_wait_v is not None:
                        pl.wait("v", *pre_wait_v)
                    pl.wait("v", ld0, ld0.val)
                if pre_kc is not None:
                    pl.wait("v", *pre_kc(kc))
                if cast_dbuf and kc >= 2:
                    pl.wait("v", tv, T_statk[kc - 2])
                V_cast[kc] = pl.emit(
                    "v", lambda v, s=src_chunk(kc), d=cast_slot(kc):
                    v.tensor_copy(d, s), vv)
                if kc == 0:
                    pl.wait("s", ld0, ld0.val)
                    if pre_wait_v is not None:
                        pl.wait("s", *pre_wait_v)
                    if pre_sq is not None:
                        pl.wait("s", *pre_sq)
                if pre_kc is not None:
                    pl.wait("s", *pre_kc(kc))
                if kc >= 2:
                    pl.wait("s", tv, T_statk[kc - 2])
                S_sq[kc] = pl.emit(
                    "s", lambda s, a=src_chunk(kc), d=sq_slot(kc):
                    s.activation(d, a, AF.Square), sv)
                if kc == 0:
                    if pre_wait_t is not None:
                        pl.wait("t", *pre_wait_t)
                    for bname in ("D0", "D1", "D2", "D3"):
                        wait_bank("t", bname)
                    pl.wait("t", vv, V_ones)
                pl.wait("t", vv, V_cast[kc])
                pl.wait("t", sv, S_sq[kc])
                st, sp = kc == 0, kc == KC - 1
                t_mm(pD0[0:1, 0:TH], ONESC[:], cast_slot(kc)[:, 0:TH], st, sp)
                t_mm(pD1[0:1, 0:TH], ONESC[:], cast_slot(kc)[:, TH:TOK], st, sp)
                t_mm(pD2[0:1, 0:TH], ONESC[:], sq_slot(kc)[:, 0:TH], st, sp)
                T_statk[kc] = t_mm(pD3[0:1, 0:TH], ONESC[:],
                                   sq_slot(kc)[:, TH:TOK], st, sp, inc=True)
            T_stats = T_statk[KC - 1]

            STTA = smalls[0:1, 0:TOK]
            STTB = smalls[0:1, TOK:2 * TOK]
            meanb = smalls[0:1, 2 * TOK:2 * TOK + TOK // 2].bitcast(BF16)
            rstdb = meanb

            def chain_v(fn):
                pl.wait("v", vv, vv.val)
                return pl.emit("v", fn, vv)

            pl.wait("v", tv, T_stats)
            pl.emit("v", lambda v: v.tensor_copy(STTA[:, 0:TH], pD0[0:1, 0:TH]), vv)
            chain_v(lambda v: v.tensor_copy(STTA[:, TH:TOK], pD1[0:1, 0:TH]))
            chain_v(lambda v: v.tensor_copy(STTB[:, 0:TH], pD2[0:1, 0:TH]))
            V_evac = chain_v(lambda v: v.tensor_copy(STTB[:, TH:TOK],
                                                     pD3[0:1, 0:TH]))
            inv = 1.0 / DM
            chain_v(lambda v: v.tensor_scalar_mul(STTA, STTA, inv))
            V_mb = chain_v(lambda v: v.tensor_copy(meanb, STTA))
            chain_v(lambda v: v.tensor_mul(STTA, STTA, STTA))
            chain_v(lambda v: v.tensor_scalar_mul(STTB, STTB, inv))
            chain_v(lambda v: v.tensor_sub(STTB, STTB, STTA))
            V_var = chain_v(lambda v: v.tensor_scalar_add(STTB, STTB, 1e-5))
            pl.wait("s", vv, V_var)
            c = pl.emit("s", lambda s: s.activation(STTA, STTB, AF.Ln), sv)
            pl.wait("s", sv, c)
            S_rstd = pl.emit("s", lambda s: s.activation(STTB, STTA, AF.Exp,
                                                         scale=-0.5), sv)

            pl.wait("t", vv, V_evac)
            pl.wait("t", vv, V_mb)
            t_mm(pD0[:, 0:TH], ONESR[:], meanb[0:1, 0:TH], True, True)
            T_bcm = t_mm(pD1[:, 0:TH], ONESR[:], meanb[0:1, TH:TOK],
                         True, True, inc=True)
            pl.wait("v", tv, T_bcm)
            pl.wait("v", sv, S_rstd)
            V_rb = chain_v(lambda v: v.tensor_copy(rstdb, STTB))
            pl.wait("t", vv, V_rb)
            t_mm(pD2[:, 0:TH], ONESR[:], rstdb[0:1, 0:TH], True, True)
            T_bc = t_mm(pD3[:, 0:TH], ONESR[:], rstdb[0:1, TH:TOK],
                        True, True, inc=True)

            last_apply = None
            kc_ms = {}
            tmpa_last = {}
            pair_i = 0
            for kc in range(KC):
                for half in range(2):
                    mb = (pD0 if half == 0 else pD1)[:, 0:TH]
                    rb = (pD2 if half == 0 else pD3)[:, 0:TH]
                    if kc == 0 and half == 0:
                        pl.wait("v", tv, T_bc)
                    sl = slice(half * TH, (half + 1) * TH)
                    ta = tmpa[pair_i % 2]
                    if pair_i >= 2:
                        pl.wait("v", vv, tmpa_last[pair_i % 2])
                    pl.emit("v", lambda v, s=src_chunk(kc), m=mb, d=ta, q=sl:
                            v.tensor_sub(d[:, 0:TH], s[:, q], m), vv)
                    last_apply = apply_out(kc, half, ta, rb)
                    tmpa_last[pair_i % 2] = last_apply
                    pair_i += 1
                    if half == 1:
                        kc_ms[kc] = last_apply
                        if kc_done_cb is not None:
                            kc_done_cb(kc, last_apply)
            for bname in ("D0", "D1", "D2", "D3"):
                bank_free[bname] = (vv, last_apply)
            return last_apply, kc_ms

        def xs_chunk(kc):
            return XS.ap()[:, kc * TOK:(kc + 1) * TOK]

        def xh_chunk(kc):
            return XH.ap()[:, kc * TOK:(kc + 1) * TOK]

        def rs_chunk(kc):
            return RS.ap()[:, kc * TOK:(kc + 1) * TOK]

        def ln12_apply(kc, half, tmpa, rb):
            sl = slice(half * TH, (half + 1) * TH)
            pl.wait("v", vv, vv.val)
            return pl.emit("v", lambda v, d=xh_chunk(kc), a=tmpa, r=rb, q=sl:
                           v.tensor_mul(d[:, q], a[:, 0:TH], r), vv)

        def ln34_apply(kc, half, tmpa, rb):
            sl = slice(half * TH, (half + 1) * TH)
            pl.wait("v", vv, vv.val)
            pl.emit("v", lambda v, a=tmpa, r=rb:
                    v.tensor_mul(a[:, 0:TH], a[:, 0:TH], r), vv)
            pl.wait("v", vv, vv.val)
            return pl.emit("v", lambda v, d=xs_chunk(kc), a=tmpa, q=sl:
                           v.tensor_add(d[:, q], d[:, q], a[:, 0:TH]), vv)

        V_xhat1, M_xhat1 = ln(
            xs_chunk, cast_slot=xh_chunk,
            sq_slot=lambda kc: SQA_A if kc % 2 == 0 else SQB_A,
            apply_out=ln12_apply, smalls=SMALLS_A, tmpa=(TMPA_A, TMPA2_A),
            pre_kc=lambda kc: (xc_c[kc], L_xc[kc]))

        pair = {0: "S0", 1: "S1"}
        L_kst = {}
        L_vst = {}
        L_qst = {}
        _qkv_state = {"qkb_free": None, "tpsum_free": None, "V_rope_last": 0,
                      "C_k1": None, "C_k2": None, "C_v1": None, "C_v2": None}

        qkv_tails = []

        def qkv_flush_tail():
            if qkv_tails:
                qkv_tails.pop(0)()

        def make_tail(h, pnum, i, bn, b, T_proj):
            def tail():
                self_ = _qkv_state
                pl.wait("s", tv, T_proj)
                for w_ in (self_["qkb_free"] or []):
                    pl.wait("s", *w_)
                c = pl.emit("s", lambda s, b=b:
                            s.activation(QKB[:, 0:TH], b[:, 0:TH], AF.Copy), sv)
                pl.wait("s", sv, c)
                S_evac = pl.emit("s", lambda s, b=b:
                                 s.activation(QKB[:, TH:TOK], b[:, 512:512 + TH],
                                              AF.Copy), sv)
                bank_free[bn] = (sv, S_evac)

                if pnum < 2:
                    stg = QSTG if pnum == 0 else KSTG
                    stgc = qstg if pnum == 0 else (kstg if h < HH else kstg2)
                    wait_bank("t", "D0")
                    wait_bank("t", "D1")
                    pl.wait("t", sv, S_evac)
                    t_mm(pD0[:, 0:TH], PRM[:], QKB[:, 0:TH], True, True)
                    T_swp = t_mm(pD1[:, 0:TH], PRM[:], QKB[:, TH:TOK], True,
                                 True, inc=True)
                    if pnum == 1:
                        pl.wait("v", kstg, kstg.val)
                        pl.wait("v", kstg2, kstg2.val)
                    else:
                        pl.wait("v", stgc, stgc.val)
                    pl.wait("v", sv, S_evac)
                    if h == 0 and pnum == 0:
                        pl.wait("v", ld0, L_ph0)
                    pl.wait("v", vv, vv.val)
                    c1 = pl.emit("v", lambda v: v.tensor_mul(RTMP, QKB, COS2), vv)
                    self_["qkb_free"] = [(vv, c1), (tv, T_swp)]
                    pl.wait("v", tv, T_swp)
                    pl.emit("v", lambda v: v.tensor_mul(
                        RU[:, 0:TH], pD0[:, 0:TH], SIN2[:, 0:TH]), vv)
                    c2 = pl.emit("v", lambda v: v.tensor_mul(
                        RU[:, TH:TOK], pD1[:, 0:TH], SIN2[:, TH:TOK]), vv)
                    bank_free["D0"] = (vv, c2)
                    bank_free["D1"] = (vv, c2)
                    pl.wait("v", vv, vv.val)
                    V_rope = pl.emit("v", lambda v, d=stg: v.tensor_add(
                        d, RTMP, RU), vv)
                    self_["V_rope_last"] = V_rope
                    pl.wait("y", vv, V_rope)
                    if pnum == 0:
                        L_qst[h] = pl.emit(
                            "y", lambda s, hh=h: s.dma_start(out=qloc3[hh],
                                                             in_=QSTG), qstg)
                    else:
                        ks = KSTG.rearrange("d (s t) -> d s t", s=NSLOT)
                        kd = kloc4[h].rearrange("s d t -> d s t")
                        L_kst[h] = pl.emit(
                            "y", lambda s, a=kd, b2=ks: s.dma_start(out=a, in_=b2),
                            stgc)
                        if h == HH and pnum == 1:
                            pl.wait("g", kstg, kstg.val)
                            self_["C_k1"] = pl.emit(
                                "g", lambda g: g.collective_compute(
                                    "AllGather", ALU.bypass,
                                    replica_groups=[list(range(NCORES))],
                                    ins=[kloc.ap()[0:HH * NSLOT * P, :].opt()],
                                    outs=[kgathA.ap().opt()]), cc)

                else:
                    vstgc = vstg if h < HH else vstg2
                    if self_["tpsum_free"] is not None:
                        pl.wait("t", *self_["tpsum_free"])
                    wait_bank("t", "D2")
                    pl.wait("t", sv, S_evac)
                    for sl_ in range(NSLOT):
                        T_vtr = pl.emit(
                            "t", lambda t, ss=sl_: t.transpose(
                                tpsum[:, ss * P:(ss + 1) * P],
                                QKB[:, ss * P:(ss + 1) * P], IDB[:]),
                            tv if sl_ == NSLOT - 1 else None)
                    self_["qkb_free"] = [(tv, T_vtr)]
                    pl.wait("s", vstg, vstg.val)
                    pl.wait("s", vstg2, vstg2.val)
                    pl.wait("s", tv, T_vtr)
                    S_vt = pl.emit("s", lambda s: s.activation(
                        VTSTG, tpsum[:, 0:NSLOT * P], AF.Copy), sv)
                    self_["tpsum_free"] = (sv, S_vt)
                    bank_free["D2"] = (sv, S_vt)
                    pl.wait("y", sv, S_vt)
                    vs = VTSTG.rearrange("t (s d) -> t s d", s=NSLOT)
                    vd = vloc4[h].rearrange("s t d -> t s d")
                    L_vst[h] = pl.emit(
                        "y", lambda s, a=vd, b2=vs: s.dma_start(out=a, in_=b2),
                        vstgc)
                    if h == HH:
                        pl.wait("g", vstg, vstg.val)
                        self_["C_v1"] = pl.emit(
                            "g", lambda g: g.collective_compute(
                                "AllGather", ALU.bypass,
                                replica_groups=[list(range(NCORES))],
                                ins=[vloc.ap()[0:HH * NSLOT * P, :].opt()],
                                outs=[vgathA.ap().opt()]), cc)

            return tail

        for h in range(H):
            for pnum in range(3):
                i = 3 * h + pnum
                ring_ensure(i + NWB - 1)
                bn = pair[i % 2]
                b = banks[bn]
                pl.wait("t", *ring_loaded[i])
                if i == 0:
                    pl.wait("t", ld0, L_ph0)
                wait_bank("t", bn)
                wb3 = ring_buf3(i)
                for kc in range(KC):
                    if i == 0:
                        pl.wait("t", vv, M_xhat1[kc])
                    st, sp = kc == 0, kc == KC - 1
                    t_mm(b[:, 0:TH], wb3[:, kc, :], xh_chunk(kc)[:, 0:TH], st, sp)
                    T_proj = t_mm(b[:, 512:512 + TH], wb3[:, kc, :],
                                  xh_chunk(kc)[:, TH:TOK], st, sp,
                                  inc=(kc == KC - 1))
                ring_consumed[i] = (tv, T_proj)
                qkv_flush_tail()
                qkv_tails.append(make_tail(h, pnum, i, bn, b, T_proj))
        qkv_flush_tail()
        C_k1 = _qkv_state["C_k1"]
        C_v1 = _qkv_state["C_v1"]
        C_k2 = None
        C_v2 = None
        T_sc_last = None
        V_rope_last = _qkv_state["V_rope_last"]


        pl.wait("s", vv, _qkv_state["V_rope_last"])
        L_msk = pl.emit("s", lambda s: s.dma_start(out=MASK, in_=msk_d), ld0)

        pl.wait("v", vv, vv.val)
        V_vbones = None
        for par in range(3):
            for i4 in range(4):
                col = par * 520 + i4 * 130 + 128
                V_vbones = pl.emit(
                    "v", lambda v, cl=col: v.memset(VB[:, cl:cl + 1], 1.0), vv)

        sbank_names = ["S0", "S1"]
        obank_names = ["D0", "D1", "D2", "D3"]
        kb_free = {}
        vb_free = {}
        pb_free = {}
        L_q = {}
        ii = 0
        jj = 0
        S_ot_last = 0
        S_ot_head = {}
        T_otr_last = [0]
        waves = [(0, NW), (NW, NSLOT)]

        for h in range(2):
            pl.wait("y", qstg, qstg.val)
            pl.wait("y", vv, V_rope_last)
            L_q[h] = pl.emit("y", lambda s, hh=h: s.dma_start(
                out=QTB[:, (hh % 2) * TOK:(hh % 2 + 1) * TOK], in_=qloc3[hh]),
                qb_c[h % 2])

        for h in range(H):
            if h == min(2, HH) and C_k2 is None:
                if T_sc_last is not None:
                    pl.wait("g", tv, T_sc_last)
                pl.wait("g", kstg2, kstg2.val)
                C_k2 = pl.emit("g", lambda g: g.collective_compute(
                    "AllGather", ALU.bypass,
                    replica_groups=[list(range(NCORES))],
                    ins=[kloc.ap()[HH * NSLOT * P:, :].opt()],
                    outs=[kgathB.ap().opt()]), cc)
                pl.wait("g", vstg2, vstg2.val)
                C_v2 = pl.emit("g", lambda g: g.collective_compute(
                    "AllGather", ALU.bypass,
                    replica_groups=[list(range(NCORES))],
                    ins=[vloc.ap()[HH * NSLOT * P:, :].opt()],
                    outs=[vgathB.ap().opt()]), cc)
            qv = QTB[:, (h % 2) * TOK:(h % 2 + 1) * TOK]
            T_sc_last = None
            for iw, (s_lo, s_hi) in enumerate(waves):
                jmax = 2 * (s_hi - 1) + 2
                T_stop = {}
                pending = []
                pairbuf = []
                np_ = [0]
                ns_ = [0]

                def flush_pair():
                    if not pairbuf:
                        return
                    pp = np_[0] % 3
                    sbank = ns_[0] % 2
                    np_[0] += 1
                    ns_[0] += 1
                    sbn = sbank_names[sbank]
                    sb_v = banks[sbn]
                    n_it = len(pairbuf)
                    pbv_pair = PB[:, pp * 1024:pp * 1024 + 1024]
                    while pending and any(e.get("ppar") == pp for e in pending):
                        emit_pv(pending.pop(0))
                    if pp in pb_free:
                        pl.wait("s", *pb_free[pp])
                    pl.wait("s", tv, pairbuf[-1]["T_sc"])
                    S_exp = pl.emit(
                        "s", lambda sc, d=pbv_pair[:, 0:n_it * 512],
                        b=sb_v[:, 0:n_it * 512]:
                        sc.activation(d, b, AF.Exp, scale=1.0 / 128.0), sv)
                    bank_free[sbn] = (sv, S_exp)
                    for it in pairbuf:
                        pv_ctr, pv_val = sv, S_exp
                        if it["diag"] is not None:
                            half = it["diag"]
                            pc = it["pcols"]
                            stride = pc[1] - pc[0]
                            if stride == P:
                                p_ap = pbv_pair.rearrange(
                                    "p (kt q) -> p kt q", q=P)[:, pc[0] // P:
                                                               pc[0] // P + 4, :]
                            else:
                                off = pc[0]
                                p_ap = pbv_pair.rearrange(
                                    "p (kt q) -> p kt q", q=stride)[:, :, off:off + P]
                            m_ap = MASK[:, half * 512:(half + 1) * 512].rearrange(
                                "p (kt q) -> p kt q", q=P)
                            pl.wait("v", ld0, L_msk)
                            pl.wait("v", sv, S_exp)
                            V_msk = pl.emit(
                                "v", lambda v, d=p_ap, m=m_ap:
                                v.tensor_mul(d, d, m), vv)
                            pv_ctr, pv_val = vv, V_msk
                        it["pv_ctr"], it["pv_val"] = pv_ctr, pv_val
                        it["ppar"] = pp
                        it["pbv"] = pbv_pair
                        pending.append(it)
                    pairbuf.clear()

                def emit_pv(p):
                    pl.wait("t", p["pv_ctr"], p["pv_val"])
                    pl.wait("t", vb_c[p["jpar"]], p["L_v"])
                    if p["first"]:
                        wait_bank("t", p["ob"])
                    for kt in range(4):
                        pc = p["pcols"][kt]
                        T_pv = t_mm(
                            p["ob_t"][:, 0:129],
                            p["pbv"][:, pc:pc + P],
                            VB[:, p["jpar"] * 520 + kt * 130:
                               p["jpar"] * 520 + kt * 130 + 129],
                            start=(p["first"] and kt == 0),
                            stop=(p["lastj"] and kt == 3),
                            inc=(kt == 3))
                    pbf = pb_free.get(p["ppar"])
                    pb_free[p["ppar"]] = (tv, T_pv)
                    vb_free[p["jpar"]] = (tv, T_pv)
                    if p["lastj"]:
                        T_stop[p["s_"]] = T_pv

                jj0 = jj

                kloads = {}
                vloads = {}

                def issue_k(jn):
                    if jn in kloads or jn >= jmax:
                        return
                    jsq = jj0 + jn
                    par = jsq % 4
                    r0n = (4 * jn) % NCORES
                    spn = jn // 2
                    kg5 = kg5A if h < HH else kg5B
                    hl = h if h < HH else h - HH
                    if iw == 0 and jn == 0:
                        pl.wait("y", cc, C_k1 if h < HH else C_k2)
                    if par in kb_free:
                        pl.wait("y", *kb_free[par])
                    ksrc = kg5[r0n:r0n + 4, hl, spn, :, :].rearrange("r d t -> d r t")
                    kdst = KB[:, par * 512:(par + 1) * 512].rearrange(
                        "d (r t) -> d r t", r=4)
                    kloads[jn] = (pl.emit("y", lambda s, a=kdst, b=ksrc:
                                          s.dma_start(out=a, in_=b), kb_c[par]), par)

                def issue_v(jn):
                    if jn in vloads or jn >= jmax:
                        return
                    jsq = jj0 + jn
                    par = jsq % 3
                    r0n = (4 * jn) % NCORES
                    spn = jn // 2
                    vg5 = vg5A if h < HH else vg5B
                    hl = h if h < HH else h - HH
                    if ii == 0 and jn == 0:
                        pl.wait("y", vv, V_vbones)
                    if iw == 0 and jn == 0:
                        pl.wait("y", cc, C_v1 if h < HH else C_v2)
                    if par in vb_free:
                        pl.wait("y", *vb_free[par])
                    vsrc = vg5[r0n:r0n + 4, hl, spn, :, :].rearrange("r t d -> t r d")
                    vdst = VB[:, par * 520:par * 520 + 520].rearrange(
                        "t (r e) -> t r e", e=130)[:, :, 0:128]
                    vloads[jn] = (pl.emit("y", lambda s, a=vdst, b=vsrc:
                                          s.dma_start(out=a, in_=b), vb_c[par]), par)

                for j in range(jmax):
                    jseq = jj
                    jj += 1
                    while pairbuf and pairbuf[0]["jseq"] <= jseq - 2:
                        flush_pair()
                    while pending and pending[0]["jseq"] <= jseq - 2:
                        emit_pv(pending.pop(0))
                    issue_k(j)
                    issue_v(j)
                    issue_v(j + 1)
                    issue_k(j + 1)
                    issue_k(j + 2)
                    L_k, kpar = kloads[j]
                    L_v, jpar = vloads[j]
                    slots = list(range(max(s_lo, j // 2), s_hi))
                    si_ = 0
                    while si_ < len(slots):
                        sbn = sbank_names[ns_[0] % 2]
                        sb_v = banks[sbn]
                        q_ = len(pairbuf)
                        two = (q_ == 0 and si_ + 1 < len(slots))
                        if q_ == 0:
                            wait_bank("t", sbn)
                        pl.wait("t", kb_c[kpar], L_k)
                        pl.wait("t", qb_c[h % 2], L_q[h])
                        if two:
                            s_a = slots[si_]
                            for kt in range(4):
                                T_sc = t_mm(
                                    sb_v[:, kt * 256:(kt + 1) * 256],
                                    KB[:, kpar * 512 + kt * P:
                                       kpar * 512 + (kt + 1) * P],
                                    qv[:, s_a * P:(s_a + 2) * P],
                                    True, True, inc=(kt == 3))
                            for idx in range(2):
                                s_ = slots[si_ + idx]
                                diag = None
                                if j == 2 * s_ or j == 2 * s_ + 1:
                                    diag = j - 2 * s_
                                pairbuf.append(dict(
                                    T_sc=T_sc, diag=diag, jpar=jpar, jseq=jseq,
                                    pcols=[kt * 256 + idx * P for kt in range(4)],
                                    ob=obank_names[s_ - s_lo],
                                    ob_t=banks[obank_names[s_ - s_lo]],
                                    first=(j == 0), lastj=(j == 2 * s_ + 1),
                                    L_v=L_v, s_=s_))
                            si_ += 2
                            ii += 2
                        else:
                            s_ = slots[si_]
                            for kt in range(4):
                                T_sc = t_mm(
                                    sb_v[:, q_ * 512 + kt * P:
                                         q_ * 512 + (kt + 1) * P],
                                    KB[:, kpar * 512 + kt * P:
                                       kpar * 512 + (kt + 1) * P],
                                    qv[:, s_ * P:(s_ + 1) * P],
                                    True, True, inc=(kt == 3))
                            diag = None
                            if j == 2 * s_ or j == 2 * s_ + 1:
                                diag = j - 2 * s_
                            pairbuf.append(dict(
                                T_sc=T_sc, diag=diag, jpar=jpar, jseq=jseq,
                                pcols=[q_ * 512 + kt * P for kt in range(4)],
                                ob=obank_names[s_ - s_lo],
                                ob_t=banks[obank_names[s_ - s_lo]],
                                first=(j == 0), lastj=(j == 2 * s_ + 1),
                                L_v=L_v, s_=s_))
                            si_ += 1
                            ii += 1
                        T_sc_last = T_sc
                        kb_free[kpar] = (tv, T_sc)
                        if len(pairbuf) == 2:
                            flush_pair()
                        while len(pending) > 3:
                            emit_pv(pending.pop(0))

                flush_pair()
                while pending:
                    emit_pv(pending.pop(0))

                V_on = {}
                for s_ in range(s_lo, s_hi):
                    si = s_ - s_lo
                    ob = obank_names[si]
                    ob_t = banks[ob]
                    li = LINV[:, si:si + 1]
                    pl.wait("v", tv, T_stop[s_])
                    c3 = pl.emit("v", lambda v, o=ob_t, l=li:
                                 v.reciprocal(l, o[:, 128:129]), vv)
                    pl.wait("v", vv, c3)
                    V_on[si] = pl.emit(
                        "v", lambda v, o=ob_t, l=li,
                        d=OBS[:, si * P:(si + 1) * P]:
                        v.tensor_scalar_mul(d, o[:, 0:128], l), vv)
                    bank_free[ob] = (vv, V_on[si])
                T_otr = None
                for si in range(s_hi - s_lo):
                    pl.wait("t", vv, V_on[0])
                    pl.wait("t", vv, V_on[si])
                    T_otr = pl.emit("t", lambda t, ss=si: t.transpose(
                        tpsum_w[:, ss * P:(ss + 1) * P],
                        OBS[:, ss * P:(ss + 1) * P], IDB[:]), tv)
                T_otr_last[0] = T_otr
                pl.wait("s", tv, T_otr)
                S_ot = pl.emit("s", lambda sc, hh=h, w=iw: sc.activation(
                    XH.ap()[:, hh * TOK + w * NW * P:hh * TOK + (w + 1) * NW * P],
                    tpsum_w[:, 0:NW * P], AF.Copy), sv)
                S_ot_last = S_ot
                S_ot_head[h] = S_ot
                bank_free["D0"] = (sv, S_ot)

            hh2 = h + 2
            if hh2 < H:
                pl.wait("y", tv, T_sc_last)
                L_q[hh2] = pl.emit("y", lambda s, q2=hh2: s.dma_start(
                    out=QTB[:, (q2 % 2) * TOK:(q2 % 2 + 1) * TOK],
                    in_=qloc3[q2]), qb_c[hh2 % 2])

        V_ro = {}
        T_oproj = {}
        for mc in range(KC):
            i = 3 * H + mc
            ring_ensure(i + NWB - 1)
            bn = pair[mc % 2]
            b = banks[bn]
            pl.wait("t", *ring_loaded[i])
            wait_bank("t", bn)
            wb3 = ring_buf3(i)
            for hh in range(H):
                if mc == 0:
                    pl.wait("t", sv, S_ot_head[hh])
                st, sp = hh == 0, hh == H - 1
                ot = XH.ap()[:, hh * TOK:(hh + 1) * TOK]
                t_mm(b[:, 0:TH], wb3[:, hh, :], ot[:, 0:TH], st, sp)
                T_o = t_mm(b[:, 512:512 + TH], wb3[:, hh, :], ot[:, TH:TOK], st, sp,
                           inc=(hh == H - 1))
            T_oproj[mc] = T_o
            ring_consumed[i] = (tv, T_o)
            pl.wait("v", tv, T_o)
            pl.emit("v", lambda v, b=b, d=rs_chunk(mc):
                    v.tensor_copy(d[:, 0:TH], b[:, 0:TH]), vv)
            V_ro[mc] = pl.emit("v", lambda v, b=b, d=rs_chunk(mc):
                               v.tensor_copy(d[:, TH:TOK], b[:, 512:512 + TH]), vv)
            bank_free[bn] = (vv, V_ro[mc])

        V_res1, M_res1 = ln(
            rs_chunk,
            cast_slot=lambda kc: xh_chunk(kc % 2),
            sq_slot=lambda kc: HREG[:, (kc % 2) * TOK:(kc % 2 + 1) * TOK],
            apply_out=ln34_apply, smalls=SMALLS_C, tmpa=(TMPA_C, TMPA2_C),
            pre_kc=lambda kc: (vv, V_ro[kc]), pre_wait_t=(tv, T_oproj[KC - 1]),
            cast_dbuf=True)

        V_xhat2, M_xhat2 = ln(
            xs_chunk, cast_slot=xh_chunk,
            sq_slot=lambda kc: HREG[:, (kc % 2) * TOK:(kc % 2 + 1) * TOK],
            apply_out=ln12_apply, smalls=SMALLS_C, tmpa=(TMPA_C, TMPA2_C),
            pre_kc=lambda kc: (vv, M_res1[kc]))

        L_wd = {}
        wd_free = {}
        V_d = {}
        h_free = {}
        prev_vh = [None]
        wd2 = wd_d
        ngrp = HC // HGRP

        def wd_load(g_, mc):
            key = (g_, mc)
            if key in L_wd:
                return
            if not L_wd:
                pl.wait("g", tv, T_otr_last[0])
            par = mc % 2
            if par in wd_free:
                pl.wait("g", *wd_free[par])
                del wd_free[par]
            src = wd2[mc][:, g_ * HGRP:(g_ + 1) * HGRP, :]
            dst = WDBUF[:, par * HGRP * P:(par + 1) * HGRP * P].rearrange(
                "p (i m) -> p i m", m=P)
            L_wd[key] = (wd_c[par], pl.emit(
                "g", lambda g, a=dst, b=src: g.dma_start(out=a, in_=b), wd_c[par]))

        for g_ in range(ngrp):
            V_h = {}
            for i_ in range(HGRP):
                hc = g_ * HGRP + i_
                sg = 3 * H + KC + 2 * hc
                su = sg + 1
                ring_ensure(sg + NWB - 1)
                pl.wait("t", *ring_loaded[sg])
                wb_g = ring_buf3(sg)
                wb_u = ring_buf3(su)
                wait_bank("t", "S0")
                for kc in range(KC):
                    if hc == 0:
                        pl.wait("t", vv, M_xhat2[kc])
                    st, sp = kc == 0, kc == KC - 1
                    t_mm(pS0[:, 0:TH], wb_g[:, kc, :], xh_chunk(kc)[:, 0:TH], st, sp)
                    T_g = t_mm(pS0[:, 512:512 + TH], wb_g[:, kc, :],
                               xh_chunk(kc)[:, TH:TOK], st, sp, inc=(kc == KC - 1))
                ring_consumed[sg] = (tv, T_g)
                ring_ensure(su + NWB - 1)
                pl.wait("t", *ring_loaded[su])
                wait_bank("t", "S1")
                for kc in range(KC):
                    st, sp = kc == 0, kc == KC - 1
                    t_mm(pS1[:, 0:TH], wb_u[:, kc, :], xh_chunk(kc)[:, 0:TH], st, sp)
                    T_u = t_mm(pS1[:, 512:512 + TH], wb_u[:, kc, :],
                               xh_chunk(kc)[:, TH:TOK], st, sp, inc=(kc == KC - 1))
                ring_consumed[su] = (tv, T_u)

                pl.wait("s", tv, T_g)
                if prev_vh[0] is not None:
                    pl.wait("s", vv, prev_vh[0])
                c4 = pl.emit("s", lambda s: s.activation(SIL[:, 0:TH], pS0[:, 0:TH],
                                                         AF.Silu), sv)
                pl.wait("s", sv, c4)
                S_sil = pl.emit("s", lambda s: s.activation(
                    SIL[:, TH:TOK], pS0[:, 512:512 + TH], AF.Silu), sv)
                bank_free["S0"] = (sv, S_sil)

                hreg = HREG[:, i_ * TOK:(i_ + 1) * TOK]
                if i_ in h_free:
                    pl.wait("v", *h_free[i_])
                pl.wait("v", sv, S_sil)
                pl.wait("v", tv, T_u)
                pl.emit("v", lambda v, d=hreg: v.tensor_mul(
                    d[:, 0:TH], SIL[:, 0:TH], pS1[:, 0:TH]), vv)
                V_h[i_] = pl.emit("v", lambda v, d=hreg: v.tensor_mul(
                    d[:, TH:TOK], SIL[:, TH:TOK], pS1[:, 512:512 + TH]), vv)
                bank_free["S1"] = (vv, V_h[i_])
                prev_vh[0] = V_h[i_]

            dpair = {0: ("D0", "D1"), 1: ("D2", "D3")}
            wd_load(g_, 0)
            wd_load(g_, 1)
            for mc in range(KC):
                d0n, d1n = dpair[mc % 2]
                d0, d1 = banks[d0n], banks[d1n]
                wait_bank("t", d0n)
                wait_bank("t", d1n)
                pl.wait("t", *L_wd[(g_, mc)])
                wb_d = WDBUF[:, (mc % 2) * HGRP * P:(mc % 2 + 1) * HGRP * P].rearrange(
                    "p (i m) -> p i m", m=P)
                for i_ in range(HGRP):
                    if mc == 0:
                        pl.wait("t", vv, V_h[i_])
                    hreg = HREG[:, i_ * TOK:(i_ + 1) * TOK]
                    st, sp = i_ == 0, i_ == HGRP - 1
                    t_mm(d0[:, 0:TH], wb_d[:, i_, :], hreg[:, 0:TH], st, sp)
                    T_d = t_mm(d1[:, 0:TH], wb_d[:, i_, :], hreg[:, TH:TOK], st, sp,
                               inc=(i_ == HGRP - 1))
                wd_free[mc % 2] = (tv, T_d)
                if mc + 2 < KC:
                    wd_load(g_, mc + 2)
                elif g_ + 1 < ngrp:
                    wd_load(g_ + 1, mc + 2 - KC)
                pl.wait("v", tv, T_d)
                if g_ == 0:
                    pl.emit("v", lambda v, b=d0, d=rs_chunk(mc):
                            v.tensor_copy(d[:, 0:TH], b[:, 0:TH]), vv)
                    V_dd = pl.emit("v", lambda v, b=d1, d=rs_chunk(mc):
                                   v.tensor_copy(d[:, TH:TOK], b[:, 0:TH]), vv)
                else:
                    pl.wait("v", vv, V_d[mc])
                    pl.emit("v", lambda v, b=d0, d=rs_chunk(mc):
                            v.tensor_add(d[:, 0:TH], d[:, 0:TH], b[:, 0:TH]), vv)
                    V_dd = pl.emit("v", lambda v, b=d1, d=rs_chunk(mc):
                                   v.tensor_add(d[:, TH:TOK], d[:, TH:TOK], b[:, 0:TH]),
                                   vv)
                V_d[mc] = V_dd
                bank_free[d0n] = (vv, V_dd)
                bank_free[d1n] = (vv, V_dd)
                if mc == KC - 1:
                    for i_ in range(HGRP):
                        h_free[i_] = (tv, T_d)

        out3 = out_d.rearrange("(kc p) t -> p kc t", p=P)

        def out_chunk_cb(kc, milestone):
            pl.wait("y", vv, milestone)
            pl.emit("y", lambda s, k=kc: s.dma_start(
                out=out3[:, k, :], in_=XS3[:, k, :]), ld0)

        V_res2, _ = ln(
            rs_chunk,
            cast_slot=lambda kc: xh_chunk(kc % 2),
            sq_slot=lambda kc: HREG[:, (kc % 2) * TOK:(kc % 2 + 1) * TOK],
            apply_out=ln34_apply, smalls=SMALLS_C, tmpa=(TMPA_C, TMPA2_C),
            pre_kc=lambda kc: (vv, V_d[kc]), pre_wait_t=(tv, tv.val),
            pre_sq=(vv, V_d[KC - 1]),
            cast_dbuf=True, kc_done_cb=out_chunk_cb)

        pl.wait("y", ld0, ld0.val)

        @block.tensor
        def _(t):
            for fn in pl.q["t"]:
                fn(t)

        @block.vector
        def _(v):
            for fn in pl.q["v"]:
                fn(v)

        @block.scalar
        def _(s):
            for fn in pl.q["s"]:
                fn(s)

        @block.sync
        def _(s):
            for fn in pl.q["y"]:
                fn(s)

        @block.gpsimd
        def _(g):
            for fn in pl.q["g"]:
                fn(g)

    return nc



def owned_tokens(c, nslot):
    return np.concatenate(
        [np.arange(P) + P * (NCORES * s + c) for s in range(nslot)])


def prep_inputs(x, w_aq, w_ak, w_av, w_ao, w_gate, w_up, w_down, cfg):
    H, KC, NSLOT, HC = cfg["H"], cfg["KC"], cfg["NSLOT"], cfg["HC"]
    DM = KC * P

    def pre_qkv(w):
        return np.ascontiguousarray(
            np.asarray(w, dtype=np.float32).reshape(H, KC, P, P).transpose(0, 2, 1, 3))

    wq = pre_qkv(w_aq)
    wk = pre_qkv(w_ak)
    wv = pre_qkv(w_av)
    wo = np.ascontiguousarray(
        np.asarray(w_ao, dtype=np.float32).reshape(H, P, KC, P).transpose(2, 1, 0, 3))
    wg = np.ascontiguousarray(
        np.asarray(w_gate, dtype=np.float32).reshape(HC, P, KC, P).transpose(0, 3, 2, 1))
    wu = np.ascontiguousarray(
        np.asarray(w_up, dtype=np.float32).reshape(HC, P, KC, P).transpose(0, 3, 2, 1))
    wd = np.ascontiguousarray(
        np.asarray(w_down, dtype=np.float32).reshape(KC, P, HC, P).transpose(0, 3, 2, 1))

    ident = np.eye(P, dtype=ml_dtypes.bfloat16)
    perm = np.zeros((P, P), dtype=ml_dtypes.bfloat16)
    perm[(np.arange(P) + P // 2) % P, np.arange(P)] = 1.0
    xf = np.asarray(x, dtype=np.float32).reshape(-1, DM)
    in_maps = []
    for c in range(NCORES):
        idx = owned_tokens(c, NSLOT)
        xc = np.ascontiguousarray(xf[idx, :].T)
        pos = idx.astype(np.float64)
        dims = np.arange(64, dtype=np.float64)
        freqs = 10000.0 ** (-dims / 64.0)
        rad = freqs[:, None] * pos[None, :]
        cos2 = np.concatenate([np.cos(rad), np.cos(rad)], axis=0)
        sin2 = np.concatenate([-np.sin(rad), np.sin(rad)], axis=0)
        kk = np.arange(P)[:, None, None]
        kt = np.arange(NCORES)[None, :, None]
        qi = np.arange(P)[None, None, :]
        m01 = (128 * kt + kk <= 128 * c + qi).astype(np.float32).reshape(
            P, NCORES * P)
        in_maps.append({
            "x": xc,
            "wq": wq, "wk": wk, "wv": wv, "wo": wo,
            "wg": wg, "wu": wu, "wd": wd,
            "cos2": cos2.astype(ml_dtypes.bfloat16),
            "sin2": sin2.astype(ml_dtypes.bfloat16),
            "mask01": m01.astype(ml_dtypes.bfloat16),
            "ident": ident,
            "perm": perm,
        })
    return in_maps


_NC_CACHE = {}


def kernel(L0_attn_x, L0_attn_w_aq, L0_attn_w_ak, L0_attn_w_av, L0_attn_w_ao,
           L0_ffn_w_gate, L0_ffn_w_up, L0_ffn_w_down):
    cfg = FULL_CFG
    in_maps = prep_inputs(L0_attn_x, L0_attn_w_aq, L0_attn_w_ak, L0_attn_w_av,
                          L0_attn_w_ao, L0_ffn_w_gate, L0_ffn_w_up,
                          L0_ffn_w_down, cfg)
    if "full" not in _NC_CACHE:
        _NC_CACHE["full"] = build_nc(cfg)
    nc = _NC_CACHE["full"]
    res = run_bass_kernel_spmd(nc, in_maps, core_ids=list(range(NCORES)))
    T = cfg["NSLOT"] * P * NCORES
    DM = cfg["KC"] * P
    out = np.empty((1, T, DM), dtype=np.float32)
    for c in range(NCORES):
        idx = owned_tokens(c, cfg["NSLOT"])
        out[0, idx, :] = res.results[c]["out"].T
    return out


# version 25 (final)
# speedup vs baseline: 1.0529x; 1.0311x over previous
import numpy as np
import ml_dtypes

import concourse.bass as bass
import concourse.mybir as mybir
from concourse.bass_utils import run_bass_kernel_spmd

F32 = mybir.dt.float32
BF16 = mybir.dt.bfloat16
AF = mybir.ActivationFunctionType
AX = mybir.AxisListType
ALU = mybir.AluOpType

NCORES = 8
P = 128

FULL_CFG = dict(H=16, KC=16, NSLOT=8, HC=64, HGRP=8)


class Ctr:
    def __init__(self, sem, step):
        self.sem = sem
        self.step = step
        self.val = 0

    def plan(self):
        self.val += self.step
        return self.val


class Plan:

    def __init__(self):
        self.q = {k: [] for k in ("t", "v", "s", "y", "g")}

    def emit(self, eng, fn, ctr=None):
        val = ctr.plan() if ctr is not None else None

        def run(e, fn=fn, ctr=ctr):
            inst = fn(e)
            if ctr is not None:
                inst.then_inc(ctr.sem, ctr.step)

        self.q[eng].append(run)
        return val

    def wait(self, eng, ctr, val):
        if val is None or val <= 0:
            return
        self.q[eng].append(lambda e, s=ctr.sem, v=val: e.wait_ge(s, v))


def build_nc(cfg):
    H, KC, NSLOT, HC, HGRP = (cfg["H"], cfg["KC"], cfg["NSLOT"], cfg["HC"],
                              cfg["HGRP"])
    TOK = NSLOT * P
    DM = KC * P
    TH = TOK // 2
    NW = NSLOT // 2

    nc = bass.Bass("TRN2", target_bir_lowering=False)

    x_d = nc.dram_tensor("x", [DM, TOK], F32, kind="ExternalInput").ap()
    wq_d = nc.dram_tensor("wq", [H, P, KC, P], F32, kind="ExternalInput").ap()
    wk_d = nc.dram_tensor("wk", [H, P, KC, P], F32, kind="ExternalInput").ap()
    wv_d = nc.dram_tensor("wv", [H, P, KC, P], F32, kind="ExternalInput").ap()
    wo_d = nc.dram_tensor("wo", [KC, P, H, P], F32, kind="ExternalInput").ap()
    wg_d = nc.dram_tensor("wg", [HC, P, KC, P], F32, kind="ExternalInput").ap()
    wu_d = nc.dram_tensor("wu", [HC, P, KC, P], F32, kind="ExternalInput").ap()
    wd_d = nc.dram_tensor("wd", [KC, P, HC, P], F32, kind="ExternalInput").ap()
    cos_d = nc.dram_tensor("cos2", [P, TOK], BF16, kind="ExternalInput").ap()
    sin_d = nc.dram_tensor("sin2", [P, TOK], BF16, kind="ExternalInput").ap()
    msk_d = nc.dram_tensor("mask01", [P, NCORES * P], BF16, kind="ExternalInput").ap()
    idn_d = nc.dram_tensor("ident", [P, P], BF16, kind="ExternalInput").ap()
    prm_d = nc.dram_tensor("perm", [P, P], BF16, kind="ExternalInput").ap()
    out_d = nc.dram_tensor("out", [DM, TOK], F32, kind="ExternalOutput").ap()

    kloc = nc.dram_tensor("kloc", [H * NSLOT * P, P], BF16)
    vloc = nc.dram_tensor("vloc", [H * NSLOT * P, P], BF16)
    qloc = nc.dram_tensor("qloc", [H * P, TOK], BF16)
    HH = H // 2
    kgathA = nc.dram_tensor("kgathA", [NCORES * HH * NSLOT * P, P], BF16)
    kgathB = nc.dram_tensor("kgathB", [NCORES * HH * NSLOT * P, P], BF16)
    vgathA = nc.dram_tensor("vgathA", [NCORES * HH * NSLOT * P, P], BF16)
    vgathB = nc.dram_tensor("vgathB", [NCORES * HH * NSLOT * P, P], BF16)

    kloc4 = kloc.ap().rearrange("(h s d) t -> h s d t", h=H, s=NSLOT)
    vloc4 = vloc.ap().rearrange("(h s t) d -> h s t d", h=H, s=NSLOT)
    qloc3 = qloc.ap().rearrange("(h d) t -> h d t", h=H)
    kg5A = kgathA.ap().rearrange("(r h s d) t -> r h s d t", r=NCORES, h=HH, s=NSLOT)
    kg5B = kgathB.ap().rearrange("(r h s d) t -> r h s d t", r=NCORES, h=HH, s=NSLOT)
    vg5A = vgathA.ap().rearrange("(r h s t) d -> r h s t d", r=NCORES, h=HH, s=NSLOT)
    vg5B = vgathB.ap().rearrange("(r h s t) d -> r h s t d", r=NCORES, h=HH, s=NSLOT)

    NWB = 3
    OVLW = max(
        3 * TOK + HGRP * TOK // 2 + 512 + HGRP * P,
        7 * TOK + 512,
        4 * TOK + TOK + 1024 + 780 + 1536,
    )

    from contextlib import ExitStack
    with ExitStack() as _ctx:
        ec = _ctx.enter_context
        XS = ec(nc.sbuf_tensor("XS", [P, KC * TOK], F32))
        XH = ec(nc.sbuf_tensor("XH", [P, KC * TOK], BF16))
        RS = ec(nc.sbuf_tensor("RS", [P, KC * TOK], F32))
        WRING = ec(nc.sbuf_tensor("WRING", [P, NWB * KC * P], BF16))
        OVL = ec(nc.sbuf_tensor("OVL", [P, OVLW], F32))
        IDB = ec(nc.sbuf_tensor("IDB", [P, P], BF16))
        PRM = ec(nc.sbuf_tensor("PRM", [P, P], BF16))
        ONESC = ec(nc.sbuf_tensor("ONESC", [P, 1], BF16))
        ONESR = ec(nc.sbuf_tensor("ONESR", [1, P], BF16))
        pS0 = ec(nc.psum_tensor("pS0", [P, 1024], F32))
        pS1 = ec(nc.psum_tensor("pS1", [P, 1024], F32))
        pD0 = ec(nc.psum_tensor("pD0", [P, 512], F32))
        pD1 = ec(nc.psum_tensor("pD1", [P, 512], F32))
        pD2 = ec(nc.psum_tensor("pD2", [P, 512], F32))
        pD3 = ec(nc.psum_tensor("pD3", [P, 512], F32))
        ld0_s = ec(nc.semaphore("ld0"))
        qstg_s = ec(nc.semaphore("qstg"))
        kstg_s = ec(nc.semaphore("kstg"))
        vstg_s = ec(nc.semaphore("vstg"))
        kstg2_s = ec(nc.semaphore("kstg2"))
        vstg2_s = ec(nc.semaphore("vstg2"))
        qb_ss = [ec(nc.semaphore(f"qb{i}")) for i in range(2)]
        xc_ss = [ec(nc.semaphore(f"xc{i}")) for i in range(KC)]
        kb_ss = [ec(nc.semaphore(f"kb{i}")) for i in range(4)]
        vb_ss = [ec(nc.semaphore(f"vb{i}")) for i in range(3)]
        ring_ss = [ec(nc.semaphore(f"ring{i}")) for i in range(4)]
        wd_ss = [ec(nc.semaphore(f"wdb{i}")) for i in range(2)]
        tv_s = ec(nc.semaphore("tv"))
        vv_s = ec(nc.semaphore("vv"))
        sv_s = ec(nc.semaphore("sv"))
        cc_s = ec(nc.semaphore("cc"))
        block = ec(nc.Block())
        ld0 = Ctr(ld0_s, 16)
        qstg = Ctr(qstg_s, 16)
        kstg = Ctr(kstg_s, 16)
        vstg = Ctr(vstg_s, 16)
        kstg2 = Ctr(kstg2_s, 16)
        vstg2 = Ctr(vstg2_s, 16)
        qb_c = [Ctr(s, 16) for s in qb_ss]
        xc_c = [Ctr(s, 16) for s in xc_ss]
        kb_c = [Ctr(s, 16) for s in kb_ss]
        vb_c = [Ctr(s, 16) for s in vb_ss]
        ring_c = [Ctr(s, 16) for s in ring_ss]
        wd_c = [Ctr(s, 16) for s in wd_ss]
        tv = Ctr(tv_s, 1)
        vv = Ctr(vv_s, 1)
        sv = Ctr(sv_s, 1)
        cc = Ctr(cc_s, 1)
        pl = Plan()

        ovl = OVL.ap()

        def carve_bf(c0, n):
            assert n % 2 == 0
            return ovl[:, c0:c0 + n // 2].bitcast(BF16)

        def carve_f(c0, n):
            return ovl[:, c0:c0 + n]

        COS2 = carve_bf(0, TOK)
        SIN2 = carve_bf(TOK // 2, TOK)
        QKB = carve_bf(TOK, TOK)
        RTMP = carve_bf(TOK + TOK // 2, TOK)
        RU = carve_bf(2 * TOK, TOK)
        QSTG = carve_bf(2 * TOK + TOK // 2, TOK)
        KSTG = carve_bf(3 * TOK, TOK)
        VTSTG = carve_bf(3 * TOK + TOK // 2, TOK)
        SQA_A = carve_bf(4 * TOK, TOK)
        SQB_A = VTSTG
        SMALLS_A = carve_f(4 * TOK + TOK // 2, 2 * TOK + TOK // 2)
        TMPA_A = carve_f(7 * TOK, 512)
        TMPA2_A = carve_f(4 * TOK, min(512, TOK // 2))

        _b = 4 * TOK
        QTB = carve_bf(_b, 2 * TOK)
        _b += TOK
        KB = carve_bf(_b, 2048)
        _b += 1024
        VB = carve_bf(_b, 1560)
        _b += 780
        PB = carve_bf(_b, 3072)
        _b += 1536
        CFGB_END = _b
        MASK = carve_bf(0, NCORES * P)
        OBS = carve_bf(NCORES * P // 2, NW * P)
        LINV = carve_f(NCORES * P // 2 + NW * P // 2, max(NW, 4))
        assert NCORES * P // 2 + NW * P // 2 + max(NW, 4) <= 4 * TOK
        assert CFGB_END <= OVLW

        SIL = carve_bf(0, TOK)
        HREG = carve_bf(TOK // 2, HGRP * TOK)
        SMALLS_C = carve_f(TOK // 2 + HGRP * TOK // 2, 2 * TOK + TOK // 2)
        TMPA_C = carve_f(3 * TOK + HGRP * TOK // 2, 512)
        TMPA2_C = carve_f(4 * TOK, min(512, TOK // 2))
        WDBUF = carve_bf(3 * TOK + HGRP * TOK // 2 + 512, 2 * HGRP * P)
        assert 3 * TOK + HGRP * TOK // 2 + 512 + HGRP * P <= OVLW, OVLW

        banks = {"S0": pS0, "S1": pS1,
                 "D0": pD0, "D1": pD1, "D2": pD2, "D3": pD3}
        tpsum = pD2.ap().bitcast(BF16)
        tpsum_w = pD0.ap().bitcast(BF16)
        bank_free = {k: None for k in banks}

        def t_mm(out, lhsT, rhs, start, stop, inc=False):
            return pl.emit(
                "t", lambda t: t.matmul(out, lhsT, rhs, start=start, stop=stop),
                tv if inc else None)

        def wait_bank(eng, name):
            bf = bank_free[name]
            if bf is not None:
                pl.wait(eng, bf[0], bf[1])

        ring_srcs = (
            [(wq_d, wk_d, wv_d)[pn][h] for h in range(H) for pn in range(3)]
            + [wo_d[mc] for mc in range(KC)]
            + [(wg_d, wu_d)[pn][hc] for hc in range(HC) for pn in range(2)]
        )
        NR = len(ring_srcs)
        ring_consumed = {}
        ring_loaded = {}
        ring_next = [0]

        def ring_buf3(i):
            return WRING.ap()[:, (i % NWB) * KC * P:(i % NWB + 1) * KC * P].rearrange(
                "p (kc d) -> p kc d", d=P)

        def ring_ensure(upto):
            while ring_next[0] <= min(upto, NR - 1):
                j = ring_next[0]
                if j >= NWB:
                    c, vval = ring_consumed[j - NWB]
                    pl.wait("g", c, vval)
                rc = ring_c[j % NWB]
                ring_loaded[j] = (rc, pl.emit(
                    "g", lambda g, s3=ring_srcs[j], d3=ring_buf3(j):
                    g.dma_start(out=d3, in_=s3), rc))
                ring_next[0] += 1

        ring_ensure(NWB - 1)

        x3s = x_d.rearrange("(kc p) t -> p kc t", p=P)
        XS3 = XS.ap().rearrange("p (kc t) -> p kc t", t=TOK)
        L_cos = pl.emit("s", lambda s: s.dma_start(out=COS2, in_=cos_d), ld0)
        L_sin = pl.emit("s", lambda s: s.dma_start(out=SIN2, in_=sin_d), ld0)
        L_idn = pl.emit("s", lambda s: s.dma_start(out=IDB[:], in_=idn_d), ld0)
        L_prm = pl.emit("s", lambda s: s.dma_start(out=PRM[:], in_=prm_d), ld0)
        L_ph0 = ld0.val
        L_xc = {}
        for kc in range(KC):
            xq = "y" if (kc < 4 or kc % 2 == 1) else "s"
            L_xc[kc] = pl.emit(xq, lambda s, k=kc: s.dma_start(
                out=XS3[:, k, :], in_=x3s[:, k, :]), xc_c[kc])

        pl.emit("v", lambda v: v.memset(ONESC[:], 1.0), vv)
        V_ones = pl.emit("v", lambda v: v.memset(ONESR[:], 1.0), vv)

        def ln(src_chunk, cast_slot, sq_slot, apply_out, smalls, tmpa,
               pre_wait_v=None, pre_wait_t=None, cast_dbuf=False,
               kc_done_cb=None, pre_kc=None, pre_sq=None):
            V_cast = {}
            S_sq = {}
            T_statk = {}
            for kc in range(KC):
                if kc == 0:
                    if pre_wait_v is not None:
                        pl.wait("v", *pre_wait_v)
                    pl.wait("v", ld0, ld0.val)
                if pre_kc is not None:
                    pl.wait("v", *pre_kc(kc))
                if cast_dbuf and kc >= 2:
                    pl.wait("v", tv, T_statk[kc - 2])
                V_cast[kc] = pl.emit(
                    "v", lambda v, s=src_chunk(kc), d=cast_slot(kc):
                    v.tensor_copy(d, s), vv)
                if kc == 0:
                    pl.wait("s", ld0, ld0.val)
                    if pre_wait_v is not None:
                        pl.wait("s", *pre_wait_v)
                    if pre_sq is not None:
                        pl.wait("s", *pre_sq)
                if pre_kc is not None:
                    pl.wait("s", *pre_kc(kc))
                if kc >= 2:
                    pl.wait("s", tv, T_statk[kc - 2])
                S_sq[kc] = pl.emit(
                    "s", lambda s, a=src_chunk(kc), d=sq_slot(kc):
                    s.activation(d, a, AF.Square), sv)
                if kc == 0:
                    if pre_wait_t is not None:
                        pl.wait("t", *pre_wait_t)
                    for bname in ("D0", "D1", "D2", "D3"):
                        wait_bank("t", bname)
                    pl.wait("t", vv, V_ones)
                pl.wait("t", vv, V_cast[kc])
                pl.wait("t", sv, S_sq[kc])
                st, sp = kc == 0, kc == KC - 1
                t_mm(pD0[0:1, 0:TH], ONESC[:], cast_slot(kc)[:, 0:TH], st, sp)
                t_mm(pD1[0:1, 0:TH], ONESC[:], cast_slot(kc)[:, TH:TOK], st, sp)
                t_mm(pD2[0:1, 0:TH], ONESC[:], sq_slot(kc)[:, 0:TH], st, sp)
                T_statk[kc] = t_mm(pD3[0:1, 0:TH], ONESC[:],
                                   sq_slot(kc)[:, TH:TOK], st, sp, inc=True)
            T_stats = T_statk[KC - 1]

            STTA = smalls[0:1, 0:TOK]
            STTB = smalls[0:1, TOK:2 * TOK]
            meanb = smalls[0:1, 2 * TOK:2 * TOK + TOK // 2].bitcast(BF16)
            rstdb = meanb

            def chain_v(fn):
                pl.wait("v", vv, vv.val)
                return pl.emit("v", fn, vv)

            pl.wait("v", tv, T_stats)
            pl.emit("v", lambda v: v.tensor_copy(STTA[:, 0:TH], pD0[0:1, 0:TH]), vv)
            chain_v(lambda v: v.tensor_copy(STTA[:, TH:TOK], pD1[0:1, 0:TH]))
            chain_v(lambda v: v.tensor_copy(STTB[:, 0:TH], pD2[0:1, 0:TH]))
            V_evac = chain_v(lambda v: v.tensor_copy(STTB[:, TH:TOK],
                                                     pD3[0:1, 0:TH]))
            inv = 1.0 / DM
            chain_v(lambda v: v.tensor_scalar_mul(STTA, STTA, inv))
            V_mb = chain_v(lambda v: v.tensor_copy(meanb, STTA))
            chain_v(lambda v: v.tensor_mul(STTA, STTA, STTA))
            chain_v(lambda v: v.tensor_scalar_mul(STTB, STTB, inv))
            chain_v(lambda v: v.tensor_sub(STTB, STTB, STTA))
            V_var = chain_v(lambda v: v.tensor_scalar_add(STTB, STTB, 1e-5))
            pl.wait("s", vv, V_var)
            c = pl.emit("s", lambda s: s.activation(STTA, STTB, AF.Ln), sv)
            pl.wait("s", sv, c)
            S_rstd = pl.emit("s", lambda s: s.activation(STTB, STTA, AF.Exp,
                                                         scale=-0.5), sv)

            pl.wait("t", vv, V_evac)
            pl.wait("t", vv, V_mb)
            t_mm(pD0[:, 0:TH], ONESR[:], meanb[0:1, 0:TH], True, True)
            T_bcm = t_mm(pD1[:, 0:TH], ONESR[:], meanb[0:1, TH:TOK],
                         True, True, inc=True)
            pl.wait("v", tv, T_bcm)
            pl.wait("v", sv, S_rstd)
            V_rb = chain_v(lambda v: v.tensor_copy(rstdb, STTB))
            pl.wait("t", vv, V_rb)
            t_mm(pD2[:, 0:TH], ONESR[:], rstdb[0:1, 0:TH], True, True)
            T_bc = t_mm(pD3[:, 0:TH], ONESR[:], rstdb[0:1, TH:TOK],
                        True, True, inc=True)

            last_apply = None
            kc_ms = {}
            tmpa_last = {}
            pair_i = 0
            for kc in range(KC):
                for half in range(2):
                    mb = (pD0 if half == 0 else pD1)[:, 0:TH]
                    rb = (pD2 if half == 0 else pD3)[:, 0:TH]
                    if kc == 0 and half == 0:
                        pl.wait("v", tv, T_bc)
                    sl = slice(half * TH, (half + 1) * TH)
                    ta = tmpa[pair_i % 2]
                    if pair_i >= 2:
                        pl.wait("v", vv, tmpa_last[pair_i % 2])
                    pl.emit("v", lambda v, s=src_chunk(kc), m=mb, d=ta, q=sl:
                            v.tensor_sub(d[:, 0:TH], s[:, q], m), vv)
                    last_apply = apply_out(kc, half, ta, rb)
                    tmpa_last[pair_i % 2] = last_apply
                    pair_i += 1
                    if half == 1:
                        kc_ms[kc] = last_apply
                        if kc_done_cb is not None:
                            kc_done_cb(kc, last_apply)
            for bname in ("D0", "D1", "D2", "D3"):
                bank_free[bname] = (vv, last_apply)
            return last_apply, kc_ms

        def xs_chunk(kc):
            return XS.ap()[:, kc * TOK:(kc + 1) * TOK]

        def xh_chunk(kc):
            return XH.ap()[:, kc * TOK:(kc + 1) * TOK]

        def rs_chunk(kc):
            return RS.ap()[:, kc * TOK:(kc + 1) * TOK]

        def ln12_apply(kc, half, tmpa, rb):
            sl = slice(half * TH, (half + 1) * TH)
            pl.wait("v", vv, vv.val)
            return pl.emit("v", lambda v, d=xh_chunk(kc), a=tmpa, r=rb, q=sl:
                           v.tensor_mul(d[:, q], a[:, 0:TH], r), vv)

        def ln34_apply(kc, half, tmpa, rb):
            sl = slice(half * TH, (half + 1) * TH)
            pl.wait("v", vv, vv.val)
            pl.emit("v", lambda v, a=tmpa, r=rb:
                    v.tensor_mul(a[:, 0:TH], a[:, 0:TH], r), vv)
            pl.wait("v", vv, vv.val)
            return pl.emit("v", lambda v, d=xs_chunk(kc), a=tmpa, q=sl:
                           v.tensor_add(d[:, q], d[:, q], a[:, 0:TH]), vv)

        V_xhat1, M_xhat1 = ln(
            xs_chunk, cast_slot=xh_chunk,
            sq_slot=lambda kc: SQA_A if kc % 2 == 0 else SQB_A,
            apply_out=ln12_apply, smalls=SMALLS_A, tmpa=(TMPA_A, TMPA2_A),
            pre_kc=lambda kc: (xc_c[kc], L_xc[kc]))

        pair = {0: "S0", 1: "S1"}
        L_kst = {}
        L_vst = {}
        L_qst = {}
        _qkv_state = {"qkb_free": None, "tpsum_free": None, "V_rope_last": 0,
                      "C_k1": None, "C_k2": None, "C_v1": None, "C_v2": None}

        qkv_tails = []

        def qkv_flush_tail():
            if qkv_tails:
                qkv_tails.pop(0)()

        def make_tail(h, pnum, i, bn, b, T_proj):
            def tail():
                self_ = _qkv_state
                pl.wait("s", tv, T_proj)
                for w_ in (self_["qkb_free"] or []):
                    pl.wait("s", *w_)
                c = pl.emit("s", lambda s, b=b:
                            s.activation(QKB[:, 0:TH], b[:, 0:TH], AF.Copy), sv)
                pl.wait("s", sv, c)
                S_evac = pl.emit("s", lambda s, b=b:
                                 s.activation(QKB[:, TH:TOK], b[:, 512:512 + TH],
                                              AF.Copy), sv)
                bank_free[bn] = (sv, S_evac)

                if pnum < 2:
                    stg = QSTG if pnum == 0 else KSTG
                    stgc = qstg if pnum == 0 else (kstg if h < HH else kstg2)
                    wait_bank("t", "D0")
                    wait_bank("t", "D1")
                    pl.wait("t", sv, S_evac)
                    t_mm(pD0[:, 0:TH], PRM[:], QKB[:, 0:TH], True, True)
                    T_swp = t_mm(pD1[:, 0:TH], PRM[:], QKB[:, TH:TOK], True,
                                 True, inc=True)
                    if pnum == 1:
                        pl.wait("v", kstg, kstg.val)
                        pl.wait("v", kstg2, kstg2.val)
                    else:
                        pl.wait("v", stgc, stgc.val)
                    pl.wait("v", sv, S_evac)
                    if h == 0 and pnum == 0:
                        pl.wait("v", ld0, L_ph0)
                    pl.wait("v", vv, vv.val)
                    c1 = pl.emit("v", lambda v: v.tensor_mul(RTMP, QKB, COS2), vv)
                    self_["qkb_free"] = [(vv, c1), (tv, T_swp)]
                    pl.wait("v", tv, T_swp)
                    pl.emit("v", lambda v: v.tensor_mul(
                        RU[:, 0:TH], pD0[:, 0:TH], SIN2[:, 0:TH]), vv)
                    c2 = pl.emit("v", lambda v: v.tensor_mul(
                        RU[:, TH:TOK], pD1[:, 0:TH], SIN2[:, TH:TOK]), vv)
                    bank_free["D0"] = (vv, c2)
                    bank_free["D1"] = (vv, c2)
                    pl.wait("v", vv, vv.val)
                    V_rope = pl.emit("v", lambda v, d=stg: v.tensor_add(
                        d, RTMP, RU), vv)
                    self_["V_rope_last"] = V_rope
                    pl.wait("y", vv, V_rope)
                    if pnum == 0:
                        L_qst[h] = pl.emit(
                            "y", lambda s, hh=h: s.dma_start(out=qloc3[hh],
                                                             in_=QSTG), qstg)
                    else:
                        ks = KSTG.rearrange("d (s t) -> d s t", s=NSLOT)
                        kd = kloc4[h].rearrange("s d t -> d s t")
                        L_kst[h] = pl.emit(
                            "y", lambda s, a=kd, b2=ks: s.dma_start(out=a, in_=b2),
                            stgc)
                        if h == HH and pnum == 1:
                            pl.wait("g", kstg, kstg.val)
                            self_["C_k1"] = pl.emit(
                                "g", lambda g: g.collective_compute(
                                    "AllGather", ALU.bypass,
                                    replica_groups=[list(range(NCORES))],
                                    ins=[kloc.ap()[0:HH * NSLOT * P, :].opt()],
                                    outs=[kgathA.ap().opt()]), cc)

                else:
                    vstgc = vstg if h < HH else vstg2
                    if self_["tpsum_free"] is not None:
                        pl.wait("t", *self_["tpsum_free"])
                    wait_bank("t", "D2")
                    pl.wait("t", sv, S_evac)
                    for sl_ in range(NSLOT):
                        T_vtr = pl.emit(
                            "t", lambda t, ss=sl_: t.transpose(
                                tpsum[:, ss * P:(ss + 1) * P],
                                QKB[:, ss * P:(ss + 1) * P], IDB[:]),
                            tv if sl_ == NSLOT - 1 else None)
                    self_["qkb_free"] = [(tv, T_vtr)]
                    pl.wait("s", vstg, vstg.val)
                    pl.wait("s", vstg2, vstg2.val)
                    pl.wait("s", tv, T_vtr)
                    S_vt = pl.emit("s", lambda s: s.activation(
                        VTSTG, tpsum[:, 0:NSLOT * P], AF.Copy), sv)
                    self_["tpsum_free"] = (sv, S_vt)
                    bank_free["D2"] = (sv, S_vt)
                    pl.wait("y", sv, S_vt)
                    vs = VTSTG.rearrange("t (s d) -> t s d", s=NSLOT)
                    vd = vloc4[h].rearrange("s t d -> t s d")
                    L_vst[h] = pl.emit(
                        "y", lambda s, a=vd, b2=vs: s.dma_start(out=a, in_=b2),
                        vstgc)
                    if h == HH:
                        pl.wait("g", vstg, vstg.val)
                        self_["C_v1"] = pl.emit(
                            "g", lambda g: g.collective_compute(
                                "AllGather", ALU.bypass,
                                replica_groups=[list(range(NCORES))],
                                ins=[vloc.ap()[0:HH * NSLOT * P, :].opt()],
                                outs=[vgathA.ap().opt()]), cc)

            return tail

        for h in range(H):
            for pnum in range(3):
                i = 3 * h + pnum
                ring_ensure(i + NWB - 1)
                bn = pair[i % 2]
                b = banks[bn]
                pl.wait("t", *ring_loaded[i])
                if i == 0:
                    pl.wait("t", ld0, L_ph0)
                wait_bank("t", bn)
                wb3 = ring_buf3(i)
                for kc in range(KC):
                    if i == 0:
                        pl.wait("t", vv, M_xhat1[kc])
                    st, sp = kc == 0, kc == KC - 1
                    t_mm(b[:, 0:TH], wb3[:, kc, :], xh_chunk(kc)[:, 0:TH], st, sp)
                    T_proj = t_mm(b[:, 512:512 + TH], wb3[:, kc, :],
                                  xh_chunk(kc)[:, TH:TOK], st, sp,
                                  inc=(kc == KC - 1))
                ring_consumed[i] = (tv, T_proj)
                qkv_flush_tail()
                qkv_tails.append(make_tail(h, pnum, i, bn, b, T_proj))
        qkv_flush_tail()
        C_k1 = _qkv_state["C_k1"]
        C_v1 = _qkv_state["C_v1"]
        C_k2 = None
        C_v2 = None
        T_sc_last = None
        V_rope_last = _qkv_state["V_rope_last"]


        pl.wait("s", vv, _qkv_state["V_rope_last"])
        L_msk = pl.emit("s", lambda s: s.dma_start(out=MASK, in_=msk_d), ld0)

        pl.wait("v", vv, vv.val)
        V_vbones = None
        for par in range(3):
            for i4 in range(4):
                col = par * 520 + i4 * 130 + 128
                V_vbones = pl.emit(
                    "v", lambda v, cl=col: v.memset(VB[:, cl:cl + 1], 1.0), vv)

        sbank_names = ["S0", "S1"]
        obank_names = ["D0", "D1", "D2", "D3"]
        kb_free = {}
        vb_free = {}
        pb_free = {}
        L_q = {}
        ii = 0
        jj = 0
        S_ot_last = 0
        S_ot_head = {}
        T_otr_last = [0]
        waves = [(0, NW), (NW, NSLOT)]

        for h in range(2):
            pl.wait("y", qstg, qstg.val)
            pl.wait("y", vv, V_rope_last)
            L_q[h] = pl.emit("y", lambda s, hh=h: s.dma_start(
                out=QTB[:, (hh % 2) * TOK:(hh % 2 + 1) * TOK], in_=qloc3[hh]),
                qb_c[h % 2])

        for h in range(H):
            if h == min(2, HH) and C_k2 is None:
                if T_sc_last is not None:
                    pl.wait("g", tv, T_sc_last)
                pl.wait("g", kstg2, kstg2.val)
                C_k2 = pl.emit("g", lambda g: g.collective_compute(
                    "AllGather", ALU.bypass,
                    replica_groups=[list(range(NCORES))],
                    ins=[kloc.ap()[HH * NSLOT * P:, :].opt()],
                    outs=[kgathB.ap().opt()]), cc)
                pl.wait("g", vstg2, vstg2.val)
                C_v2 = pl.emit("g", lambda g: g.collective_compute(
                    "AllGather", ALU.bypass,
                    replica_groups=[list(range(NCORES))],
                    ins=[vloc.ap()[HH * NSLOT * P:, :].opt()],
                    outs=[vgathB.ap().opt()]), cc)
            qv = QTB[:, (h % 2) * TOK:(h % 2 + 1) * TOK]
            T_sc_last = None
            for iw, (s_lo, s_hi) in enumerate(waves):
                jmax = 2 * (s_hi - 1) + 2
                T_stop = {}
                pending = []
                pairbuf = []
                np_ = [0]
                ns_ = [0]

                def flush_pair():
                    if not pairbuf:
                        return
                    pp = np_[0] % 3
                    sbank = ns_[0] % 2
                    np_[0] += 1
                    ns_[0] += 1
                    sbn = sbank_names[sbank]
                    sb_v = banks[sbn]
                    n_it = len(pairbuf)
                    pbv_pair = PB[:, pp * 1024:pp * 1024 + 1024]
                    while pending and any(e.get("ppar") == pp for e in pending):
                        emit_pv(pending.pop(0))
                    if pp in pb_free:
                        pl.wait("s", *pb_free[pp])
                    pl.wait("s", tv, pairbuf[-1]["T_sc"])
                    S_exp = pl.emit(
                        "s", lambda sc, d=pbv_pair[:, 0:n_it * 512],
                        b=sb_v[:, 0:n_it * 512]:
                        sc.activation(d, b, AF.Exp, scale=1.0 / 128.0), sv)
                    bank_free[sbn] = (sv, S_exp)
                    for it in pairbuf:
                        pv_ctr, pv_val = sv, S_exp
                        if it["diag"] is not None:
                            half = it["diag"]
                            pc = it["pcols"]
                            stride = pc[1] - pc[0]
                            if stride == P:
                                p_ap = pbv_pair.rearrange(
                                    "p (kt q) -> p kt q", q=P)[:, pc[0] // P:
                                                               pc[0] // P + 4, :]
                            else:
                                off = pc[0]
                                p_ap = pbv_pair.rearrange(
                                    "p (kt q) -> p kt q", q=stride)[:, :, off:off + P]
                            m_ap = MASK[:, half * 512:(half + 1) * 512].rearrange(
                                "p (kt q) -> p kt q", q=P)
                            pl.wait("v", ld0, L_msk)
                            pl.wait("v", sv, S_exp)
                            V_msk = pl.emit(
                                "v", lambda v, d=p_ap, m=m_ap:
                                v.tensor_mul(d, d, m), vv)
                            pv_ctr, pv_val = vv, V_msk
                        it["pv_ctr"], it["pv_val"] = pv_ctr, pv_val
                        it["ppar"] = pp
                        it["pbv"] = pbv_pair
                        pending.append(it)
                    pairbuf.clear()

                def emit_pv(p):
                    pl.wait("t", p["pv_ctr"], p["pv_val"])
                    pl.wait("t", vb_c[p["jpar"]], p["L_v"])
                    if p["first"]:
                        wait_bank("t", p["ob"])
                    for kt in range(4):
                        pc = p["pcols"][kt]
                        T_pv = t_mm(
                            p["ob_t"][:, 0:129],
                            p["pbv"][:, pc:pc + P],
                            VB[:, p["jpar"] * 520 + kt * 130:
                               p["jpar"] * 520 + kt * 130 + 129],
                            start=(p["first"] and kt == 0),
                            stop=(p["lastj"] and kt == 3),
                            inc=(kt == 3))
                    pbf = pb_free.get(p["ppar"])
                    pb_free[p["ppar"]] = (tv, T_pv)
                    vb_free[p["jpar"]] = (tv, T_pv)
                    if p["lastj"]:
                        T_stop[p["s_"]] = T_pv

                jj0 = jj

                kloads = {}
                vloads = {}

                def issue_k(jn):
                    if jn in kloads or jn >= jmax:
                        return
                    jsq = jj0 + jn
                    par = jsq % 4
                    r0n = (4 * jn) % NCORES
                    spn = jn // 2
                    kg5 = kg5A if h < HH else kg5B
                    hl = h if h < HH else h - HH
                    if iw == 0 and jn == 0:
                        pl.wait("y", cc, C_k1 if h < HH else C_k2)
                    if par in kb_free:
                        pl.wait("y", *kb_free[par])
                    ksrc = kg5[r0n:r0n + 4, hl, spn, :, :].rearrange("r d t -> d r t")
                    kdst = KB[:, par * 512:(par + 1) * 512].rearrange(
                        "d (r t) -> d r t", r=4)
                    kloads[jn] = (pl.emit("y", lambda s, a=kdst, b=ksrc:
                                          s.dma_start(out=a, in_=b), kb_c[par]), par)

                def issue_v(jn):
                    if jn in vloads or jn >= jmax:
                        return
                    jsq = jj0 + jn
                    par = jsq % 3
                    r0n = (4 * jn) % NCORES
                    spn = jn // 2
                    vg5 = vg5A if h < HH else vg5B
                    hl = h if h < HH else h - HH
                    if ii == 0 and jn == 0:
                        pl.wait("y", vv, V_vbones)
                    if iw == 0 and jn == 0:
                        pl.wait("y", cc, C_v1 if h < HH else C_v2)
                    if par in vb_free:
                        pl.wait("y", *vb_free[par])
                    vsrc = vg5[r0n:r0n + 4, hl, spn, :, :].rearrange("r t d -> t r d")
                    vdst = VB[:, par * 520:par * 520 + 520].rearrange(
                        "t (r e) -> t r e", e=130)[:, :, 0:128]
                    vloads[jn] = (pl.emit("y", lambda s, a=vdst, b=vsrc:
                                          s.dma_start(out=a, in_=b), vb_c[par]), par)

                for j in range(jmax):
                    jseq = jj
                    jj += 1
                    while pairbuf and pairbuf[0]["jseq"] <= jseq - 2:
                        flush_pair()
                    while pending and pending[0]["jseq"] <= jseq - 2:
                        emit_pv(pending.pop(0))
                    issue_k(j)
                    issue_v(j)
                    issue_v(j + 1)
                    issue_k(j + 1)
                    issue_k(j + 2)
                    issue_k(j + 3)
                    L_k, kpar = kloads[j]
                    L_v, jpar = vloads[j]
                    slots = list(range(max(s_lo, j // 2), s_hi))
                    si_ = 0
                    while si_ < len(slots):
                        sbn = sbank_names[ns_[0] % 2]
                        sb_v = banks[sbn]
                        q_ = len(pairbuf)
                        two = (q_ == 0 and si_ + 1 < len(slots))
                        if q_ == 0:
                            wait_bank("t", sbn)
                        pl.wait("t", kb_c[kpar], L_k)
                        pl.wait("t", qb_c[h % 2], L_q[h])
                        if two:
                            s_a = slots[si_]
                            for kt in range(4):
                                T_sc = t_mm(
                                    sb_v[:, kt * 256:(kt + 1) * 256],
                                    KB[:, kpar * 512 + kt * P:
                                       kpar * 512 + (kt + 1) * P],
                                    qv[:, s_a * P:(s_a + 2) * P],
                                    True, True, inc=(kt == 3))
                            for idx in range(2):
                                s_ = slots[si_ + idx]
                                diag = None
                                if j == 2 * s_ or j == 2 * s_ + 1:
                                    diag = j - 2 * s_
                                pairbuf.append(dict(
                                    T_sc=T_sc, diag=diag, jpar=jpar, jseq=jseq,
                                    pcols=[kt * 256 + idx * P for kt in range(4)],
                                    ob=obank_names[s_ - s_lo],
                                    ob_t=banks[obank_names[s_ - s_lo]],
                                    first=(j == 0), lastj=(j == 2 * s_ + 1),
                                    L_v=L_v, s_=s_))
                            si_ += 2
                            ii += 2
                        else:
                            s_ = slots[si_]
                            for kt in range(4):
                                T_sc = t_mm(
                                    sb_v[:, q_ * 512 + kt * P:
                                         q_ * 512 + (kt + 1) * P],
                                    KB[:, kpar * 512 + kt * P:
                                       kpar * 512 + (kt + 1) * P],
                                    qv[:, s_ * P:(s_ + 1) * P],
                                    True, True, inc=(kt == 3))
                            diag = None
                            if j == 2 * s_ or j == 2 * s_ + 1:
                                diag = j - 2 * s_
                            pairbuf.append(dict(
                                T_sc=T_sc, diag=diag, jpar=jpar, jseq=jseq,
                                pcols=[q_ * 512 + kt * P for kt in range(4)],
                                ob=obank_names[s_ - s_lo],
                                ob_t=banks[obank_names[s_ - s_lo]],
                                first=(j == 0), lastj=(j == 2 * s_ + 1),
                                L_v=L_v, s_=s_))
                            si_ += 1
                            ii += 1
                        T_sc_last = T_sc
                        kb_free[kpar] = (tv, T_sc)
                        if len(pairbuf) == 2:
                            flush_pair()
                        while len(pending) > 4:
                            emit_pv(pending.pop(0))

                flush_pair()
                while pending:
                    emit_pv(pending.pop(0))

                V_on = {}
                for s_ in range(s_lo, s_hi):
                    si = s_ - s_lo
                    ob = obank_names[si]
                    ob_t = banks[ob]
                    li = LINV[:, si:si + 1]
                    pl.wait("v", tv, T_stop[s_])
                    c3 = pl.emit("v", lambda v, o=ob_t, l=li:
                                 v.reciprocal(l, o[:, 128:129]), vv)
                    pl.wait("v", vv, c3)
                    V_on[si] = pl.emit(
                        "v", lambda v, o=ob_t, l=li,
                        d=OBS[:, si * P:(si + 1) * P]:
                        v.tensor_scalar_mul(d, o[:, 0:128], l), vv)
                    bank_free[ob] = (vv, V_on[si])
                T_otr = None
                for si in range(s_hi - s_lo):
                    pl.wait("t", vv, V_on[0])
                    pl.wait("t", vv, V_on[si])
                    T_otr = pl.emit("t", lambda t, ss=si: t.transpose(
                        tpsum_w[:, ss * P:(ss + 1) * P],
                        OBS[:, ss * P:(ss + 1) * P], IDB[:]), tv)
                T_otr_last[0] = T_otr
                pl.wait("s", tv, T_otr)
                S_ot = pl.emit("s", lambda sc, hh=h, w=iw: sc.activation(
                    XH.ap()[:, hh * TOK + w * NW * P:hh * TOK + (w + 1) * NW * P],
                    tpsum_w[:, 0:NW * P], AF.Copy), sv)
                S_ot_last = S_ot
                S_ot_head[h] = S_ot
                bank_free["D0"] = (sv, S_ot)

            hh2 = h + 2
            if hh2 < H:
                pl.wait("y", tv, T_sc_last)
                L_q[hh2] = pl.emit("y", lambda s, q2=hh2: s.dma_start(
                    out=QTB[:, (q2 % 2) * TOK:(q2 % 2 + 1) * TOK],
                    in_=qloc3[q2]), qb_c[hh2 % 2])

        V_ro = {}
        T_oproj = {}
        for mc in range(KC):
            i = 3 * H + mc
            ring_ensure(i + NWB - 1)
            bn = pair[mc % 2]
            b = banks[bn]
            pl.wait("t", *ring_loaded[i])
            wait_bank("t", bn)
            wb3 = ring_buf3(i)
            for hh in range(H):
                if mc == 0:
                    pl.wait("t", sv, S_ot_head[hh])
                st, sp = hh == 0, hh == H - 1
                ot = XH.ap()[:, hh * TOK:(hh + 1) * TOK]
                t_mm(b[:, 0:TH], wb3[:, hh, :], ot[:, 0:TH], st, sp)
                T_o = t_mm(b[:, 512:512 + TH], wb3[:, hh, :], ot[:, TH:TOK], st, sp,
                           inc=(hh == H - 1))
            T_oproj[mc] = T_o
            ring_consumed[i] = (tv, T_o)
            pl.wait("v", tv, T_o)
            pl.emit("v", lambda v, b=b, d=rs_chunk(mc):
                    v.tensor_copy(d[:, 0:TH], b[:, 0:TH]), vv)
            V_ro[mc] = pl.emit("v", lambda v, b=b, d=rs_chunk(mc):
                               v.tensor_copy(d[:, TH:TOK], b[:, 512:512 + TH]), vv)
            bank_free[bn] = (vv, V_ro[mc])

        V_res1, M_res1 = ln(
            rs_chunk,
            cast_slot=lambda kc: xh_chunk(kc % 2),
            sq_slot=lambda kc: HREG[:, (kc % 2) * TOK:(kc % 2 + 1) * TOK],
            apply_out=ln34_apply, smalls=SMALLS_C, tmpa=(TMPA_C, TMPA2_C),
            pre_kc=lambda kc: (vv, V_ro[kc]), pre_wait_t=(tv, T_oproj[KC - 1]),
            cast_dbuf=True)

        V_xhat2, M_xhat2 = ln(
            xs_chunk, cast_slot=xh_chunk,
            sq_slot=lambda kc: HREG[:, (kc % 2) * TOK:(kc % 2 + 1) * TOK],
            apply_out=ln12_apply, smalls=SMALLS_C, tmpa=(TMPA_C, TMPA2_C),
            pre_kc=lambda kc: (vv, M_res1[kc]))

        L_wd = {}
        wd_free = {}
        V_d = {}
        h_free = {}
        prev_vh = [None]
        wd2 = wd_d
        ngrp = HC // HGRP

        def wd_load(g_, mc):
            key = (g_, mc)
            if key in L_wd:
                return
            if not L_wd:
                pl.wait("g", tv, T_otr_last[0])
            par = mc % 2
            if par in wd_free:
                pl.wait("g", *wd_free[par])
                del wd_free[par]
            src = wd2[mc][:, g_ * HGRP:(g_ + 1) * HGRP, :]
            dst = WDBUF[:, par * HGRP * P:(par + 1) * HGRP * P].rearrange(
                "p (i m) -> p i m", m=P)
            L_wd[key] = (wd_c[par], pl.emit(
                "g", lambda g, a=dst, b=src: g.dma_start(out=a, in_=b), wd_c[par]))

        for g_ in range(ngrp):
            V_h = {}
            for i_ in range(HGRP):
                hc = g_ * HGRP + i_
                sg = 3 * H + KC + 2 * hc
                su = sg + 1
                ring_ensure(sg + NWB - 1)
                pl.wait("t", *ring_loaded[sg])
                wb_g = ring_buf3(sg)
                wb_u = ring_buf3(su)
                wait_bank("t", "S0")
                for kc in range(KC):
                    if hc == 0:
                        pl.wait("t", vv, M_xhat2[kc])
                    st, sp = kc == 0, kc == KC - 1
                    t_mm(pS0[:, 0:TH], wb_g[:, kc, :], xh_chunk(kc)[:, 0:TH], st, sp)
                    T_g = t_mm(pS0[:, 512:512 + TH], wb_g[:, kc, :],
                               xh_chunk(kc)[:, TH:TOK], st, sp, inc=(kc == KC - 1))
                ring_consumed[sg] = (tv, T_g)
                ring_ensure(su + NWB - 1)
                pl.wait("t", *ring_loaded[su])
                wait_bank("t", "S1")
                for kc in range(KC):
                    st, sp = kc == 0, kc == KC - 1
                    t_mm(pS1[:, 0:TH], wb_u[:, kc, :], xh_chunk(kc)[:, 0:TH], st, sp)
                    T_u = t_mm(pS1[:, 512:512 + TH], wb_u[:, kc, :],
                               xh_chunk(kc)[:, TH:TOK], st, sp, inc=(kc == KC - 1))
                ring_consumed[su] = (tv, T_u)

                pl.wait("s", tv, T_g)
                if prev_vh[0] is not None:
                    pl.wait("s", vv, prev_vh[0])
                c4 = pl.emit("s", lambda s: s.activation(SIL[:, 0:TH], pS0[:, 0:TH],
                                                         AF.Silu), sv)
                pl.wait("s", sv, c4)
                S_sil = pl.emit("s", lambda s: s.activation(
                    SIL[:, TH:TOK], pS0[:, 512:512 + TH], AF.Silu), sv)
                bank_free["S0"] = (sv, S_sil)

                hreg = HREG[:, i_ * TOK:(i_ + 1) * TOK]
                if i_ in h_free:
                    pl.wait("v", *h_free[i_])
                pl.wait("v", sv, S_sil)
                pl.wait("v", tv, T_u)
                pl.emit("v", lambda v, d=hreg: v.tensor_mul(
                    d[:, 0:TH], SIL[:, 0:TH], pS1[:, 0:TH]), vv)
                V_h[i_] = pl.emit("v", lambda v, d=hreg: v.tensor_mul(
                    d[:, TH:TOK], SIL[:, TH:TOK], pS1[:, 512:512 + TH]), vv)
                bank_free["S1"] = (vv, V_h[i_])
                prev_vh[0] = V_h[i_]

            dpair = {0: ("D0", "D1"), 1: ("D2", "D3")}
            wd_load(g_, 0)
            wd_load(g_, 1)
            for mc in range(KC):
                d0n, d1n = dpair[mc % 2]
                d0, d1 = banks[d0n], banks[d1n]
                wait_bank("t", d0n)
                wait_bank("t", d1n)
                pl.wait("t", *L_wd[(g_, mc)])
                wb_d = WDBUF[:, (mc % 2) * HGRP * P:(mc % 2 + 1) * HGRP * P].rearrange(
                    "p (i m) -> p i m", m=P)
                for i_ in range(HGRP):
                    if mc == 0:
                        pl.wait("t", vv, V_h[i_])
                    hreg = HREG[:, i_ * TOK:(i_ + 1) * TOK]
                    st, sp = i_ == 0, i_ == HGRP - 1
                    t_mm(d0[:, 0:TH], wb_d[:, i_, :], hreg[:, 0:TH], st, sp)
                    T_d = t_mm(d1[:, 0:TH], wb_d[:, i_, :], hreg[:, TH:TOK], st, sp,
                               inc=(i_ == HGRP - 1))
                wd_free[mc % 2] = (tv, T_d)
                if mc + 2 < KC:
                    wd_load(g_, mc + 2)
                elif g_ + 1 < ngrp:
                    wd_load(g_ + 1, mc + 2 - KC)
                pl.wait("v", tv, T_d)
                if g_ == 0:
                    pl.emit("v", lambda v, b=d0, d=rs_chunk(mc):
                            v.tensor_copy(d[:, 0:TH], b[:, 0:TH]), vv)
                    V_dd = pl.emit("v", lambda v, b=d1, d=rs_chunk(mc):
                                   v.tensor_copy(d[:, TH:TOK], b[:, 0:TH]), vv)
                else:
                    pl.wait("v", vv, V_d[mc])
                    pl.emit("v", lambda v, b=d0, d=rs_chunk(mc):
                            v.tensor_add(d[:, 0:TH], d[:, 0:TH], b[:, 0:TH]), vv)
                    V_dd = pl.emit("v", lambda v, b=d1, d=rs_chunk(mc):
                                   v.tensor_add(d[:, TH:TOK], d[:, TH:TOK], b[:, 0:TH]),
                                   vv)
                V_d[mc] = V_dd
                bank_free[d0n] = (vv, V_dd)
                bank_free[d1n] = (vv, V_dd)
                if mc == KC - 1:
                    for i_ in range(HGRP):
                        h_free[i_] = (tv, T_d)

        out3 = out_d.rearrange("(kc p) t -> p kc t", p=P)

        def out_chunk_cb(kc, milestone):
            pl.wait("y", vv, milestone)
            pl.emit("y", lambda s, k=kc: s.dma_start(
                out=out3[:, k, :], in_=XS3[:, k, :]), ld0)

        V_res2, _ = ln(
            rs_chunk,
            cast_slot=lambda kc: xh_chunk(kc % 2),
            sq_slot=lambda kc: HREG[:, (kc % 2) * TOK:(kc % 2 + 1) * TOK],
            apply_out=ln34_apply, smalls=SMALLS_C, tmpa=(TMPA_C, TMPA2_C),
            pre_kc=lambda kc: (vv, V_d[kc]), pre_wait_t=(tv, tv.val),
            pre_sq=(vv, V_d[KC - 1]),
            cast_dbuf=True, kc_done_cb=out_chunk_cb)

        pl.wait("y", ld0, ld0.val)

        @block.tensor
        def _(t):
            for fn in pl.q["t"]:
                fn(t)

        @block.vector
        def _(v):
            for fn in pl.q["v"]:
                fn(v)

        @block.scalar
        def _(s):
            for fn in pl.q["s"]:
                fn(s)

        @block.sync
        def _(s):
            for fn in pl.q["y"]:
                fn(s)

        @block.gpsimd
        def _(g):
            for fn in pl.q["g"]:
                fn(g)

    return nc



def owned_tokens(c, nslot):
    return np.concatenate(
        [np.arange(P) + P * (NCORES * s + c) for s in range(nslot)])


def prep_inputs(x, w_aq, w_ak, w_av, w_ao, w_gate, w_up, w_down, cfg):
    H, KC, NSLOT, HC = cfg["H"], cfg["KC"], cfg["NSLOT"], cfg["HC"]
    DM = KC * P

    def pre_qkv(w):
        return np.ascontiguousarray(
            np.asarray(w, dtype=np.float32).reshape(H, KC, P, P).transpose(0, 2, 1, 3))

    wq = pre_qkv(w_aq)
    wk = pre_qkv(w_ak)
    wv = pre_qkv(w_av)
    wo = np.ascontiguousarray(
        np.asarray(w_ao, dtype=np.float32).reshape(H, P, KC, P).transpose(2, 1, 0, 3))
    wg = np.ascontiguousarray(
        np.asarray(w_gate, dtype=np.float32).reshape(HC, P, KC, P).transpose(0, 3, 2, 1))
    wu = np.ascontiguousarray(
        np.asarray(w_up, dtype=np.float32).reshape(HC, P, KC, P).transpose(0, 3, 2, 1))
    wd = np.ascontiguousarray(
        np.asarray(w_down, dtype=np.float32).reshape(KC, P, HC, P).transpose(0, 3, 2, 1))

    ident = np.eye(P, dtype=ml_dtypes.bfloat16)
    perm = np.zeros((P, P), dtype=ml_dtypes.bfloat16)
    perm[(np.arange(P) + P // 2) % P, np.arange(P)] = 1.0
    xf = np.asarray(x, dtype=np.float32).reshape(-1, DM)
    in_maps = []
    for c in range(NCORES):
        idx = owned_tokens(c, NSLOT)
        xc = np.ascontiguousarray(xf[idx, :].T)
        pos = idx.astype(np.float64)
        dims = np.arange(64, dtype=np.float64)
        freqs = 10000.0 ** (-dims / 64.0)
        rad = freqs[:, None] * pos[None, :]
        cos2 = np.concatenate([np.cos(rad), np.cos(rad)], axis=0)
        sin2 = np.concatenate([-np.sin(rad), np.sin(rad)], axis=0)
        kk = np.arange(P)[:, None, None]
        kt = np.arange(NCORES)[None, :, None]
        qi = np.arange(P)[None, None, :]
        m01 = (128 * kt + kk <= 128 * c + qi).astype(np.float32).reshape(
            P, NCORES * P)
        in_maps.append({
            "x": xc,
            "wq": wq, "wk": wk, "wv": wv, "wo": wo,
            "wg": wg, "wu": wu, "wd": wd,
            "cos2": cos2.astype(ml_dtypes.bfloat16),
            "sin2": sin2.astype(ml_dtypes.bfloat16),
            "mask01": m01.astype(ml_dtypes.bfloat16),
            "ident": ident,
            "perm": perm,
        })
    return in_maps


_NC_CACHE = {}


def kernel(L0_attn_x, L0_attn_w_aq, L0_attn_w_ak, L0_attn_w_av, L0_attn_w_ao,
           L0_ffn_w_gate, L0_ffn_w_up, L0_ffn_w_down):
    cfg = FULL_CFG
    in_maps = prep_inputs(L0_attn_x, L0_attn_w_aq, L0_attn_w_ak, L0_attn_w_av,
                          L0_attn_w_ao, L0_ffn_w_gate, L0_ffn_w_up,
                          L0_ffn_w_down, cfg)
    if "full" not in _NC_CACHE:
        _NC_CACHE["full"] = build_nc(cfg)
    nc = _NC_CACHE["full"]
    res = run_bass_kernel_spmd(nc, in_maps, core_ids=list(range(NCORES)))
    T = cfg["NSLOT"] * P * NCORES
    DM = cfg["KC"] * P
    out = np.empty((1, T, DM), dtype=np.float32)
    for c in range(NCORES):
        idx = owned_tokens(c, cfg["NSLOT"])
        out[0, idx, :] = res.results[c]["out"].T
    return out
